# version 02
# baseline (speedup 1.0000x reference)
import math
import numpy as np
import ml_dtypes
from contextlib import ExitStack

import concourse.bass as bass
import concourse.mybir as mybir
from concourse.bass_utils import run_bass_kernel_spmd

F32 = mybir.dt.float32
BF16 = mybir.dt.bfloat16
U32 = mybir.dt.uint32
AF = mybir.ActivationFunctionType
ALU = mybir.AluOpType
AX = mybir.AxisListType

NCORES = 8
D = 1024
DC = D // 128
SEQ = 2048
BL = 2
TL = BL * SEQ
TT = 512
NT = TL // TT
E = 8
K_SEL = 8192
NTOK = 16 * SEQ
DI = 4096
NH = 16
DH = 64
I1 = 938
LN_EPS = 1e-5
BIS_ITERS = 42


class Sem:
    def __init__(self, h):
        self.h = h
        self.n = 0

    def inc(self, ins, v=1):
        self.n += v
        ins.then_inc(self.h, v)
        return self.n



_BLK = [0]


def blk(nc, name):
    _BLK[0] += 1
    return nc.Block(f"{name}_{_BLK[0]}")

def sem(ctx, nc, name):
    return Sem(ctx.enter_context(nc.semaphore(name)))


def sbuf(ctx, nc, name, shape, dt):
    return ctx.enter_context(nc.sbuf_tensor(name, shape, dt))


def psum(ctx, nc, name, shape, dt=F32):
    return ctx.enter_context(nc.psum_tensor(name, shape, dt))


def load_fm_tile(q, dst, src_hbm, t0, sem_, width=TT, nchunk=DC, src_cols=TL):
    ap_in = bass.AP(src_hbm, t0,
                    [[src_cols, 128], [src_cols * 128, nchunk], [1, width]])
    ap_out = bass.AP(dst, 0, [[nchunk * width, 128], [width, nchunk], [1, width]])
    sem_.inc(q.dma_start(out=ap_out, in_=ap_in), 16)


def store_fm_tile(q, dst_hbm, src, t0, sem_, width=TT, nchunk=DC, dst_cols=TL,
                  accum=False):
    ap_out = bass.AP(dst_hbm, t0,
                     [[dst_cols, 128], [dst_cols * 128, nchunk], [1, width]])
    ap_in = bass.AP(src, 0, [[nchunk * width, 128], [width, nchunk], [1, width]])
    kw = {"accum_op": ALU.add} if accum else {}
    sem_.inc(q.dma_start(out=ap_out, in_=ap_in, **kw), 16)



def emit_ln_stats(nc, bufs, W=TT):
    xt = bufs["xt"]; xsq = bufs["xsq"]
    ps_sum = bufs["ps_sum"]; ps_sq = bufs["ps_sq"]; ps_tr = bufs["ps_tr"]
    m_row = bufs["m_row"]; r_row = bufs["r_row"]; var_row = bufs["var_row"]
    mcol = bufs["mcol"]; rcol = bufs["rcol"]
    nsub = W // 128

    with blk(nc, "lnsq") as b:
        @b.scalar
        def _(sc):
            sc.activation(xsq[:, :DC * W], xt[:, :DC * W], AF.Square)

    with blk(nc, "lnmm") as b:
        @b.tensor
        def _(t):
            for c in range(DC):
                t.matmul(ps_sum[0:1, :W], bufs["ones_col"][:, :],
                         xt[:, c * W:(c + 1) * W],
                         start=(c == 0), stop=(c == DC - 1))
            for c in range(DC):
                t.matmul(ps_sq[0:1, :W], bufs["ones_col"][:, :],
                         xsq[:, c * W:(c + 1) * W],
                         start=(c == 0), stop=(c == DC - 1))

    with blk(nc, "lnst1") as b:
        @b.vector
        def _(v):
            v.tensor_scalar_mul(m_row[0:1, :W], ps_sum[0:1, :W], 1.0 / D)
            v.tensor_scalar_mul(var_row[0:1, :W], ps_sq[0:1, :W], 1.0 / D)

    with blk(nc, "lnst2") as b:
        @b.scalar
        def _(sc):
            sc.activation(r_row[0:1, :W], m_row[0:1, :W], AF.Square)

    with blk(nc, "lnst3") as b:
        @b.vector
        def _(v):
            v.tensor_sub(var_row[0:1, :W], var_row[0:1, :W], r_row[0:1, :W])

    with blk(nc, "lnst4") as b:
        @b.scalar
        def _(sc):
            sc.activation(var_row[0:1, :W], var_row[0:1, :W], AF.Sqrt,
                          bias=bufs["eps1"][0:1, 0:1])

    with blk(nc, "lnst5") as b:
        @b.vector
        def _(v):
            v.reciprocal(r_row[0:1, :W], var_row[0:1, :W])

    with blk(nc, "lntr") as b:
        @b.tensor
        def _(t):
            one11 = bufs["ones_row"][0:1, 0:1]
            for s in range(nsub):
                t.matmul(ps_tr[:, s:s + 1], m_row[0:1, s * 128:(s + 1) * 128],
                         one11, start=True, stop=True)
            for s in range(nsub):
                t.matmul(ps_tr[:, nsub + s:nsub + s + 1],
                         r_row[0:1, s * 128:(s + 1) * 128], one11,
                         start=True, stop=True)

    with blk(nc, "lnev") as b:
        @b.vector
        def _(v):
            v.tensor_copy(mcol[:, 0:nsub], ps_tr[:, 0:nsub])
            v.tensor_copy(rcol[:, 0:nsub], ps_tr[:, nsub:2 * nsub])



def emit_scores_tile(nc, bufs, W=TT):
    xt = bufs["xt"]; wr_sb = bufs["wr_sb"]; csw_bc = bufs["csw_bc"]
    ps_raw = bufs["ps_raw"]
    mcol = bufs["mcol"]; rcol = bufs["rcol"]
    neg2 = bufs["neg2"]; pre = bufs["pre"]
    mx = bufs["mx"]; zrow = bufs["zrow"]
    exps = bufs["exps"]; sc_t = bufs["sc_t"]
    nsub = W // 128

    with blk(nc, "scraw") as b:
        @b.tensor
        def _(t):
            for s in range(nsub):
                for c in range(DC):
                    t.matmul(ps_raw[:, s * 8:(s + 1) * 8],
                             xt[:, c * W + s * 128: c * W + s * 128 + 128],
                             wr_sb[:, c * 8:(c + 1) * 8],
                             start=(c == 0), stop=(c == DC - 1))

    with blk(nc, "scadj0") as b:
        @b.vector
        def _(v):
            v.tensor_scalar_mul(bufs["rneg"][:, 0:nsub], rcol[:, 0:nsub], -1.0)

    with blk(nc, "scadj") as b:
        @b.vector
        def _(v):
            for s in range(nsub):
                v.scalar_tensor_tensor(neg2[:, s * 8:(s + 1) * 8], csw_bc[:, 0:8],
                                       mcol[:, s:s + 1], ps_raw[:, s * 8:(s + 1) * 8],
                                       ALU.mult, ALU.subtract)
            for s in range(nsub):
                v.tensor_scalar(pre[:, s * 8:(s + 1) * 8], neg2[:, s * 8:(s + 1) * 8],
                                bufs["rneg"][:, s:s + 1], None, ALU.mult)
            v.tensor_reduce(mx[:, 0:nsub].unsqueeze(2),
                            pre[:, 0:nsub * 8].rearrange("p (s e) -> p s e", e=8),
                            AX.X, ALU.max, negate=True)

    with blk(nc, "scexp") as b:
        @b.scalar
        def _(sc):
            for s in range(nsub):
                sc.activation(exps[:, s * 8:(s + 1) * 8], pre[:, s * 8:(s + 1) * 8],
                              AF.Exp, bias=mx[:, s:s + 1],
                              accum_out=zrow[:, s:s + 1])

    with blk(nc, "scdiv") as b:
        @b.vector
        def _(v):
            v.reciprocal(bufs["rz"][:, 0:nsub], zrow[:, 0:nsub])

    with blk(nc, "scdiv2") as b:
        @b.vector
        def _(v):
            for s in range(nsub):
                v.tensor_scalar(sc_t[:, s * 8:(s + 1) * 8], exps[:, s * 8:(s + 1) * 8],
                                bufs["rz"][:, s:s + 1], None, ALU.mult)


def alloc_scores_bufs(ctx, nc, tag, W=TT, ps=None):
    nsub = W // 128
    if ps is None:
        ps = {
            "ps_sum": psum(ctx, nc, f"{tag}_pssum", [1, W]),
            "ps_sq": psum(ctx, nc, f"{tag}_pssq", [1, W]),
            "ps_tr": psum(ctx, nc, f"{tag}_pstr", [128, 2 * nsub]),
            "ps_raw": psum(ctx, nc, f"{tag}_psraw", [128, nsub * 8]),
            "ps_bc": psum(ctx, nc, f"{tag}_psbc", [128, 8]),
        }
    return {
        **ps,
        "xt": sbuf(ctx, nc, f"{tag}_xt", [128, DC * W], F32),
        "xsq": sbuf(ctx, nc, f"{tag}_xsq", [128, DC * W], F32),
        "ones_col": sbuf(ctx, nc, f"{tag}_onec", [128, 1], F32),
        "eps1": sbuf(ctx, nc, f"{tag}_eps1", [1, 1], F32),
        "ones_row": sbuf(ctx, nc, f"{tag}_oner", [1, 128], F32),
        "ident_f32": sbuf(ctx, nc, f"{tag}_id", [128, 128], F32),
        "m_row": sbuf(ctx, nc, f"{tag}_mrow", [1, W], F32),
        "r_row": sbuf(ctx, nc, f"{tag}_rrow", [1, W], F32),
        "var_row": sbuf(ctx, nc, f"{tag}_vrow", [1, W], F32),
        "mcol": sbuf(ctx, nc, f"{tag}_mcol", [128, nsub], F32),
        "rcol": sbuf(ctx, nc, f"{tag}_rcol", [128, nsub], F32),
        "rneg": sbuf(ctx, nc, f"{tag}_rneg", [128, nsub], F32),
        "wr_sb": sbuf(ctx, nc, f"{tag}_wr", [128, DC * 8], F32),
        "csw_bc": sbuf(ctx, nc, f"{tag}_csw", [128, 8], F32),
        "neg2": sbuf(ctx, nc, f"{tag}_neg2", [128, nsub * 8], F32),
        "pre": sbuf(ctx, nc, f"{tag}_pre", [128, nsub * 8], F32),
        "mx": sbuf(ctx, nc, f"{tag}_mx", [128, nsub], F32),
        "zrow": sbuf(ctx, nc, f"{tag}_zrow", [128, nsub], F32),
        "rz": sbuf(ctx, nc, f"{tag}_rz", [128, nsub], F32),
        "exps": sbuf(ctx, nc, f"{tag}_exps", [128, nsub * 8], F32),
        "sc_t": sbuf(ctx, nc, f"{tag}_sct", [128, nsub * 8], F32),
    }


def emit_scores_prep(nc, bufs, wr_hbm, ld):
    with blk(nc, "scp1") as b:
        @b.sync
        def _(s):
            ap_in = bass.AP(wr_hbm, 0, [[8, 128], [8 * 128, DC], [1, 8]])
            ap_out = bass.AP(bufs["wr_sb"], 0, [[DC * 8, 128], [8, DC], [1, 8]])
            ld.inc(s.dma_start(out=ap_out, in_=ap_in), 16)
            s.wait_ge(ld.h, ld.n)

        @b.gpsimd
        def _(g):
            g.memset(bufs["ones_col"][:, :], 1.0)
            g.memset(bufs["ones_row"][:, :], 1.0)
            g.memset(bufs["eps1"][:, :], LN_EPS)

    with blk(nc, "scp2") as b:
        @b.tensor
        def _(t):
            for c in range(DC):
                t.matmul(bufs["ps_bc"][0:1, 0:8], bufs["ones_col"][:, :],
                         bufs["wr_sb"][:, c * 8:(c + 1) * 8],
                         start=(c == 0), stop=(c == DC - 1))

    with blk(nc, "scp3") as b:
        @b.vector
        def _(v):
            v.tensor_copy(bufs["csw_bc"][0:1, 0:8], bufs["ps_bc"][0:1, 0:8])

    with blk(nc, "scp4") as b:
        @b.tensor
        def _(t):
            t.matmul(bufs["ps_bc"][:, 0:8], bufs["ones_row"][:, :],
                     bufs["csw_bc"][0:1, 0:8], start=True, stop=True)

    with blk(nc, "scp5") as b:
        @b.vector
        def _(v):
            v.tensor_copy(bufs["csw_bc"][:, 0:8], bufs["ps_bc"][:, 0:8])


def build_scores(debug=False):
    nc = bass.Bass("TRN2", target_bir_lowering=False, debug=False)
    x = nc.dram_tensor("x", [D, TL], F32, kind="ExternalInput")
    wr = nc.dram_tensor("wr", [D, E], F32, kind="ExternalInput")
    ident_in = nc.dram_tensor("ident", [128, 128], F32, kind="ExternalInput")
    sc_out = nc.dram_tensor("sc", [TL, E], F32, kind="ExternalOutput")
    if debug:
        dbg_m = nc.dram_tensor("dbg_m", [1, TT], F32, kind="ExternalOutput")
        dbg_r = nc.dram_tensor("dbg_r", [1, TT], F32, kind="ExternalOutput")
        dbg_pre = nc.dram_tensor("dbg_pre", [128, 32], F32, kind="ExternalOutput")
        dbg_csw = nc.dram_tensor("dbg_csw", [128, 8], F32, kind="ExternalOutput")
        dbg_exps = nc.dram_tensor("dbg_exps", [128, 32], F32, kind="ExternalOutput")
        dbg_z = nc.dram_tensor("dbg_z", [128, 4], F32, kind="ExternalOutput")
        dbg_sct = nc.dram_tensor("dbg_sct", [128, 32], F32, kind="ExternalOutput")
        dbg_mx = nc.dram_tensor("dbg_mx", [128, 4], F32, kind="ExternalOutput")

    with ExitStack() as ctx:
        bufs = alloc_scores_bufs(ctx, nc, "s")
        ld = sem(ctx, nc, "ld")
        st = sem(ctx, nc, "st")

        with blk(nc, "id") as b:
            @b.sync
            def _(s):
                ld.inc(s.dma_start(out=bufs["ident_f32"][:, :],
                                   in_=ident_in.ap()), 16)
                s.wait_ge(ld.h, ld.n)

        emit_scores_prep(nc, bufs, wr, ld)

        for t0 in range(NT):
            with blk(nc, f"xl{t0}") as b:
                @b.sync
                def _(s):
                    load_fm_tile(s, bufs["xt"], x, t0 * TT, ld)
                    s.wait_ge(ld.h, ld.n)
            emit_ln_stats(nc, bufs)
            emit_scores_tile(nc, bufs)
            if debug and t0 == 0:
                with blk(nc, "dbg") as b:
                    @b.sync
                    def _(s):
                        st.inc(s.dma_start(out=dbg_m.ap(), in_=bufs["m_row"][0:1, :]), 16)
                        st.inc(s.dma_start(out=dbg_r.ap(), in_=bufs["r_row"][0:1, :]), 16)
                        st.inc(s.dma_start(out=dbg_pre.ap(), in_=bufs["pre"][:, :]), 16)
                        st.inc(s.dma_start(out=dbg_csw.ap(), in_=bufs["csw_bc"][:, :]), 16)
                        st.inc(s.dma_start(out=dbg_exps.ap(), in_=bufs["exps"][:, :]), 16)
                        st.inc(s.dma_start(out=dbg_z.ap(), in_=bufs["zrow"][:, :]), 16)
                        st.inc(s.dma_start(out=dbg_sct.ap(), in_=bufs["sc_t"][:, :]), 16)
                        st.inc(s.dma_start(out=dbg_mx.ap(), in_=bufs["mx"][:, :]), 16)
            with blk(nc, f"xs{t0}") as b:
                @b.sync
                def _(s):
                    for sub in range(TT // 128):
                        row0 = t0 * TT + sub * 128
                        st.inc(s.dma_start(
                            out=bass.AP(sc_out, row0 * E, [[E, 128], [1, E]]),
                            in_=bufs["sc_t"][:, sub * 8:(sub + 1) * 8]), 16)
                    s.wait_ge(st.h, st.n)

        with blk(nc, "fin") as b:
            @b.sync
            def _(s):
                s.wait_ge(st.h, st.n)

    return nc



def _host_constants():
    return {"ident": np.eye(128, dtype=np.float32)}


def _fm_shards(x):
    out = []
    for c in range(NCORES):
        xc = x[BL * c:BL * c + BL]
        xc = np.ascontiguousarray(
            xc.transpose(2, 0, 1).reshape(D, TL)).astype(np.float32, copy=False)
        out.append(xc)
    return out


def run_scores(x_fm_shards, wr, debug=False):
    nc = build_scores(debug)
    consts = _host_constants()
    maps = [{"x": s, "wr": np.ascontiguousarray(wr), "ident": consts["ident"]}
            for s in x_fm_shards]
    res = run_bass_kernel_spmd(nc, maps, core_ids=list(range(NCORES)))
    if debug:
        return res
    return np.concatenate([r["sc"] for r in res.results], axis=0)





def build_moe(debug=False):
    nc = bass.Bass("TRN2", target_bir_lowering=False, debug=False)
    x = nc.dram_tensor("x", [D, TL], F32, kind="ExternalInput")
    sbis_in = nc.dram_tensor("sbis", [128, NTOK // 16], F32, kind="ExternalInput")
    slocT_in = nc.dram_tensor("slocT", [E, TL], F32, kind="ExternalInput")
    w1_in = nc.dram_tensor("w1", [E, D, DI], F32, kind="ExternalInput")
    w2_in = nc.dram_tensor("w2", [E, DI, D], F32, kind="ExternalInput")
    b1f_in = nc.dram_tensor("b1f", [E, 128, DI // 128], F32, kind="ExternalInput")
    b2_in = nc.dram_tensor("b2", [E, D], F32, kind="ExternalInput")
    g16_in = nc.dram_tensor("g16", [128, 8], F32, kind="ExternalInput")
    lohi_in = nc.dram_tensor("lohi", [8, 2], F32, kind="ExternalInput")
    r16_in = nc.dram_tensor("r16", [8, 128], F32, kind="ExternalInput")
    xo = nc.dram_tensor("xo", [D, TL], F32, kind="ExternalOutput")
    if debug:
        dbg_t = nc.dram_tensor("dbg_t", [8, 1], F32, kind="ExternalOutput")
        dbg_m = nc.dram_tensor("dbg_m", [E, TL], F32, kind="ExternalOutput")
        dbg_it = nc.dram_tensor("dbg_it", [8, 12], F32, kind="ExternalOutput")
    mdram = nc.dram_tensor("mdram", [E, TL], F32)
    lndram = nc.dram_tensor("lndram", [D, TL], BF16)

    NHC = DI // 128
    with ExitStack() as ctx:
        lo8 = sbuf(ctx, nc, "lo8", [8, 1], F32)
        hi8 = sbuf(ctx, nc, "hi8", [8, 1], F32)
        mid8 = sbuf(ctx, nc, "mid8", [8, 1], F32)
        gef = sbuf(ctx, nc, "gef", [8, 1], F32)
        d8 = sbuf(ctx, nc, "d8", [8, 1], F32)
        d8b = sbuf(ctx, nc, "d8b", [8, 1], F32)
        t_col = sbuf(ctx, nc, "tcol", [8, 1], F32)
        itdbg = sbuf(ctx, nc, "itdbg", [8, 12], F32)
        ones_col = sbuf(ctx, nc, "onec", [128, 1], F32)
        ones_row = sbuf(ctx, nc, "oner", [1, 128], F32)
        eps1 = sbuf(ctx, nc, "eps1", [1, 1], F32)
        g16 = sbuf(ctx, nc, "g16s", [128, 8], F32)
        r16 = sbuf(ctx, nc, "r16s", [8, 128], F32)
        midbc = sbuf(ctx, nc, "midbc", [128, 1], F32)
        t_row = sbuf(ctx, nc, "trow", [1, E], F32)
        cnt = sbuf(ctx, nc, "cnt", [128, 1], F32)
        b2sb = sbuf(ctx, nc, "b2sb", [8, D], F32)
        mt8 = sbuf(ctx, nc, "mt8", [8, TT], F32)
        lnt = sbuf(ctx, nc, "lnt", [128, DC * TT], BF16)
        mrow_cur = sbuf(ctx, nc, "mrowc", [1, TL], F32)
        mbc_t = sbuf(ctx, nc, "mbct", [128, TT], F32)
        gtmp = [sbuf(ctx, nc, f"gtmp{i}", [128, TT], F32) for i in range(2)]
        osb = [sbuf(ctx, nc, f"osb{i}", [128, TT], F32) for i in range(2)]
        b1sb = sbuf(ctx, nc, "b1sb", [128, DI // 128], F32)
        ps_h = [psum(ctx, nc, f"psh{i}", [128, TT]) for i in range(4)]
        ps_o = [psum(ctx, nc, f"pso{i}", [128, TT]) for i in range(2)]
        ps_g = psum(ctx, nc, "psg", [128, TT])
        ps_h2 = psum(ctx, nc, "psmisc2", [128, TT])
        ld = sem(ctx, nc, "ld")
        lnld = sem(ctx, nc, "lnld")
        wsem = sem(ctx, nc, "wsem")
        dvs = sem(ctx, nc, "dvs")
        mms = sem(ctx, nc, "mms")
        acs = sem(ctx, nc, "acs")
        gps = sem(ctx, nc, "gps")
        sto = sem(ctx, nc, "sto")

        with blk(nc, "m_p0") as b:
            @b.sync
            def _(s):
                ld.inc(s.dma_start(out=g16[:, :], in_=g16_in.ap()), 16)
                ld.inc(s.dma_start(out=r16[:, :], in_=r16_in.ap()), 16)
                ld.inc(s.dma_start(out=b2sb[:, :], in_=b2_in.ap()), 16)
                s.wait_ge(ld.h, ld.n)

            @b.gpsimd
            def _(g):
                g.memset(ones_col[:, :], 1.0)
                g.memset(ones_row[:, :], 1.0)
                g.memset(eps1[:, :], LN_EPS)

            @b.sync
            def _(s2):
                ld.inc(s2.dma_start(out=lo8[:, :],
                                    in_=bass.AP(lohi_in, 0, [[2, 8], [1, 1]]),
                                    allow_slow_non_contiguous=True), 16)
                ld.inc(s2.dma_start(out=hi8[:, :],
                                    in_=bass.AP(lohi_in, 1, [[2, 8], [1, 1]]),
                                    allow_slow_non_contiguous=True), 16)
                s2.wait_ge(ld.h, ld.n)

        with ExitStack() as c1:
            sbis = sbuf(c1, nc, "sbiss", [128, NTOK // 16], F32)
            junk = sbuf(c1, nc, "junk", [128, NTOK // 16], BF16)
            with blk(nc, "m_p1l") as b:
                @b.sync
                def _(s):
                    ld.inc(s.dma_start(out=sbis[:, :], in_=sbis_in.ap()), 16)
                    s.wait_ge(ld.h, ld.n)

            for i in range(BIS_ITERS):
                with blk(nc, "bisa") as b:
                    @b.vector
                    def _(v):
                        v.tensor_scalar(mid8[:, :], lo8[:, :], hi8[:, 0:1], 0.5,
                                        ALU.add, ALU.mult)

                with blk(nc, "bisb") as b:
                    @b.tensor
                    def _(t):
                        t.matmul(ps_g[:, 0:1], r16[:, :], mid8[:, :],
                                 start=True, stop=True)

                with blk(nc, "bisc") as b:
                    @b.vector
                    def _(v):
                        v.tensor_copy(midbc[:, :], ps_g[:, 0:1])

                with blk(nc, "bisd") as b:
                    @b.vector
                    def _(v):
                        v.tensor_scalar(junk[:, :], sbis[:, :], midbc[:, 0:1],
                                        None, ALU.is_ge)
                        v.drain()
                        v.reduce_sum(cnt[:, 0:1], junk[:, :], AX.X)

                with blk(nc, "bise") as b:
                    @b.tensor
                    def _(t):
                        t.matmul(ps_h2[0:8, 0:1], g16[:, :], cnt[:, :],
                                 start=True, stop=True)

                with blk(nc, "bisf") as b:
                    @b.vector
                    def _(v):
                        v.tensor_scalar(gef[:, :], ps_h2[0:8, 0:1],
                                        float(K_SEL) - 0.5, None, ALU.is_ge)

                if debug and i < 4:
                    with blk(nc, "bisdbg") as b:
                        @b.vector
                        def _(v):
                            v.tensor_copy(itdbg[:, 3 * i:3 * i + 1], mid8[:, :])
                            v.tensor_copy(itdbg[:, 3 * i + 1:3 * i + 2], ps_h2[0:8, 0:1])
                            v.tensor_copy(itdbg[:, 3 * i + 2:3 * i + 3], gef[:, :])

                with blk(nc, "bisg") as b:
                    @b.vector
                    def _(v):
                        v.tensor_sub(d8[:, :], mid8[:, :], lo8[:, :])
                        v.tensor_sub(d8b[:, :], hi8[:, :], mid8[:, :])
                        v.drain()
                        v.scalar_tensor_tensor(lo8[:, :], d8[:, :], gef[:, 0:1],
                                               lo8[:, :], ALU.mult, ALU.add)
                        v.scalar_tensor_tensor(hi8[:, :], d8b[:, :], gef[:, 0:1],
                                               mid8[:, :], ALU.mult, ALU.add)

            with blk(nc, "m_p1f") as b:
                @b.vector
                def _(v):
                    dvs.inc(v.tensor_copy(t_col[:, :], lo8[:, :]))

        c2 = ctx.enter_context(ExitStack())
        srow = sbuf(c2, nc, "srow", [1, TL], F32)
        mrow_w = sbuf(c2, nc, "mroww", [1, TL], F32)
        tdram = nc.dram_tensor("tdram", [8, 1], F32)
        with blk(nc, "m_p2a") as b:
            @b.sync
            def _(s):
                sto.inc(s.dma_start(out=tdram.ap(), in_=t_col[:, :]), 16)
                s.wait_ge(sto.h, sto.n)
                sto.inc(s.dma_start(out=t_row[0:1, :],
                                    in_=bass.AP(tdram, 0, [[8, 1], [1, 8]])), 16)
                s.wait_ge(sto.h, sto.n)
                if debug:
                    sto.inc(s.dma_start(out=dbg_t.ap(), in_=t_col[:, :]), 16)
                    sto.inc(s.dma_start(out=dbg_it.ap(), in_=itdbg[:, :]), 16)
                    s.wait_ge(sto.h, sto.n)

        for ee in range(E):
            with blk(nc, f"m_p2r{ee}") as b:
                @b.sync
                def _(s):
                    sto.inc(s.dma_start(
                        out=srow[0:1, :],
                        in_=bass.AP(slocT_in, ee * TL, [[TL, 1], [1, TL]])), 16)
                    s.wait_ge(sto.h, sto.n)

            with blk(nc, f"m_p2m{ee}") as b:
                @b.vector
                def _(v):
                    v.tensor_scalar(mrow_w[0:1, :], srow[0:1, :],
                                    t_row[0:1, ee:ee + 1], None, ALU.is_ge)

            with blk(nc, f"m_p2n{ee}") as b:
                @b.vector
                def _(v):
                    v.scalar_tensor_tensor(mrow_w[0:1, :], mrow_w[0:1, :], 0.5,
                                           srow[0:1, :], ALU.mult, ALU.mult)

            with blk(nc, f"m_p2s{ee}") as b:
                @b.sync
                def _(s):
                    sto.inc(s.dma_start(
                        out=bass.AP(mdram, ee * TL, [[TL, 1], [1, TL]]),
                        in_=mrow_w[0:1, :]), 16)
                    s.wait_ge(sto.h, sto.n)
                    if debug:
                        sto.inc(s.dma_start(
                            out=bass.AP(dbg_m, ee * TL, [[TL, 1], [1, TL]]),
                            in_=mrow_w[0:1, :]), 16)
                        s.wait_ge(sto.h, sto.n)

        c2.close()

        with ExitStack() as c3:
            lbufs = {
                "xt": sbuf(c3, nc, "xt", [128, DC * TT], F32),
                "xsq": sbuf(c3, nc, "xsq", [128, DC * TT], F32),
                "ones_col": ones_col, "ones_row": ones_row, "eps1": eps1,
                "ps_sum": ps_g[0:1, 0:TT], "ps_sq": ps_h2[0:1, 0:TT],
                "ps_tr": ps_h[0][:, 0:8],
                "m_row": sbuf(c3, nc, "mrow", [1, TT], F32),
                "r_row": sbuf(c3, nc, "rrow", [1, TT], F32),
                "var_row": sbuf(c3, nc, "vrow", [1, TT], F32),
                "mcol": sbuf(c3, nc, "mcol", [128, 4], F32),
                "rcol": sbuf(c3, nc, "rcol", [128, 4], F32),
            }
            lnc_s = [sbuf(c3, nc, f"lnc{i}", [128, TT], BF16) for i in range(2)]
            tmpc = sbuf(c3, nc, "tmpc", [128, TT], F32)
            for t0 in range(NT):
                with blk(nc, f"m_p3l{t0}") as b:
                    @b.sync
                    def _(s):
                        s.wait_ge(sto.h, sto.n)
                        load_fm_tile(s, lbufs["xt"], x, t0 * TT, lnld)
                        s.wait_ge(lnld.h, lnld.n)
                        store_fm_tile(s, xo, lbufs["xt"], t0 * TT, sto)
                emit_ln_stats(nc, lbufs)
                with blk(nc, f"m_p3b{t0}") as b:
                    @b.tensor
                    def _(t):
                        mms.inc(t.matmul(ps_g[:, 0:TT], ones_row[:, :],
                                         lbufs["m_row"][0:1, :], start=True, stop=True))
                        mms.inc(t.matmul(ps_h2[:, 0:TT], ones_row[:, :],
                                         lbufs["r_row"][0:1, :], start=True, stop=True))

                with blk(nc, f"m_p3c{t0}") as b:
                    dv0, st0 = dvs.n, sto.n

                    @b.vector
                    def _(v):
                        for c in range(DC):
                            if c >= 2:
                                v.wait_ge(sto.h, st0 + 16 * (c - 1))
                            v.tensor_sub(tmpc[:, :], lbufs["xt"][:, c * TT:(c + 1) * TT],
                                         ps_g[:, 0:TT])
                            dvs.inc(v.tensor_mul(lnc_s[c % 2][:, :], tmpc[:, :],
                                                 ps_h2[:, 0:TT]))

                    @b.sync
                    def _(s):
                        for c in range(DC):
                            s.wait_ge(dvs.h, dv0 + c + 1)
                            ap_out = bass.AP(lndram, t0 * TT + c * 128 * TL,
                                             [[TL, 128], [1, TT]])
                            sto.inc(s.dma_start(out=ap_out, in_=lnc_s[c % 2][:, :]), 16)

        with blk(nc, "m_p3fin") as b:
            @b.sync
            def _(s):
                s.wait_ge(sto.h, sto.n)

        with ExitStack() as c5:
            w1sb = sbuf(c5, nc, "w1sb", [128, DC * DI], BF16)
            w2sb = sbuf(c5, nc, "w2sb", [128, NHC * D], BF16)
            h_sb = sbuf(c5, nc, "hsb", [128, NHC * TT], BF16)

            for e in range(E):
                with blk(nc, f"m_w{e}") as b:
                    @b.gpsimd
                    def _(g):
                        for kc in range(DC):
                            ap_in = bass.AP(w1_in, e * D * DI + kc * 128 * DI,
                                            [[DI, 128], [1, DI]])
                            wsem.inc(g.dma_start(
                                out=w1sb[:, kc * DI:(kc + 1) * DI], in_=ap_in), 16)
                        for kc in range(NHC):
                            ap_in = bass.AP(w2_in, e * DI * D + kc * 128 * D,
                                            [[D, 128], [1, D]])
                            wsem.inc(g.dma_start(
                                out=w2sb[:, kc * D:(kc + 1) * D], in_=ap_in), 16)
                        wsem.inc(g.dma_start(
                            out=b1sb[:, :],
                            in_=bass.AP(b1f_in, e * 128 * NHC, [[NHC, 128], [1, NHC]])), 16)
                        wsem.inc(g.dma_start(
                            out=mrow_cur[0:1, :],
                            in_=bass.AP(mdram, e * TL, [[TL, 1], [1, TL]])), 16)

                for t0 in range(NT):
                    with blk(nc, f"m_e{e}t{t0}") as b:
                        mm0, ac0, dv0, gp0 = mms.n, acs.n, dvs.n, gps.n

                        @b.sync
                        def _(s):
                            ap_in = bass.AP(lndram, t0 * TT,
                                            [[TL, 128], [TL * 128, DC], [1, TT]])
                            ap_o = bass.AP(lnt, 0, [[DC * TT, 128], [TT, DC], [1, TT]])
                            lnld.inc(s.dma_start(out=ap_o, in_=ap_in), 16)

                        @b.tensor
                        def _(t):
                            t.wait_ge(wsem.h, wsem.n)
                            t.wait_ge(lnld.h, lnld.n)
                            mms.inc(t.matmul(
                                ps_g[:, 0:TT], ones_row[:, :],
                                mrow_cur[0:1, t0 * TT:(t0 + 1) * TT],
                                start=True, stop=True))
                            for hc in range(NHC):
                                if hc >= 4:
                                    t.wait_ge(acs.h, ac0 + hc - 3)
                                for kc in range(DC):
                                    ins = t.matmul(
                                        ps_h[hc % 4][:, :],
                                        w1sb[:, kc * DI + hc * 128: kc * DI + hc * 128 + 128],
                                        lnt[:, kc * TT:(kc + 1) * TT],
                                        start=(kc == 0), stop=(kc == DC - 1))
                                mms.inc(ins)
                            t.wait_ge(dvs.h, dv0 + 1 + NHC)
                            for oc in range(DC):
                                if oc >= 2:
                                    t.wait_ge(acs.h, ac0 + NHC + oc - 1)
                                for hc in range(NHC):
                                    ins = t.matmul(
                                        ps_o[oc % 2][:, :],
                                        w2sb[:, hc * D + oc * 128: hc * D + oc * 128 + 128],
                                        h_sb[:, hc * TT:(hc + 1) * TT],
                                        start=(hc == 0), stop=(hc == NHC - 1))
                                mms.inc(ins)

                        @b.vector
                        def _(v):
                            v.wait_ge(mms.h, mm0 + 1)
                            dvs.inc(v.tensor_copy(mbc_t[:, :], ps_g[:, 0:TT]))
                            for hc in range(NHC):
                                v.wait_ge(acs.h, ac0 + hc + 1)
                                dvs.inc(v.tensor_mul(h_sb[:, hc * TT:(hc + 1) * TT],
                                                     gtmp[hc % 2][:, :], mbc_t[:, :]))

                        @b.scalar
                        def _(sc):
                            sc.wait_ge(wsem.h, wsem.n)
                            for hc in range(NHC):
                                sc.wait_ge(mms.h, mm0 + 1 + hc + 1)
                                if hc >= 2:
                                    sc.wait_ge(dvs.h, dv0 + 1 + hc - 1)
                                acs.inc(sc.activation(gtmp[hc % 2][:, :],
                                                      ps_h[hc % 4][:, :], AF.Gelu,
                                                      bias=b1sb[:, hc:hc + 1]))
                            for oc in range(DC):
                                sc.wait_ge(mms.h, mm0 + 1 + NHC + oc + 1)
                                if oc >= 2:
                                    sc.wait_ge(gps.h, gp0 + 16 * (oc - 1))
                                acs.inc(sc.activation(osb[oc % 2][:, :],
                                                      ps_o[oc % 2][:, :], AF.Copy))

                        @b.gpsimd
                        def _(g):
                            for oc in range(DC):
                                g.wait_ge(acs.h, ac0 + NHC + oc + 1)
                                ap_out = bass.AP(xo, oc * 128 * TL + t0 * TT,
                                                 [[TL, 128], [1, TT]])
                                gps.inc(g.dma_start(out=ap_out, in_=osb[oc % 2][:, :],
                                                    accum_op=ALU.add), 16)

        for t0 in range(NT):
            with blk(nc, f"m_p6_{t0}") as b:
                mm0, ac0, gp0 = mms.n, acs.n, gps.n

                @b.sync
                def _(s):
                    lnld.inc(s.dma_start(
                        out=mt8[:, :],
                        in_=bass.AP(mdram, t0 * TT, [[TL, E], [1, TT]])), 16)

                @b.tensor
                def _(t):
                    t.wait_ge(lnld.h, lnld.n)
                    for oc in range(DC):
                        if oc >= 2:
                            t.wait_ge(acs.h, ac0 + oc - 1)
                        mms.inc(t.matmul(ps_o[oc % 2][:, :],
                                         b2sb[:, oc * 128:(oc + 1) * 128],
                                         mt8[:, :], start=True, stop=True))

                @b.scalar
                def _(sc):
                    for oc in range(DC):
                        sc.wait_ge(mms.h, mm0 + oc + 1)
                        if oc >= 2:
                            sc.wait_ge(gps.h, gp0 + 16 * (oc - 1))
                        acs.inc(sc.activation(osb[oc % 2][:, :], ps_o[oc % 2][:, :],
                                              AF.Copy))

                @b.gpsimd
                def _(g):
                    for oc in range(DC):
                        g.wait_ge(acs.h, ac0 + oc + 1)
                        ap_out = bass.AP(xo, oc * 128 * TL + t0 * TT,
                                         [[TL, 128], [1, TT]])
                        gps.inc(g.dma_start(out=ap_out, in_=osb[oc % 2][:, :],
                                            accum_op=ALU.add), 16)

        with blk(nc, "m_fin") as b:
            @b.sync
            def _(s):
                s.wait_ge(gps.h, gps.n)
                s.wait_ge(sto.h, sto.n)

    return nc


def moe_host_inputs(scores_full, core, ins_w, x_fm):
    sT = np.ascontiguousarray(scores_full.T)
    sbis = np.ascontiguousarray(sT.reshape(128, NTOK // 16))
    slocT = np.ascontiguousarray(sT[:, core * TL:(core + 1) * TL])
    g16 = np.zeros((128, 8), np.float32)
    for r in range(128):
        g16[r, r // 16] = 1.0
    r16 = np.zeros((8, 128), np.float32)
    for r in range(128):
        r16[r // 16, r] = 1.0
    b1f = np.ascontiguousarray(
        ins_w["L0_ffn_b1"].reshape(E, DI // 128, 128).transpose(0, 2, 1))
    return {
        "x": x_fm, "sbis": sbis, "slocT": slocT,
        "w1": ins_w["L0_ffn_W1"], "w2": ins_w["L0_ffn_W2"],
        "b1f": b1f, "b2": ins_w["L0_ffn_b2"],
        "g16": g16, "r16": r16,
        "lohi": np.stack([np.zeros(8, np.float32), np.ones(8, np.float32)], 1),
    }


def emit_ln_stats2(nc, bufs, W=TT):
    xt = bufs["xt"]; xsqc = bufs["xsqc"]
    ps_sum = bufs["ps_sum"]; ps_sq = bufs["ps_sq"]; ps_tr = bufs["ps_tr"]
    m_row = bufs["m_row"]; r_row = bufs["r_row"]; var_row = bufs["var_row"]
    mcol = bufs["mcol"]; rcol = bufs["rcol"]
    nsub = W // 128
    mms2 = bufs["_mms"]; acs2 = bufs["_acs"]

    with blk(nc, "ln2a") as b:
        mm0, ac0 = mms2.n, acs2.n

        @b.scalar
        def _(sc):
            for c in range(DC):
                if c >= 2:
                    sc.wait_ge(mms2.h, mm0 + c - 1)
                acs2.inc(sc.activation(xsqc[c % 2][:, :], xt[:, c * W:(c + 1) * W],
                                       AF.Square))

        @b.tensor
        def _(t):
            for c in range(DC):
                t.matmul(ps_sum[0:1, :W], bufs["ones_col"][:, :],
                         xt[:, c * W:(c + 1) * W],
                         start=(c == 0), stop=(c == DC - 1))
            for c in range(DC):
                t.wait_ge(acs2.h, ac0 + c + 1)
                ins = t.matmul(ps_sq[0:1, :W], bufs["ones_col"][:, :],
                               xsqc[c % 2][:, :],
                               start=(c == 0), stop=(c == DC - 1))
                mms2.inc(ins)

    with blk(nc, "ln2b") as b:
        @b.vector
        def _(v):
            v.tensor_scalar_mul(m_row[0:1, :W], ps_sum[0:1, :W], 1.0 / D)
            v.tensor_scalar_mul(var_row[0:1, :W], ps_sq[0:1, :W], 1.0 / D)

    with blk(nc, "ln2c") as b:
        @b.scalar
        def _(sc):
            sc.activation(r_row[0:1, :W], m_row[0:1, :W], AF.Square)

    with blk(nc, "ln2d") as b:
        @b.vector
        def _(v):
            v.tensor_sub(var_row[0:1, :W], var_row[0:1, :W], r_row[0:1, :W])

    with blk(nc, "ln2e") as b:
        @b.scalar
        def _(sc):
            sc.activation(var_row[0:1, :W], var_row[0:1, :W], AF.Sqrt,
                          bias=bufs["eps1"][0:1, 0:1])

    with blk(nc, "ln2f") as b:
        @b.vector
        def _(v):
            v.reciprocal(r_row[0:1, :W], var_row[0:1, :W])

    with blk(nc, "ln2g") as b:
        @b.tensor
        def _(t):
            one11 = bufs["ones_row"][0:1, 0:1]
            for s in range(nsub):
                t.matmul(ps_tr[:, s:s + 1], m_row[0:1, s * 128:(s + 1) * 128],
                         one11, start=True, stop=True)
            for s in range(nsub):
                t.matmul(ps_tr[:, nsub + s:nsub + s + 1],
                         r_row[0:1, s * 128:(s + 1) * 128], one11,
                         start=True, stop=True)

    with blk(nc, "ln2h") as b:
        @b.vector
        def _(v):
            v.tensor_copy(mcol[:, 0:nsub], ps_tr[:, 0:nsub])
            v.tensor_copy(rcol[:, 0:nsub], ps_tr[:, nsub:2 * nsub])




def build_attn0():
    nc = bass.Bass("TRN2", target_bir_lowering=False, debug=False)
    x = nc.dram_tensor("x", [D, TL], F32, kind="ExternalInput")
    wq_in = nc.dram_tensor("wq", [D, D], F32, kind="ExternalInput")
    wk_in = nc.dram_tensor("wk", [D, D], F32, kind="ExternalInput")
    wv_in = nc.dram_tensor("wv", [D, D], F32, kind="ExternalInput")
    wo_in = nc.dram_tensor("wo", [D, D], F32, kind="ExternalInput")
    wr_in = nc.dram_tensor("wr", [D, E], F32, kind="ExternalInput")
    xo = nc.dram_tensor("xo", [D, TL], F32, kind="ExternalOutput")
    sc_out = nc.dram_tensor("sc", [TL, E], F32, kind="ExternalOutput")
    lnf = nc.dram_tensor("lnf", [D, TL], F32)

    NKC = SEQ // 128
    NQT = SEQ // TT

    with ExitStack() as ctx:
        ones_col = sbuf(ctx, nc, "onec", [128, 1], F32)
        ones_row = sbuf(ctx, nc, "oner", [1, 128], F32)
        eps1 = sbuf(ctx, nc, "eps1", [1, 1], F32)
        c20n = sbuf(ctx, nc, "c20n", [128, 1], F32)
        rzA = sbuf(ctx, nc, "rzA", [1, TT], F32)
        rzB = sbuf(ctx, nc, "rzB", [1, TT], F32)
        psSA = [psum(ctx, nc, f"psSA{i}", [128, TT]) for i in range(2)]
        psSB = [psum(ctx, nc, f"psSB{i}", [128, TT]) for i in range(2)]
        psZA = psum(ctx, nc, "psZA", [128, TT])
        psZB = psum(ctx, nc, "psZB", [128, TT])
        psRBC = psum(ctx, nc, "psRBC", [128, TT])
        psO = psum(ctx, nc, "psO", [128, TT])

        ld = sem(ctx, nc, "ld")
        wsem = sem(ctx, nc, "wsem")
        mms = sem(ctx, nc, "mms")
        acs = sem(ctx, nc, "acs")
        dvs = sem(ctx, nc, "dvs")
        sto = sem(ctx, nc, "sto")

        actx = ctx.enter_context(ExitStack())
        w0 = sbuf(actx, nc, "w0", [128, DC * D], BF16)
        w1 = sbuf(actx, nc, "w1b", [128, DC * D], BF16)
        w2 = sbuf(actx, nc, "w2b", [128, DC * D], BF16)
        q_sb = sbuf(actx, nc, "q_sb", [128, DC * SEQ], BF16)
        k_sb = sbuf(actx, nc, "k_sb", [128, DC * SEQ], BF16)
        v_sb = sbuf(actx, nc, "v_sb", [128, 16 * 1040], BF16)
        znorm = sbuf(actx, nc, "znorm", [128, DC * TT], BF16)
        zsb = sbuf(actx, nc, "zsb", [128, TT], F32)
        pA = [sbuf(actx, nc, f"pA{i}", [128, TT], BF16) for i in range(2)]
        pB = [sbuf(actx, nc, f"pB{i}", [128, TT], BF16) for i in range(2)]
        osb = [sbuf(actx, nc, f"osb{i}", [128, TT], F32) for i in range(2)]
        xc1 = sbuf(actx, nc, "xc1", [128, TT], F32)
        lc1 = sbuf(actx, nc, "lc1", [128, TT], F32)

        def load_w(q, dst, src, sem_):
            for kc in range(DC):
                ap_in = bass.AP(src, kc * 128 * D, [[D, 128], [1, D]])
                sem_.inc(q.dma_start(out=dst[:, kc * D:(kc + 1) * D], in_=ap_in), 16)

        with blk(nc, "a_p0") as b:
            @b.gpsimd
            def _(g):
                load_w(g, w0, wq_in, wsem)
                load_w(g, w1, wk_in, wsem)
                load_w(g, w2, wv_in, wsem)

            @b.vector
            def _(v):
                v.memset(ones_col[:, :], 1.0)
                v.memset(ones_row[:, :], 1.0)
                v.memset(eps1[:, :], LN_EPS)
                v.memset(c20n[:, :], -20.0)
                ap = bass.AP(v_sb, 64, [[16 * 1040, 128], [1040, 16], [65, 16], [1, 1]])
                v.memset(ap, 1.0)

        lbufs = {
            "xt": sbuf(actx, nc, "xt", [128, DC * TT], F32),
            "xsqc": [sbuf(actx, nc, f"xsqc{i}", [128, TT], F32) for i in range(2)],
            "_mms": mms, "_acs": acs,
            "ones_col": ones_col, "ones_row": ones_row, "eps1": eps1,
            "ps_sum": psZA[0:1, 0:TT], "ps_sq": psZB[0:1, 0:TT],
            "ps_tr": psRBC[:, 0:8],
            "m_row": sbuf(actx, nc, "mrow", [1, TT], F32),
            "r_row": sbuf(actx, nc, "rrow", [1, TT], F32),
            "var_row": sbuf(actx, nc, "vrow", [1, TT], F32),
            "mcol": sbuf(actx, nc, "mcol", [128, 4], F32),
            "rcol": sbuf(actx, nc, "rcol", [128, 4], F32),
        }
        lnt = sbuf(actx, nc, "lnta", [128, DC * TT], BF16)
        tmpc = lbufs["xsqc"][0]
        lnfc = [lbufs["xsqc"][1], sbuf(actx, nc, "lnfc1", [128, TT], F32)]

        for bb_ in range(BL):
            if bb_ > 0:
                with blk(nc, f"a_wq{bb_}") as b:
                    @b.gpsimd
                    def _(g):
                        load_w(g, w0, wq_in, wsem)
            for t0 in range(NQT):
                gt = bb_ * NQT + t0
                with blk(nc, f"a_l{gt}") as b:
                    @b.sync
                    def _(s):
                        s.wait_ge(sto.h, sto.n)
                        load_fm_tile(s, lbufs["xt"], x, gt * TT, ld)
                        s.wait_ge(ld.h, ld.n)
                emit_ln_stats2(nc, lbufs)
                with blk(nc, f"a_lb{gt}") as b:
                    @b.tensor
                    def _(t):
                        mms.inc(t.matmul(psZA[:, 0:TT], ones_row[:, :],
                                         lbufs["m_row"][0:1, :], start=True, stop=True))
                        mms.inc(t.matmul(psZB[:, 0:TT], ones_row[:, :],
                                         lbufs["r_row"][0:1, :], start=True, stop=True))

                with blk(nc, f"a_lc{gt}") as b:
                    dv0, st0 = dvs.n, sto.n

                    @b.vector
                    def _(v):
                        for c in range(DC):
                            if c >= 2:
                                v.wait_ge(sto.h, st0 + 16 * (c - 1))
                            v.tensor_sub(tmpc[:, :], lbufs["xt"][:, c * TT:(c + 1) * TT],
                                         psZA[:, 0:TT])
                            v.tensor_mul(lnfc[c % 2][:, :], tmpc[:, :], psZB[:, 0:TT])
                            v.drain()
                            dvs.inc(v.tensor_copy(lnt[:, c * TT:(c + 1) * TT],
                                                  lnfc[c % 2][:, :]))

                    @b.sync
                    def _(s):
                        for c in range(DC):
                            s.wait_ge(dvs.h, dv0 + c + 1)
                            ap_out = bass.AP(lnf, gt * TT + c * 128 * TL,
                                             [[TL, 128], [1, TT]])
                            sto.inc(s.dma_start(out=ap_out, in_=lnfc[c % 2][:, :]), 16)

                with blk(nc, f"a_pr{gt}") as b:
                    mm0, ac0 = mms.n, acs.n

                    @b.tensor
                    def _(t):
                        t.wait_ge(wsem.h, wsem.n)
                        n_ev = 0
                        for oc in range(DC):
                            for (wsb, dst_q) in ((w0, True), (w1, False)):
                                if n_ev >= 2:
                                    t.wait_ge(acs.h, ac0 + n_ev - 1)
                                bank = psSA[n_ev % 2]
                                for kc in range(DC):
                                    ins = t.matmul(
                                        bank[:, :],
                                        wsb[:, kc * D + oc * 128: kc * D + oc * 128 + 128],
                                        lnt[:, kc * TT:(kc + 1) * TT],
                                        start=(kc == 0), stop=(kc == DC - 1))
                                mms.inc(ins)
                                n_ev += 1
                        for sub in range(TT // 128):
                            s4 = t0 * 4 + sub
                            for g2 in range(2):
                                if n_ev >= 2:
                                    t.wait_ge(acs.h, ac0 + n_ev - 1)
                                bank = psSB[n_ev % 2]
                                for kc in range(DC):
                                    ins = t.matmul(
                                        bank[:, :],
                                        lnt[:, kc * TT + sub * 128: kc * TT + sub * 128 + 128],
                                        w2[:, kc * D + g2 * 512: kc * D + g2 * 512 + 512],
                                        start=(kc == 0), stop=(kc == DC - 1))
                                mms.inc(ins)
                                n_ev += 1

                    @b.scalar
                    def _(sc):
                        n_ev = 0
                        for oc in range(DC):
                            for dst_q in (True, False):
                                sc.wait_ge(mms.h, mm0 + n_ev + 1)
                                dst = q_sb if dst_q else k_sb
                                acs.inc(sc.activation(
                                    dst[:, oc * SEQ + t0 * TT: oc * SEQ + (t0 + 1) * TT],
                                    psSA[n_ev % 2][:, :], AF.Copy))
                                n_ev += 1
                        for sub in range(TT // 128):
                            s4 = t0 * 4 + sub
                            for g2 in range(2):
                                sc.wait_ge(mms.h, mm0 + n_ev + 1)
                                ap_out = bass.AP(
                                    v_sb, s4 * 1040 + g2 * 8 * 65,
                                    [[16 * 1040, 128], [65, 8], [1, 64]])
                                acs.inc(sc.activation(ap_out, psSB[n_ev % 2][:, :],
                                                      AF.Copy))
                                n_ev += 1

            with blk(nc, f"a_wo{bb_}") as b:
                @b.gpsimd
                def _(g):
                    load_w(g, w0, wo_in, wsem)

            for qt in range(NQT):
                gq = bb_ * NQT + qt
                for hp in range(DC):
                    with blk(nc, f"a_at{gq}_{hp}") as b:
                        mm0, ac0, dv0 = mms.n, acs.n, dvs.n

                        @b.tensor
                        def _(t):
                            t.wait_ge(wsem.h, wsem.n)
                            for kc in range(NKC):
                                if kc >= 2:
                                    t.wait_ge(acs.h, ac0 + kc - 1)
                                t.matmul(psSA[kc % 2][:, :],
                                         k_sb[0:64, hp * SEQ + kc * 128: hp * SEQ + kc * 128 + 128],
                                         q_sb[0:64, hp * SEQ + qt * TT: hp * SEQ + (qt + 1) * TT],
                                         start=True, stop=True)
                                mms.inc(t.matmul(
                                    psSB[kc % 2][:, :],
                                    k_sb[64:128, hp * SEQ + kc * 128: hp * SEQ + kc * 128 + 128],
                                    q_sb[64:128, hp * SEQ + qt * TT: hp * SEQ + (qt + 1) * TT],
                                    start=True, stop=True))
                                if kc >= 1:
                                    t.wait_ge(acs.h, ac0 + kc)
                                    t.matmul(psZA[0:65, :],
                                             v_sb[:, (kc - 1) * 1040 + (2 * hp) * 65: (kc - 1) * 1040 + (2 * hp) * 65 + 65],
                                             pA[(kc - 1) % 2][:, :],
                                             start=(kc == 1), stop=(kc == NKC))
                                    mms.inc(t.matmul(
                                        psZB[0:65, :],
                                        v_sb[:, (kc - 1) * 1040 + (2 * hp + 1) * 65: (kc - 1) * 1040 + (2 * hp + 1) * 65 + 65],
                                        pB[(kc - 1) % 2][:, :],
                                        start=(kc == 1), stop=(kc == NKC)))
                            kc = NKC
                            t.wait_ge(acs.h, ac0 + kc)
                            t.matmul(psZA[0:65, :],
                                     v_sb[:, (kc - 1) * 1040 + (2 * hp) * 65: (kc - 1) * 1040 + (2 * hp) * 65 + 65],
                                     pA[(kc - 1) % 2][:, :], start=False, stop=True)
                            mms.inc(t.matmul(
                                psZB[0:65, :],
                                v_sb[:, (kc - 1) * 1040 + (2 * hp + 1) * 65: (kc - 1) * 1040 + (2 * hp + 1) * 65 + 65],
                                pB[(kc - 1) % 2][:, :], start=False, stop=True))
                            t.wait_ge(dvs.h, dv0 + 1)
                            t.matmul(psRBC[0:64, :], ones_row[0:1, 0:64], rzA[0:1, :],
                                     start=True, stop=True)
                            mms.inc(t.matmul(psRBC[64:128, :], ones_row[0:1, 0:64],
                                             rzB[0:1, :], start=True, stop=True,
                                             tile_position=(0, 64)))

                        @b.scalar
                        def _(sc):
                            for kc in range(NKC):
                                sc.wait_ge(mms.h, mm0 + (2 * kc + 1 if kc else 1))
                                sc.activation(pA[kc % 2][:, :], psSA[kc % 2][:, :],
                                              AF.Exp, bias=c20n[:, 0:1], scale=0.125)
                                acs.inc(sc.activation(pB[kc % 2][:, :], psSB[kc % 2][:, :],
                                                      AF.Exp, bias=c20n[:, 0:1],
                                                      scale=0.125))
                            sc.wait_ge(mms.h, mm0 + 2 * NKC)
                            sc.activation(zsb[0:64, :], psZA[0:64, :], AF.Copy)
                            acs.inc(sc.activation(zsb[64:128, :], psZB[0:64, :], AF.Copy))

                        @b.vector
                        def _(v):
                            v.wait_ge(mms.h, mm0 + 2 * NKC)
                            v.reciprocal(rzA[0:1, :], psZA[64:65, :])
                            v.reciprocal(rzB[0:1, :], psZB[64:65, :])
                            v.drain()
                            dvs.inc(v.engine_nop())
                            v.wait_ge(mms.h, mm0 + 2 * NKC + 1)
                            v.wait_ge(acs.h, ac0 + NKC + 1)
                            v.tensor_mul(znorm[:, hp * TT:(hp + 1) * TT],
                                         zsb[:, :], psRBC[:, :])

                with blk(nc, f"a_op{gq}") as b:
                    mm0, ac0, dv0, st0 = mms.n, acs.n, dvs.n, sto.n

                    @b.sync
                    def _(s):
                        for oc in range(DC):
                            ap_x = bass.AP(x, oc * 128 * TL + gq * TT, [[TL, 128], [1, TT]])
                            ld.inc(s.dma_start(out=xc1[:, :], in_=ap_x), 16)
                            ap_l = bass.AP(lnf, oc * 128 * TL + gq * TT, [[TL, 128], [1, TT]])
                            ld.inc(s.dma_start(out=lc1[:, :], in_=ap_l), 16)
                            s.wait_ge(dvs.h, dv0 + oc + 1)
                            ap_o = bass.AP(xo, oc * 128 * TL + gq * TT, [[TL, 128], [1, TT]])
                            sto.inc(s.dma_start(out=ap_o, in_=osb[oc % 2][:, :]), 16)
                        s.wait_ge(sto.h, sto.n)

                    @b.tensor
                    def _(t):
                        for oc in range(DC):
                            if oc >= 1:
                                t.wait_ge(acs.h, ac0 + oc)
                            for hp in range(DC):
                                ins = t.matmul(
                                    psO[:, :],
                                    w0[:, hp * D + oc * 128: hp * D + oc * 128 + 128],
                                    znorm[:, hp * TT:(hp + 1) * TT],
                                    start=(hp == 0), stop=(hp == DC - 1))
                            mms.inc(ins)

                    @b.scalar
                    def _(sc):
                        for oc in range(DC):
                            sc.wait_ge(mms.h, mm0 + oc + 1)
                            if oc >= 2:
                                sc.wait_ge(dvs.h, dv0 + oc - 1)
                                sc.wait_ge(sto.h, st0 + 16 * (oc - 1))
                            acs.inc(sc.activation(osb[oc % 2][:, :], psO[:, :], AF.Copy))

                    @b.vector
                    def _(v):
                        for oc in range(DC):
                            v.wait_ge(acs.h, ac0 + oc + 1)
                            v.wait_ge(ld.h, st0 * 0 + mm0 * 0 + ld.n - 16 * (2 * (DC - 1 - oc)))
                            v.tensor_add(osb[oc % 2][:, :], osb[oc % 2][:, :],
                                         xc1[:, :])
                            v.drain()
                            dvs.inc(v.tensor_add(osb[oc % 2][:, :], osb[oc % 2][:, :],
                                                 lc1[:, :]))

        actx.close()
        sbufs = alloc_scores_bufs(ctx, nc, "s3", ps={
            "ps_sum": psSA[0][0:1, 0:TT], "ps_sq": psSA[1][0:1, 0:TT],
            "ps_tr": psSB[0][:, 0:8], "ps_raw": psSB[1][:, 0:32],
            "ps_bc": psZA[:, 0:8]})
        with blk(nc, "a_id") as b:
            @b.vector
            def _(v):
                v.memset(sbufs["ones_col"][:, :], 1.0)
                v.memset(sbufs["ones_row"][:, :], 1.0)
                v.memset(sbufs["eps1"][:, :], LN_EPS)
        emit_scores_prep(nc, sbufs, wr_in, ld)
        st2 = sem(ctx, nc, "st2")
        for t0 in range(NT):
            with blk(nc, f"a_sl{t0}") as b:
                @b.sync
                def _(s):
                    s.wait_ge(st2.h, st2.n)
                    load_fm_tile(s, sbufs["xt"], xo, t0 * TT, ld)
                    s.wait_ge(ld.h, ld.n)
            emit_ln_stats(nc, sbufs)
            emit_scores_tile(nc, sbufs)
            with blk(nc, f"a_ss{t0}") as b:
                @b.sync
                def _(s):
                    for sub in range(TT // 128):
                        row0 = t0 * TT + sub * 128
                        st2.inc(s.dma_start(
                            out=bass.AP(sc_out, row0 * E, [[E, 128], [1, E]]),
                            in_=sbufs["sc_t"][:, sub * 8:(sub + 1) * 8]), 16)
                    s.wait_ge(st2.h, st2.n)

        with blk(nc, "a_fin") as b:
            @b.sync
            def _(s):
                s.wait_ge(sto.h, sto.n)
                s.wait_ge(st2.h, st2.n)

    return nc



def build_l1ffn(nmat=9, skip=()):
    nc = bass.Bass("TRN2", target_bir_lowering=False, debug=False)
    x = nc.dram_tensor("x", [D, TL], F32, kind="ExternalInput")
    gwT_in = nc.dram_tensor("gwT", [D, E], F32, kind="ExternalInput")
    sg_in = nc.dram_tensor("sg", [D, D], F32, kind="ExternalInput")
    su_in = nc.dram_tensor("su", [D, D], F32, kind="ExternalInput")
    sd_in = nc.dram_tensor("sd", [D, D], F32, kind="ExternalInput")
    eg_in = nc.dram_tensor("eg", [E, D, D], F32, kind="ExternalInput")
    eu_in = nc.dram_tensor("eu", [E, D, D], F32, kind="ExternalInput")
    ed_in = nc.dram_tensor("ed", [E, D, D], F32, kind="ExternalInput")
    ident_in = nc.dram_tensor("ident", [128, 128], F32, kind="ExternalInput")
    xo = nc.dram_tensor("xo", [D, TL], F32, kind="ExternalOutput")
    lndram = nc.dram_tensor("lndram", [D, TL], BF16)
    cwdram = nc.dram_tensor("cwdram", [E, TL], F32)

    with ExitStack() as ctx:
        ld = sem(ctx, nc, "ld")
        lnld = sem(ctx, nc, "lnld")
        wsem = sem(ctx, nc, "wsem")
        dvs = sem(ctx, nc, "dvs")
        mms = sem(ctx, nc, "mms")
        acs = sem(ctx, nc, "acs")
        gps = sem(ctx, nc, "gps")
        sto = sem(ctx, nc, "sto")

        ps = [psum(ctx, nc, f"f_ps{i}", [128, TT]) for i in range(8)]
        ps_g = [ps[0], ps[1]]
        ps_u = [ps[2], ps[3]]
        ps_o = [ps[4], ps[5]]
        ps_m1 = ps[6]
        ps_m2 = ps[7]

        sbufs = alloc_scores_bufs(ctx, nc, "f", ps={
            "ps_sum": ps_m1[0:1, 0:TT], "ps_sq": ps_m2[0:1, 0:TT],
            "ps_tr": ps[0][:, 0:8], "ps_raw": ps[1][:, 0:32],
            "ps_bc": ps[2][:, 0:8]})
        ident = sbuf(ctx, nc, "identb", [128, 128], F32)
        lnc_s = [sbuf(ctx, nc, f"lnc{i}", [128, TT], BF16) for i in range(2)]
        cwsb = sbuf(ctx, nc, "cwsb", [8, TL], F32)
        cw_t = sbuf(ctx, nc, "cw_t", [128, 32], F32)
        m2c = sbuf(ctx, nc, "m2c", [128, 4], F32)
        eqm = sbuf(ctx, nc, "eqm", [128, 32], F32)
        l2 = sbuf(ctx, nc, "l2", [128, 32], F32)
        pind = sbuf(ctx, nc, "pind", [128, 32], F32)
        s2c = sbuf(ctx, nc, "s2c", [128, 4], F32)
        denc = sbuf(ctx, nc, "denc", [128, 4], F32)
        mxp = sbuf(ctx, nc, "mxp", [128, 4], F32)

        with blk(nc, "f_p0") as b:
            @b.sync
            def _(s):
                ld.inc(s.dma_start(out=ident[:, :], in_=ident_in.ap()), 16)
                s.wait_ge(ld.h, ld.n)

            @b.vector
            def _(v):
                v.memset(sbufs["ones_col"][:, :], 1.0)
                v.memset(sbufs["ones_row"][:, :], 1.0)
                v.memset(sbufs["eps1"][:, :], LN_EPS)

        emit_scores_prep(nc, sbufs, gwT_in, ld)

        for t0 in range(NT):
            with blk(nc, f"f_l{t0}") as b:
                @b.sync
                def _(s):
                    s.wait_ge(sto.h, sto.n)
                    load_fm_tile(s, sbufs["xt"], x, t0 * TT, lnld)
                    s.wait_ge(lnld.h, lnld.n)
                    store_fm_tile(s, xo, sbufs["xt"], t0 * TT, sto)
            emit_ln_stats(nc, sbufs)
            emit_scores_tile(nc, sbufs)
            with blk(nc, f"f_cw{t0}") as b:
                @b.vector
                def _(v):
                    pre = sbufs["pre"]; exps = sbufs["exps"]
                    v.tensor_reduce(mxp[:, 0:4].unsqueeze(2),
                                    pre[:, 0:32].rearrange("p (s e) -> p s e", e=8),
                                    AX.X, ALU.max)
                    v.drain()
                    for s in range(4):
                        v.tensor_scalar(eqm[:, s * 8:(s + 1) * 8],
                                        pre[:, s * 8:(s + 1) * 8],
                                        mxp[:, s:s + 1], None, ALU.is_ge)
                    v.drain()
                    for s in range(4):
                        v.scalar_tensor_tensor(l2[:, s * 8:(s + 1) * 8],
                                               eqm[:, s * 8:(s + 1) * 8], -1e30,
                                               pre[:, s * 8:(s + 1) * 8],
                                               ALU.mult, ALU.add)
                    v.drain()
                    v.tensor_reduce(m2c[:, 0:4].unsqueeze(2),
                                    l2[:, 0:32].rearrange("p (s e) -> p s e", e=8),
                                    AX.X, ALU.max)
                    v.drain()
                    for s in range(4):
                        v.tensor_scalar(eqm[:, s * 8:(s + 1) * 8],
                                        pre[:, s * 8:(s + 1) * 8],
                                        m2c[:, s:s + 1], None, ALU.is_ge)
                    v.drain()
                    v.tensor_mul(pind[:, 0:32], exps[:, 0:32], eqm[:, 0:32])
                    v.drain()
                    v.tensor_reduce(s2c[:, 0:4].unsqueeze(2),
                                    pind[:, 0:32].rearrange("p (s e) -> p s e", e=8),
                                    AX.X, ALU.add)
                    v.drain()
                    v.scalar_tensor_tensor(denc[:, 0:4], sbufs["zrow"][:, 0:4], 1e-8,
                                           s2c[:, 0:4], ALU.mult, ALU.add)
                    v.drain()
                    v.reciprocal(denc[:, 0:4], denc[:, 0:4])
                    v.drain()
                    for s in range(4):
                        dvs.inc(v.tensor_scalar(cw_t[:, s * 8:(s + 1) * 8],
                                                pind[:, s * 8:(s + 1) * 8],
                                                denc[:, s:s + 1], 0.5,
                                                ALU.mult, ALU.mult))

            with blk(nc, f"f_cwt{t0}") as b:
                @b.tensor
                def _(t):
                    for s in range(4):
                        t.transpose(ps_m1[0:8, s * 128:(s + 1) * 128],
                                    cw_t[:, s * 8:(s + 1) * 8], ident[:, :])

            with blk(nc, f"f_cwe{t0}") as b:
                @b.vector
                def _(v):
                    v.tensor_copy(cwsb[0:8, t0 * TT:(t0 + 1) * TT],
                                  ps_m1[0:8, 0:TT])

            with blk(nc, f"f_lb{t0}") as b:
                @b.tensor
                def _(t):
                    t.matmul(ps_m1[:, 0:TT], sbufs["ones_row"][:, :],
                             sbufs["m_row"][0:1, :], start=True, stop=True)
                    t.matmul(ps_m2[:, 0:TT], sbufs["ones_row"][:, :],
                             sbufs["r_row"][0:1, :], start=True, stop=True)

            with blk(nc, f"f_lc{t0}") as b:
                dv0, st0 = dvs.n, sto.n

                @b.vector
                def _(v):
                    for c in range(DC):
                        if c >= 2:
                            v.wait_ge(sto.h, st0 + 16 * (c - 1))
                        v.tensor_sub(sbufs["xsq"][:, 0:TT],
                                     sbufs["xt"][:, c * TT:(c + 1) * TT],
                                     ps_m1[:, 0:TT])
                        dvs.inc(v.tensor_mul(lnc_s[c % 2][:, :],
                                             sbufs["xsq"][:, 0:TT], ps_m2[:, 0:TT]))

                @b.sync
                def _(s):
                    for c in range(DC):
                        s.wait_ge(dvs.h, dv0 + c + 1)
                        ap_out = bass.AP(lndram, t0 * TT + c * 128 * TL,
                                         [[TL, 128], [1, TT]])
                        sto.inc(s.dma_start(out=ap_out, in_=lnc_s[c % 2][:, :]), 16)

        with blk(nc, "f_cwd") as b:
            @b.sync
            def _(s):
                sto.inc(s.dma_start(out=cwdram.ap(), in_=cwsb[:, :]), 16)
                s.wait_ge(sto.h, sto.n)

        with ExitStack() as c5:
            gg = sbuf(c5, nc, "gg", [128, DC * D], BF16)
            gu = sbuf(c5, nc, "gu", [128, DC * D], BF16)
            dt_ = sbuf(c5, nc, "dt", [128, DC * D], BF16)
            lnt = sbuf(c5, nc, "lnt5", [128, DC * TT], BF16)
            t_sb = sbuf(c5, nc, "tsb", [128, DC * TT], BF16)
            sg_s = [sbuf(c5, nc, f"sgs{i}", [128, TT], F32) for i in range(2)]
            u_s = [sbuf(c5, nc, f"us{i}", [128, TT], F32) for i in range(2)]
            t1f = sbuf(c5, nc, "t1f", [128, TT], F32)
            wbc = sbuf(c5, nc, "wbc", [128, TT], F32)
            osb = [sbuf(c5, nc, f"osb5{i}", [128, TT], F32) for i in range(2)]
            wrow = sbuf(c5, nc, "wrow", [1, TL], F32)

            for mat in range(nmat):
                if "fm_only" in skip:
                    pass
                with blk(nc, f"f_w{mat}") as b:
                    @b.gpsimd
                    def _(g):
                        for kc in range(DC):
                            if mat == 0:
                                srcs = [(sg_in, 0), (su_in, 0), (sd_in, 0)]
                            else:
                                ee = mat - 1
                                srcs = [(eg_in, ee * D * D), (eu_in, ee * D * D),
                                        (ed_in, ee * D * D)]
                            for (srct, base), dst in zip(srcs, (gg, gu, dt_)):
                                ap_in = bass.AP(srct, base + kc * 128 * D,
                                                [[D, 128], [1, D]])
                                wsem.inc(g.dma_start(
                                    out=dst[:, kc * D:(kc + 1) * D], in_=ap_in), 16)
                        if mat > 0:
                            wsem.inc(g.dma_start(
                                out=wrow[0:1, :],
                                in_=bass.AP(cwdram, (mat - 1) * TL,
                                            [[TL, 1], [1, TL]])), 16)

                    if mat == 0:
                        @b.vector
                        def _(v):
                            v.memset(wbc[:, :], 0.5)

                for t0 in range(NT if "fm" not in skip else 0):
                    with blk(nc, f"f_m{mat}t{t0}") as b:
                        mm0, ac0, dv0, gp0 = mms.n, acs.n, dvs.n, gps.n
                        off = 1 if mat > 0 else 0

                        @b.sync
                        def _(s):
                            ap_in = bass.AP(lndram, t0 * TT,
                                            [[TL, 128], [TL * 128, DC], [1, TT]])
                            ap_o = bass.AP(lnt, 0, [[DC * TT, 128], [TT, DC], [1, TT]])
                            lnld.inc(s.dma_start(out=ap_o, in_=ap_in), 16)

                        @b.tensor
                        def _(t):
                            t.wait_ge(wsem.h, wsem.n)
                            t.wait_ge(lnld.h, lnld.n)
                            if mat > 0:
                                mms.inc(t.matmul(ps_m1[:, 0:TT], sbufs["ones_row"][:, :],
                                                 wrow[0:1, t0 * TT:(t0 + 1) * TT],
                                                 start=True, stop=True))
                            for fc in range(DC):
                                if fc >= 2:
                                    t.wait_ge(acs.h, ac0 + 2 * (fc - 2) + 2)
                                for kc in range(DC):
                                    i1 = t.matmul(
                                        ps_g[fc % 2][:, :],
                                        gg[:, kc * D + fc * 128: kc * D + fc * 128 + 128],
                                        lnt[:, kc * TT:(kc + 1) * TT],
                                        start=(kc == 0), stop=(kc == DC - 1))
                                mms.inc(i1)
                                for kc in range(DC):
                                    i2 = t.matmul(
                                        ps_u[fc % 2][:, :],
                                        gu[:, kc * D + fc * 128: kc * D + fc * 128 + 128],
                                        lnt[:, kc * TT:(kc + 1) * TT],
                                        start=(kc == 0), stop=(kc == DC - 1))
                                mms.inc(i2)
                            if "down" not in skip:
                                t.wait_ge(dvs.h, dv0 + off + 2 * DC)
                                for oc in range(DC):
                                    if oc >= 2:
                                        t.wait_ge(acs.h, ac0 + 2 * DC + oc - 1)
                                    for ic in range(DC):
                                        i3 = t.matmul(
                                            ps_o[oc % 2][:, :],
                                            dt_[:, ic * D + oc * 128: ic * D + oc * 128 + 128],
                                            t_sb[:, ic * TT:(ic + 1) * TT],
                                            start=(ic == 0), stop=(ic == DC - 1))
                                    mms.inc(i3)

                        @b.scalar
                        def _(sc):
                            for fc in range(DC):
                                sc.wait_ge(mms.h, mm0 + off + 2 * fc + 1)
                                if fc >= 2:
                                    sc.wait_ge(dvs.h, dv0 + off + 2 * (fc - 2) + 2)
                                acs.inc(sc.activation(sg_s[fc % 2][:, :],
                                                      ps_g[fc % 2][:, :],
                                                      AF.Copy if "silu" in skip else AF.Silu))
                                sc.wait_ge(mms.h, mm0 + off + 2 * fc + 2)
                                acs.inc(sc.activation(u_s[fc % 2][:, :],
                                                      ps_u[fc % 2][:, :], AF.Copy))
                            for oc in range(DC if "down" not in skip else 0):
                                sc.wait_ge(mms.h, mm0 + off + 2 * DC + oc + 1)
                                if oc >= 2:
                                    sc.wait_ge(gps.h, gp0 + 16 * (oc - 1))
                                acs.inc(sc.activation(osb[oc % 2][:, :],
                                                      ps_o[oc % 2][:, :], AF.Copy))

                        @b.vector
                        def _(v):
                            if mat > 0:
                                v.wait_ge(mms.h, mm0 + 1)
                                dvs.inc(v.tensor_copy(wbc[:, :], ps_m1[:, 0:TT]))
                            for fc in range(DC):
                                v.wait_ge(acs.h, ac0 + 2 * fc + 2)
                                dvs.inc(v.tensor_mul(t1f[:, :], sg_s[fc % 2][:, :],
                                                     u_s[fc % 2][:, :]))
                                dvs.inc(v.tensor_mul(t_sb[:, fc * TT:(fc + 1) * TT],
                                                     t1f[:, :], wbc[:, :]))

                        @b.gpsimd
                        def _(g):
                            for oc in range(DC if "down" not in skip else 0):
                                g.wait_ge(acs.h, ac0 + 2 * DC + oc + 1)
                                ap_out = bass.AP(xo, oc * 128 * TL + t0 * TT,
                                                 [[TL, 128], [1, TT]])
                                kw = {} if "add" in skip else {"accum_op": ALU.add}
                                gps.inc(g.dma_start(out=ap_out, in_=osb[oc % 2][:, :],
                                                    **kw), 16)

        with blk(nc, "f_fin") as b:
            @b.sync
            def _(s):
                s.wait_ge(gps.h, gps.n)
                s.wait_ge(sto.h, sto.n)

    return nc


def l1ffn_host_inputs(ins_w, x_fm):
    def padT(a, rows=None):
        aT = np.ascontiguousarray(a.T.astype(np.float32))
        out = np.zeros((D, D), np.float32)
        out[:aT.shape[0], :aT.shape[1]] = aT
        return out

    sg = ins_w["L1_ffn_shared_gate_up"][:I1]
    su = ins_w["L1_ffn_shared_gate_up"][I1:2 * I1]
    sd = ins_w["L1_ffn_shared_down"]
    eg = np.stack([padT(ins_w["L1_ffn_experts_gate_up"][e][:I1]) for e in range(E)])
    eu = np.stack([padT(ins_w["L1_ffn_experts_gate_up"][e][I1:2 * I1]) for e in range(E)])
    ed = np.stack([padT(ins_w["L1_ffn_experts_down"][e]) for e in range(E)])
    return {
        "x": x_fm,
        "gwT": np.ascontiguousarray(ins_w["L1_ffn_gate_weight"].T.astype(np.float32)),
        "sg": padT(sg), "su": padT(su), "sd": padT(sd),
        "eg": eg, "eu": eu, "ed": ed,
        "ident": np.eye(128, dtype=np.float32),
    }



def build_attn1():
    nc = bass.Bass("TRN2", target_bir_lowering=False, debug=False)
    x = nc.dram_tensor("x", [D, TL], F32, kind="ExternalInput")
    wqkv_in = nc.dram_tensor("wqkv", [D, 3 * D], F32, kind="ExternalInput")
    wo_in = nc.dram_tensor("wo", [D, D], F32, kind="ExternalInput")
    ctab_in = nc.dram_tensor("ctab", [128, SEQ], F32, kind="ExternalInput")
    stab_in = nc.dram_tensor("stab", [128, SEQ], F32, kind="ExternalInput")
    p32_in = nc.dram_tensor("p32", [128, 128], F32, kind="ExternalInput")
    tri_in = nc.dram_tensor("tri", [128, 896], F32, kind="ExternalInput")
    xo = nc.dram_tensor("xo", [D, TL], F32, kind="ExternalOutput")

    NKC = SEQ // 128
    NQT = SEQ // TT

    with ExitStack() as ctx:
        ones_col = sbuf(ctx, nc, "onec", [128, 1], F32)
        ones_row = sbuf(ctx, nc, "oner", [1, 128], F32)
        eps1 = sbuf(ctx, nc, "eps1", [1, 1], F32)
        c20n = sbuf(ctx, nc, "c20n", [128, 1], F32)
        rzB = sbuf(ctx, nc, "rzB", [1, TT], F32)
        psSA = [psum(ctx, nc, f"psSA{i}", [128, TT]) for i in range(2)]
        psSB = [psum(ctx, nc, f"psSB{i}", [128, TT]) for i in range(2)]
        psZA = psum(ctx, nc, "psZA", [128, TT])
        psZB = psum(ctx, nc, "psZB", [128, TT])
        psRBC = psum(ctx, nc, "psRBC", [128, TT])
        psO = psum(ctx, nc, "psO", [128, TT])

        ld = sem(ctx, nc, "ld")
        wsem = sem(ctx, nc, "wsem")
        mms = sem(ctx, nc, "mms")
        acs = sem(ctx, nc, "acs")
        dvs = sem(ctx, nc, "dvs")
        sto = sem(ctx, nc, "sto")

        wq = sbuf(ctx, nc, "wq1", [128, DC * D], BF16)
        wk = sbuf(ctx, nc, "wk1", [128, DC * D], BF16)
        wv = sbuf(ctx, nc, "wv1", [128, DC * D], BF16)
        wo = sbuf(ctx, nc, "wo1", [128, DC * D], BF16)
        q_sb = sbuf(ctx, nc, "q_sb1", [128, DC * SEQ], BF16)
        k_sb = sbuf(ctx, nc, "k_sb1", [128, DC * SEQ], BF16)
        v_sb = sbuf(ctx, nc, "v_sb1", [128, 16 * 1040], BF16)
        ctab = sbuf(ctx, nc, "ctab_s", [128, SEQ], BF16)
        stab = sbuf(ctx, nc, "stab_s", [128, SEQ], BF16)
        p32 = sbuf(ctx, nc, "p32s", [128, 128], BF16)
        tri = sbuf(ctx, nc, "tris", [128, 896], BF16)

        xt1h = sbuf(ctx, nc, "xt1", [128, DC * TT], F32)
        znorm = xt1h.bitcast(BF16)
        lbufs = {
            "xt": xt1h,
            "xsqc": [sbuf(ctx, nc, f"xs1{i}", [128, TT], F32) for i in range(2)],
            "_mms": mms, "_acs": acs,
            "ones_col": ones_col, "ones_row": ones_row, "eps1": eps1,
            "ps_sum": psZA[0:1, 0:TT], "ps_sq": psZB[0:1, 0:TT],
            "ps_tr": psRBC[:, 0:8],
            "m_row": sbuf(ctx, nc, "mrow1", [1, TT], F32),
            "r_row": sbuf(ctx, nc, "rrow1", [1, TT], F32),
            "var_row": sbuf(ctx, nc, "vrow1", [1, TT], F32),
            "mcol": sbuf(ctx, nc, "mcol1", [128, 4], F32),
            "rcol": sbuf(ctx, nc, "rcol1", [128, 4], F32),
        }
        lnt = sbuf(ctx, nc, "lnt1", [128, DC * TT], BF16)
        pf32 = lnt.bitcast(F32)
        pA = [lnt[:, 0:TT], lnt[:, 0:TT]]
        pB = [lnt[:, TT:2 * TT], lnt[:, TT:2 * TT]]
        zsb = pf32[:, TT:2 * TT]
        osb = [pf32[:, 2 * TT:3 * TT], pf32[:, 2 * TT:3 * TT]]
        xc1 = zsb
        rzA = pf32[0:1, 3 * TT:3 * TT + TT]
        rtmp = lbufs["xsqc"][0]
        rtmp2 = lbufs["xsqc"][1]

        def load_wslice(g, dst, col0):
            for kc in range(DC):
                ap_in = bass.AP(wqkv_in, kc * 128 * 3 * D + col0,
                                [[3 * D, 128], [1, D]])
                wsem.inc(g.dma_start(out=dst[:, kc * D:(kc + 1) * D], in_=ap_in), 16)

        with blk(nc, "b_p0") as b:
            @b.gpsimd
            def _(g):
                load_wslice(g, wq, 0)
                load_wslice(g, wk, D)
                load_wslice(g, wv, 2 * D)
                for kc in range(DC):
                    ap_in = bass.AP(wo_in, kc * 128 * D, [[D, 128], [1, D]])
                    wsem.inc(g.dma_start(out=wo[:, kc * D:(kc + 1) * D], in_=ap_in), 16)
                wsem.inc(g.dma_start(out=ctab[:, :], in_=ctab_in.ap()), 16)
                wsem.inc(g.dma_start(out=stab[:, :], in_=stab_in.ap()), 16)
                wsem.inc(g.dma_start(out=p32[:, :], in_=p32_in.ap()), 16)
                wsem.inc(g.dma_start(out=tri[:, :], in_=tri_in.ap()), 16)

            @b.vector
            def _(v):
                v.memset(ones_col[:, :], 1.0)
                v.memset(ones_row[:, :], 1.0)
                v.memset(eps1[:, :], LN_EPS)
                v.memset(c20n[:, :], -20.0)
                ap = bass.AP(v_sb, 64, [[16 * 1040, 128], [1040, 16], [65, 16], [1, 1]])
                v.memset(ap, 1.0)

        for bb_ in range(BL):
            for t0 in range(NQT):
                gt = bb_ * NQT + t0
                with blk(nc, f"b_l{gt}") as b:
                    @b.sync
                    def _(s):
                        s.wait_ge(sto.h, sto.n)
                        load_fm_tile(s, lbufs["xt"], x, gt * TT, ld)
                        s.wait_ge(ld.h, ld.n)
                emit_ln_stats2(nc, lbufs)
                with blk(nc, f"b_lb{gt}") as b:
                    @b.tensor
                    def _(t):
                        mms.inc(t.matmul(psZA[:, 0:TT], ones_row[:, :],
                                         lbufs["m_row"][0:1, :], start=True, stop=True))
                        mms.inc(t.matmul(psZB[:, 0:TT], ones_row[:, :],
                                         lbufs["r_row"][0:1, :], start=True, stop=True))

                with blk(nc, f"b_lc{gt}") as b:
                    @b.vector
                    def _(v):
                        for c in range(DC):
                            v.tensor_sub(rtmp[:, :], lbufs["xt"][:, c * TT:(c + 1) * TT],
                                         psZA[:, 0:TT])
                            v.tensor_mul(lnt[:, c * TT:(c + 1) * TT], rtmp[:, :],
                                         psZB[:, 0:TT])

                with blk(nc, f"b_pr{gt}") as b:
                    mm0, ac0 = mms.n, acs.n

                    @b.tensor
                    def _(t):
                        t.wait_ge(wsem.h, wsem.n)
                        n_ev = 0
                        for oc in range(DC):
                            for wsb in (wq, wk):
                                if n_ev >= 2:
                                    t.wait_ge(acs.h, ac0 + n_ev - 1)
                                bank = psSA[n_ev % 2]
                                for kc in range(DC):
                                    ins = t.matmul(
                                        bank[:, :],
                                        wsb[:, kc * D + oc * 128: kc * D + oc * 128 + 128],
                                        lnt[:, kc * TT:(kc + 1) * TT],
                                        start=(kc == 0), stop=(kc == DC - 1))
                                mms.inc(ins)
                                n_ev += 1
                        for sub in range(TT // 128):
                            for g2 in range(2):
                                if n_ev >= 2:
                                    t.wait_ge(acs.h, ac0 + n_ev - 1)
                                bank = psSB[n_ev % 2]
                                for kc in range(DC):
                                    ins = t.matmul(
                                        bank[:, :],
                                        lnt[:, kc * TT + sub * 128: kc * TT + sub * 128 + 128],
                                        wv[:, kc * D + g2 * 512: kc * D + g2 * 512 + 512],
                                        start=(kc == 0), stop=(kc == DC - 1))
                                mms.inc(ins)
                                n_ev += 1

                    @b.scalar
                    def _(sc):
                        n_ev = 0
                        for oc in range(DC):
                            for dst in (q_sb, k_sb):
                                sc.wait_ge(mms.h, mm0 + n_ev + 1)
                                acs.inc(sc.activation(
                                    dst[:, oc * SEQ + t0 * TT: oc * SEQ + (t0 + 1) * TT],
                                    psSA[n_ev % 2][:, :], AF.Copy))
                                n_ev += 1
                        for sub in range(TT // 128):
                            s4 = t0 * 4 + sub
                            for g2 in range(2):
                                sc.wait_ge(mms.h, mm0 + n_ev + 1)
                                ap_out = bass.AP(
                                    v_sb, s4 * 1040 + g2 * 8 * 65,
                                    [[16 * 1040, 128], [65, 8], [1, 64]])
                                acs.inc(sc.activation(ap_out, psSB[n_ev % 2][:, :],
                                                      AF.Copy))
                                n_ev += 1

            for qk, buf in (("q", q_sb), ("k", k_sb)):
                for hp in range(DC):
                    with blk(nc, f"b_ro{bb_}{qk}{hp}") as b:
                        mm0 = mms.n

                        @b.tensor
                        def _(t):
                            for qt in range(NQT):
                                bank = (psSA + psSB)[qt]
                                mms.inc(t.matmul(
                                    bank[:, :], p32[:, :],
                                    buf[:, hp * SEQ + qt * TT: hp * SEQ + (qt + 1) * TT],
                                    start=True, stop=True))

                        @b.vector
                        def _(v):
                            for qt in range(NQT):
                                bank = (psSA + psSB)[qt]
                                v.wait_ge(mms.h, mm0 + qt + 1)
                                v.tensor_mul(rtmp[:, :], bank[:, :],
                                             stab[:, qt * TT:(qt + 1) * TT])
                                v.tensor_mul(rtmp2[:, :],
                                             buf[:, hp * SEQ + qt * TT: hp * SEQ + (qt + 1) * TT],
                                             ctab[:, qt * TT:(qt + 1) * TT])
                                v.tensor_add(
                                    buf[:, hp * SEQ + qt * TT: hp * SEQ + (qt + 1) * TT],
                                    rtmp[:, :], rtmp2[:, :])

            for qt in range(NQT):
                gq = bb_ * NQT + qt
                nkc = 4 * (qt + 1)
                for hp in range(DC):
                    with blk(nc, f"b_at{gq}_{hp}") as b:
                        mm0, ac0, dv0 = mms.n, acs.n, dvs.n
                        zw = []

                        @b.scalar
                        def _(sc):
                            nonlocal zw
                            for kc in range(nkc):
                                sc.wait_ge(mms.h, mm0 + (2 * kc + 1 if kc else 1))
                                sc.activation(pA[kc % 2][:, :], psSA[kc % 2][:, :],
                                              AF.Exp, bias=c20n[:, 0:1], scale=0.125)
                                acs.inc(sc.activation(pB[kc % 2][:, :], psSB[kc % 2][:, :],
                                                      AF.Exp, bias=c20n[:, 0:1],
                                                      scale=0.125))
                            sc.wait_ge(mms.h, mm0 + 2 * nkc)
                            sc.activation(zsb[0:64, :], psZA[0:64, :], AF.Copy)
                            acs.inc(sc.activation(zsb[64:128, :], psZB[0:64, :], AF.Copy))

                        @b.vector
                        def _(v):
                            for kc in range(nkc):
                                if kc >= 4 * qt:
                                    j = kc - 4 * qt
                                    v.wait_ge(acs.h, ac0 + kc + 1)
                                    o_j = 384 - 128 * j
                                    v.tensor_mul(pA[kc % 2][:, :], pA[kc % 2][:, :],
                                                 tri[:, o_j:o_j + TT])
                                    dvs.inc(v.tensor_mul(pB[kc % 2][:, :], pB[kc % 2][:, :],
                                                         tri[:, o_j:o_j + TT]))
                            v.wait_ge(mms.h, mm0 + 2 * nkc)
                            v.reciprocal(rzA[0:1, :], psZA[64:65, :])
                            v.reciprocal(rzB[0:1, :], psZB[64:65, :])
                            v.drain()
                            dvs.inc(v.engine_nop())
                            v.wait_ge(mms.h, mm0 + 2 * nkc + 1)
                            v.wait_ge(acs.h, ac0 + nkc + 1)
                            v.tensor_mul(znorm[:, hp * TT:(hp + 1) * TT],
                                         zsb[:, :], psRBC[:, :])

                        @b.tensor
                        def _(t):
                            ndv = 0
                            for kc in range(nkc):
                                if kc >= 2:
                                    t.wait_ge(acs.h, ac0 + kc - 1)
                                t.matmul(psSA[kc % 2][:, :],
                                         k_sb[0:64, hp * SEQ + kc * 128: hp * SEQ + kc * 128 + 128],
                                         q_sb[0:64, hp * SEQ + qt * TT: hp * SEQ + (qt + 1) * TT],
                                         start=True, stop=True)
                                mms.inc(t.matmul(
                                    psSB[kc % 2][:, :],
                                    k_sb[64:128, hp * SEQ + kc * 128: hp * SEQ + kc * 128 + 128],
                                    q_sb[64:128, hp * SEQ + qt * TT: hp * SEQ + (qt + 1) * TT],
                                    start=True, stop=True))
                                if kc >= 1:
                                    kp = kc - 1
                                    if kp >= 4 * qt:
                                        ndv += 1
                                        t.wait_ge(dvs.h, dv0 + ndv)
                                    else:
                                        t.wait_ge(acs.h, ac0 + kc)
                                    t.matmul(psZA[0:65, :],
                                             v_sb[:, kp * 1040 + (2 * hp) * 65: kp * 1040 + (2 * hp) * 65 + 65],
                                             pA[kp % 2][:, :],
                                             start=(kp == 0), stop=False)
                                    mms.inc(t.matmul(
                                        psZB[0:65, :],
                                        v_sb[:, kp * 1040 + (2 * hp + 1) * 65: kp * 1040 + (2 * hp + 1) * 65 + 65],
                                        pB[kp % 2][:, :],
                                        start=(kp == 0), stop=False))
                            kp = nkc - 1
                            ndv += 1
                            t.wait_ge(dvs.h, dv0 + ndv)
                            t.matmul(psZA[0:65, :],
                                     v_sb[:, kp * 1040 + (2 * hp) * 65: kp * 1040 + (2 * hp) * 65 + 65],
                                     pA[kp % 2][:, :], start=(kp == 0), stop=True)
                            mms.inc(t.matmul(
                                psZB[0:65, :],
                                v_sb[:, kp * 1040 + (2 * hp + 1) * 65: kp * 1040 + (2 * hp + 1) * 65 + 65],
                                pB[kp % 2][:, :], start=(kp == 0), stop=True))
                            t.wait_ge(dvs.h, dv0 + ndv + 1)
                            t.matmul(psRBC[0:64, :], ones_row[0:1, 0:64], rzA[0:1, :],
                                     start=True, stop=True)
                            mms.inc(t.matmul(psRBC[64:128, :], ones_row[0:1, 0:64],
                                             rzB[0:1, :], start=True, stop=True,
                                             tile_position=(0, 64)))

                with blk(nc, f"b_op{gq}") as b:
                    mm0, ac0, dv0, st0 = mms.n, acs.n, dvs.n, sto.n

                    @b.sync
                    def _(s):
                        for oc in range(DC):
                            ap_x = bass.AP(x, oc * 128 * TL + gq * TT, [[TL, 128], [1, TT]])
                            ld.inc(s.dma_start(out=xc1[:, :], in_=ap_x), 16)
                            s.wait_ge(dvs.h, dv0 + oc + 1)
                            ap_o = bass.AP(xo, oc * 128 * TL + gq * TT, [[TL, 128], [1, TT]])
                            sto.inc(s.dma_start(out=ap_o, in_=osb[oc % 2][:, :]), 16)
                        s.wait_ge(sto.h, sto.n)

                    @b.tensor
                    def _(t):
                        for oc in range(DC):
                            if oc >= 1:
                                t.wait_ge(acs.h, ac0 + oc)
                            for hp in range(DC):
                                ins = t.matmul(
                                    psO[:, :],
                                    wo[:, hp * D + oc * 128: hp * D + oc * 128 + 128],
                                    znorm[:, hp * TT:(hp + 1) * TT],
                                    start=(hp == 0), stop=(hp == DC - 1))
                            mms.inc(ins)

                    @b.scalar
                    def _(sc):
                        for oc in range(DC):
                            sc.wait_ge(mms.h, mm0 + oc + 1)
                            if oc >= 1:
                                sc.wait_ge(dvs.h, dv0 + oc)
                                sc.wait_ge(sto.h, st0 + 16 * oc)
                            acs.inc(sc.activation(osb[oc % 2][:, :], psO[:, :], AF.Copy))

                    @b.vector
                    def _(v):
                        for oc in range(DC):
                            v.wait_ge(acs.h, ac0 + oc + 1)
                            v.wait_ge(ld.h, ld.n - 16 * (DC - 1 - oc))
                            dvs.inc(v.tensor_add(osb[oc % 2][:, :], osb[oc % 2][:, :],
                                                 xc1[:, :]))

        with blk(nc, "b_fin") as b:
            @b.sync
            def _(s):
                s.wait_ge(sto.h, sto.n)

    return nc


def attn1_host_inputs(ins_w, x_fm):
    half = DH // 2
    freqs = np.arange(half, dtype=np.float32) / half
    theta = 1.0 / 10000.0 ** freqs
    t = np.arange(SEQ, dtype=np.float32)
    ang = t[:, None] * theta[None, :]
    cos, sin = np.cos(ang), np.sin(ang)
    ctab = np.zeros((128, SEQ), np.float32)
    stab = np.zeros((128, SEQ), np.float32)
    for r in range(128):
        i = r % half
        ctab[r] = cos[:, i]
        stab[r] = sin[:, i] * (1.0 if (r % DH) >= half else -1.0)
    p32 = np.zeros((128, 128), np.float32)
    for k in range(128):
        p32[k, k ^ 32] = 1.0
    tri = np.zeros((128, 896), np.float32)
    for p in range(128):
        tri[p, p + 384:] = 1.0
    return {
        "x": x_fm, "wqkv": ins_w["L1_attn_W_qkv"], "wo": ins_w["L1_attn_W_o"],
        "ctab": ctab, "stab": stab, "p32": p32, "tri": tri,
    }


def _run(nc, maps):
    res = run_bass_kernel_spmd(nc, maps, core_ids=list(range(NCORES)))
    return res.results


def kernel(**inputs):
    ins = {k: np.asarray(v) for k, v in inputs.items()}
    x_fm = _fm_shards(ins["L0_attn_x"].astype(np.float32, copy=False))
    ident = np.eye(128, dtype=np.float32)
    wr = np.ascontiguousarray(ins["L0_ffn_Wr"].astype(np.float32, copy=False))

    nc_sc = build_scores()
    r = _run(nc_sc, [{"x": s, "wr": wr, "ident": ident} for s in x_fm])
    scores1 = np.concatenate([rr["sc"] for rr in r], axis=0)

    nc_moe = build_moe()
    r = _run(nc_moe, [moe_host_inputs(scores1, c, ins, x_fm[c])
                      for c in range(NCORES)])
    x1 = [rr["xo"] for rr in r]

    nc_a0 = build_attn0()
    r = _run(nc_a0, [{"x": x1[c], "wq": ins["L0_attn_Wq"], "wk": ins["L0_attn_Wk"],
                      "wv": ins["L0_attn_Wv"], "wo": ins["L0_attn_Wout"], "wr": wr}
                     for c in range(NCORES)])
    x2 = [rr["xo"] for rr in r]
    scores3 = np.concatenate([rr["sc"] for rr in r], axis=0)

    r = _run(nc_moe, [moe_host_inputs(scores3, c, ins, x2[c])
                      for c in range(NCORES)])
    x3 = [rr["xo"] for rr in r]

    nc_f1 = build_l1ffn()
    r = _run(nc_f1, [l1ffn_host_inputs(ins, x3[c]) for c in range(NCORES)])
    x4 = [rr["xo"] for rr in r]

    nc_a1 = build_attn1()
    r = _run(nc_a1, [attn1_host_inputs(ins, x4[c]) for c in range(NCORES)])
    x5 = [rr["xo"] for rr in r]

    r = _run(nc_f1, [l1ffn_host_inputs(ins, x5[c]) for c in range(NCORES)])
    x6 = [rr["xo"] for rr in r]

    out = np.empty((16, SEQ, D), dtype=np.float32)
    for c in range(NCORES):
        out[BL * c:BL * c + BL] = x6[c].reshape(D, BL, SEQ).transpose(1, 2, 0)
    return out


# version 04
# speedup vs baseline: 1.0071x; 1.0071x over previous
import math
import numpy as np
import ml_dtypes
from contextlib import ExitStack

import concourse.bass as bass
import concourse.mybir as mybir
from concourse.bass_utils import run_bass_kernel_spmd

F32 = mybir.dt.float32
BF16 = mybir.dt.bfloat16
U32 = mybir.dt.uint32
AF = mybir.ActivationFunctionType
ALU = mybir.AluOpType
AX = mybir.AxisListType

NCORES = 8
D = 1024
DC = D // 128
SEQ = 2048
BL = 2
TL = BL * SEQ
TT = 512
NT = TL // TT
E = 8
K_SEL = 8192
NTOK = 16 * SEQ
DI = 4096
NH = 16
DH = 64
I1 = 938
LN_EPS = 1e-5
BIS_ITERS = 42


class Sem:
    def __init__(self, h):
        self.h = h
        self.n = 0

    def inc(self, ins, v=1):
        self.n += v
        ins.then_inc(self.h, v)
        return self.n



_BLK = [0]


def blk(nc, name):
    _BLK[0] += 1
    return nc.Block(f"{name}_{_BLK[0]}")

def sem(ctx, nc, name):
    return Sem(ctx.enter_context(nc.semaphore(name)))


def sbuf(ctx, nc, name, shape, dt):
    return ctx.enter_context(nc.sbuf_tensor(name, shape, dt))


def psum(ctx, nc, name, shape, dt=F32):
    return ctx.enter_context(nc.psum_tensor(name, shape, dt))


def load_fm_tile(q, dst, src_hbm, t0, sem_, width=TT, nchunk=DC, src_cols=TL):
    ap_in = bass.AP(src_hbm, t0,
                    [[src_cols, 128], [src_cols * 128, nchunk], [1, width]])
    ap_out = bass.AP(dst, 0, [[nchunk * width, 128], [width, nchunk], [1, width]])
    sem_.inc(q.dma_start(out=ap_out, in_=ap_in), 16)


def store_fm_tile(q, dst_hbm, src, t0, sem_, width=TT, nchunk=DC, dst_cols=TL,
                  accum=False):
    ap_out = bass.AP(dst_hbm, t0,
                     [[dst_cols, 128], [dst_cols * 128, nchunk], [1, width]])
    ap_in = bass.AP(src, 0, [[nchunk * width, 128], [width, nchunk], [1, width]])
    kw = {"accum_op": ALU.add} if accum else {}
    sem_.inc(q.dma_start(out=ap_out, in_=ap_in, **kw), 16)



def emit_ln_stats(nc, bufs, W=TT):
    xt = bufs["xt"]; xsq = bufs["xsq"]
    ps_sum = bufs["ps_sum"]; ps_sq = bufs["ps_sq"]; ps_tr = bufs["ps_tr"]
    m_row = bufs["m_row"]; r_row = bufs["r_row"]; var_row = bufs["var_row"]
    mcol = bufs["mcol"]; rcol = bufs["rcol"]
    nsub = W // 128

    with blk(nc, "lnsq") as b:
        @b.scalar
        def _(sc):
            sc.activation(xsq[:, :DC * W], xt[:, :DC * W], AF.Square)

    with blk(nc, "lnmm") as b:
        @b.tensor
        def _(t):
            for c in range(DC):
                t.matmul(ps_sum[0:1, :W], bufs["ones_col"][:, :],
                         xt[:, c * W:(c + 1) * W],
                         start=(c == 0), stop=(c == DC - 1))
            for c in range(DC):
                t.matmul(ps_sq[0:1, :W], bufs["ones_col"][:, :],
                         xsq[:, c * W:(c + 1) * W],
                         start=(c == 0), stop=(c == DC - 1))

    with blk(nc, "lnst1") as b:
        @b.vector
        def _(v):
            v.tensor_scalar_mul(m_row[0:1, :W], ps_sum[0:1, :W], 1.0 / D)
            v.tensor_scalar_mul(var_row[0:1, :W], ps_sq[0:1, :W], 1.0 / D)

    with blk(nc, "lnst2") as b:
        @b.scalar
        def _(sc):
            sc.activation(r_row[0:1, :W], m_row[0:1, :W], AF.Square)

    with blk(nc, "lnst3") as b:
        @b.vector
        def _(v):
            v.tensor_sub(var_row[0:1, :W], var_row[0:1, :W], r_row[0:1, :W])

    with blk(nc, "lnst4") as b:
        @b.scalar
        def _(sc):
            sc.activation(var_row[0:1, :W], var_row[0:1, :W], AF.Sqrt,
                          bias=bufs["eps1"][0:1, 0:1])

    with blk(nc, "lnst5") as b:
        @b.vector
        def _(v):
            v.reciprocal(r_row[0:1, :W], var_row[0:1, :W])

    with blk(nc, "lntr") as b:
        @b.tensor
        def _(t):
            one11 = bufs["ones_row"][0:1, 0:1]
            for s in range(nsub):
                t.matmul(ps_tr[:, s:s + 1], m_row[0:1, s * 128:(s + 1) * 128],
                         one11, start=True, stop=True)
            for s in range(nsub):
                t.matmul(ps_tr[:, nsub + s:nsub + s + 1],
                         r_row[0:1, s * 128:(s + 1) * 128], one11,
                         start=True, stop=True)

    with blk(nc, "lnev") as b:
        @b.vector
        def _(v):
            v.tensor_copy(mcol[:, 0:nsub], ps_tr[:, 0:nsub])
            v.tensor_copy(rcol[:, 0:nsub], ps_tr[:, nsub:2 * nsub])



def emit_scores_tile(nc, bufs, W=TT):
    xt = bufs["xt"]; wr_sb = bufs["wr_sb"]; csw_bc = bufs["csw_bc"]
    ps_raw = bufs["ps_raw"]
    mcol = bufs["mcol"]; rcol = bufs["rcol"]
    neg2 = bufs["neg2"]; pre = bufs["pre"]
    mx = bufs["mx"]; zrow = bufs["zrow"]
    exps = bufs["exps"]; sc_t = bufs["sc_t"]
    nsub = W // 128

    with blk(nc, "scraw") as b:
        @b.tensor
        def _(t):
            for s in range(nsub):
                for c in range(DC):
                    t.matmul(ps_raw[:, s * 8:(s + 1) * 8],
                             xt[:, c * W + s * 128: c * W + s * 128 + 128],
                             wr_sb[:, c * 8:(c + 1) * 8],
                             start=(c == 0), stop=(c == DC - 1))

    with blk(nc, "scadj0") as b:
        @b.vector
        def _(v):
            v.tensor_scalar_mul(bufs["rneg"][:, 0:nsub], rcol[:, 0:nsub], -1.0)

    with blk(nc, "scadj") as b:
        @b.vector
        def _(v):
            for s in range(nsub):
                v.scalar_tensor_tensor(neg2[:, s * 8:(s + 1) * 8], csw_bc[:, 0:8],
                                       mcol[:, s:s + 1], ps_raw[:, s * 8:(s + 1) * 8],
                                       ALU.mult, ALU.subtract)
            for s in range(nsub):
                v.tensor_scalar(pre[:, s * 8:(s + 1) * 8], neg2[:, s * 8:(s + 1) * 8],
                                bufs["rneg"][:, s:s + 1], None, ALU.mult)
            v.tensor_reduce(mx[:, 0:nsub].unsqueeze(2),
                            pre[:, 0:nsub * 8].rearrange("p (s e) -> p s e", e=8),
                            AX.X, ALU.max, negate=True)

    with blk(nc, "scexp") as b:
        @b.scalar
        def _(sc):
            for s in range(nsub):
                sc.activation(exps[:, s * 8:(s + 1) * 8], pre[:, s * 8:(s + 1) * 8],
                              AF.Exp, bias=mx[:, s:s + 1],
                              accum_out=zrow[:, s:s + 1])

    with blk(nc, "scdiv") as b:
        @b.vector
        def _(v):
            v.reciprocal(bufs["rz"][:, 0:nsub], zrow[:, 0:nsub])

    with blk(nc, "scdiv2") as b:
        @b.vector
        def _(v):
            for s in range(nsub):
                v.tensor_scalar(sc_t[:, s * 8:(s + 1) * 8], exps[:, s * 8:(s + 1) * 8],
                                bufs["rz"][:, s:s + 1], None, ALU.mult)


def alloc_scores_bufs(ctx, nc, tag, W=TT, ps=None):
    nsub = W // 128
    if ps is None:
        ps = {
            "ps_sum": psum(ctx, nc, f"{tag}_pssum", [1, W]),
            "ps_sq": psum(ctx, nc, f"{tag}_pssq", [1, W]),
            "ps_tr": psum(ctx, nc, f"{tag}_pstr", [128, 2 * nsub]),
            "ps_raw": psum(ctx, nc, f"{tag}_psraw", [128, nsub * 8]),
            "ps_bc": psum(ctx, nc, f"{tag}_psbc", [128, 8]),
        }
    return {
        **ps,
        "xt": sbuf(ctx, nc, f"{tag}_xt", [128, DC * W], F32),
        "xsq": sbuf(ctx, nc, f"{tag}_xsq", [128, DC * W], F32),
        "ones_col": sbuf(ctx, nc, f"{tag}_onec", [128, 1], F32),
        "eps1": sbuf(ctx, nc, f"{tag}_eps1", [1, 1], F32),
        "ones_row": sbuf(ctx, nc, f"{tag}_oner", [1, 128], F32),
        "ident_f32": sbuf(ctx, nc, f"{tag}_id", [128, 128], F32),
        "m_row": sbuf(ctx, nc, f"{tag}_mrow", [1, W], F32),
        "r_row": sbuf(ctx, nc, f"{tag}_rrow", [1, W], F32),
        "var_row": sbuf(ctx, nc, f"{tag}_vrow", [1, W], F32),
        "mcol": sbuf(ctx, nc, f"{tag}_mcol", [128, nsub], F32),
        "rcol": sbuf(ctx, nc, f"{tag}_rcol", [128, nsub], F32),
        "rneg": sbuf(ctx, nc, f"{tag}_rneg", [128, nsub], F32),
        "wr_sb": sbuf(ctx, nc, f"{tag}_wr", [128, DC * 8], F32),
        "csw_bc": sbuf(ctx, nc, f"{tag}_csw", [128, 8], F32),
        "neg2": sbuf(ctx, nc, f"{tag}_neg2", [128, nsub * 8], F32),
        "pre": sbuf(ctx, nc, f"{tag}_pre", [128, nsub * 8], F32),
        "mx": sbuf(ctx, nc, f"{tag}_mx", [128, nsub], F32),
        "zrow": sbuf(ctx, nc, f"{tag}_zrow", [128, nsub], F32),
        "rz": sbuf(ctx, nc, f"{tag}_rz", [128, nsub], F32),
        "exps": sbuf(ctx, nc, f"{tag}_exps", [128, nsub * 8], F32),
        "sc_t": sbuf(ctx, nc, f"{tag}_sct", [128, nsub * 8], F32),
    }


def emit_scores_prep(nc, bufs, wr_hbm, ld):
    with blk(nc, "scp1") as b:
        @b.sync
        def _(s):
            ap_in = bass.AP(wr_hbm, 0, [[8, 128], [8 * 128, DC], [1, 8]])
            ap_out = bass.AP(bufs["wr_sb"], 0, [[DC * 8, 128], [8, DC], [1, 8]])
            ld.inc(s.dma_start(out=ap_out, in_=ap_in), 16)
            s.wait_ge(ld.h, ld.n)

        @b.gpsimd
        def _(g):
            g.memset(bufs["ones_col"][:, :], 1.0)
            g.memset(bufs["ones_row"][:, :], 1.0)
            g.memset(bufs["eps1"][:, :], LN_EPS)

    with blk(nc, "scp2") as b:
        @b.tensor
        def _(t):
            for c in range(DC):
                t.matmul(bufs["ps_bc"][0:1, 0:8], bufs["ones_col"][:, :],
                         bufs["wr_sb"][:, c * 8:(c + 1) * 8],
                         start=(c == 0), stop=(c == DC - 1))

    with blk(nc, "scp3") as b:
        @b.vector
        def _(v):
            v.tensor_copy(bufs["csw_bc"][0:1, 0:8], bufs["ps_bc"][0:1, 0:8])

    with blk(nc, "scp4") as b:
        @b.tensor
        def _(t):
            t.matmul(bufs["ps_bc"][:, 0:8], bufs["ones_row"][:, :],
                     bufs["csw_bc"][0:1, 0:8], start=True, stop=True)

    with blk(nc, "scp5") as b:
        @b.vector
        def _(v):
            v.tensor_copy(bufs["csw_bc"][:, 0:8], bufs["ps_bc"][:, 0:8])


def build_scores(debug=False):
    nc = bass.Bass("TRN2", target_bir_lowering=False, debug=False)
    x = nc.dram_tensor("x", [D, TL], F32, kind="ExternalInput")
    wr = nc.dram_tensor("wr", [D, E], F32, kind="ExternalInput")
    ident_in = nc.dram_tensor("ident", [128, 128], F32, kind="ExternalInput")
    sc_out = nc.dram_tensor("sc", [TL, E], F32, kind="ExternalOutput")
    if debug:
        dbg_m = nc.dram_tensor("dbg_m", [1, TT], F32, kind="ExternalOutput")
        dbg_r = nc.dram_tensor("dbg_r", [1, TT], F32, kind="ExternalOutput")
        dbg_pre = nc.dram_tensor("dbg_pre", [128, 32], F32, kind="ExternalOutput")
        dbg_csw = nc.dram_tensor("dbg_csw", [128, 8], F32, kind="ExternalOutput")
        dbg_exps = nc.dram_tensor("dbg_exps", [128, 32], F32, kind="ExternalOutput")
        dbg_z = nc.dram_tensor("dbg_z", [128, 4], F32, kind="ExternalOutput")
        dbg_sct = nc.dram_tensor("dbg_sct", [128, 32], F32, kind="ExternalOutput")
        dbg_mx = nc.dram_tensor("dbg_mx", [128, 4], F32, kind="ExternalOutput")

    with ExitStack() as ctx:
        bufs = alloc_scores_bufs(ctx, nc, "s")
        ld = sem(ctx, nc, "ld")
        st = sem(ctx, nc, "st")

        with blk(nc, "id") as b:
            @b.sync
            def _(s):
                ld.inc(s.dma_start(out=bufs["ident_f32"][:, :],
                                   in_=ident_in.ap()), 16)
                s.wait_ge(ld.h, ld.n)

        emit_scores_prep(nc, bufs, wr, ld)

        for t0 in range(NT):
            with blk(nc, f"xl{t0}") as b:
                @b.sync
                def _(s):
                    load_fm_tile(s, bufs["xt"], x, t0 * TT, ld)
                    s.wait_ge(ld.h, ld.n)
            emit_ln_stats(nc, bufs)
            emit_scores_tile(nc, bufs)
            if debug and t0 == 0:
                with blk(nc, "dbg") as b:
                    @b.sync
                    def _(s):
                        st.inc(s.dma_start(out=dbg_m.ap(), in_=bufs["m_row"][0:1, :]), 16)
                        st.inc(s.dma_start(out=dbg_r.ap(), in_=bufs["r_row"][0:1, :]), 16)
                        st.inc(s.dma_start(out=dbg_pre.ap(), in_=bufs["pre"][:, :]), 16)
                        st.inc(s.dma_start(out=dbg_csw.ap(), in_=bufs["csw_bc"][:, :]), 16)
                        st.inc(s.dma_start(out=dbg_exps.ap(), in_=bufs["exps"][:, :]), 16)
                        st.inc(s.dma_start(out=dbg_z.ap(), in_=bufs["zrow"][:, :]), 16)
                        st.inc(s.dma_start(out=dbg_sct.ap(), in_=bufs["sc_t"][:, :]), 16)
                        st.inc(s.dma_start(out=dbg_mx.ap(), in_=bufs["mx"][:, :]), 16)
            with blk(nc, f"xs{t0}") as b:
                @b.sync
                def _(s):
                    for sub in range(TT // 128):
                        row0 = t0 * TT + sub * 128
                        st.inc(s.dma_start(
                            out=bass.AP(sc_out, row0 * E, [[E, 128], [1, E]]),
                            in_=bufs["sc_t"][:, sub * 8:(sub + 1) * 8]), 16)
                    s.wait_ge(st.h, st.n)

        with blk(nc, "fin") as b:
            @b.sync
            def _(s):
                s.wait_ge(st.h, st.n)

    return nc



def _host_constants():
    return {"ident": np.eye(128, dtype=np.float32)}


def _fm_shards(x):
    out = []
    for c in range(NCORES):
        xc = x[BL * c:BL * c + BL]
        xc = np.ascontiguousarray(
            xc.transpose(2, 0, 1).reshape(D, TL)).astype(np.float32, copy=False)
        out.append(xc)
    return out


def run_scores(x_fm_shards, wr, debug=False):
    nc = build_scores(debug)
    consts = _host_constants()
    maps = [{"x": s, "wr": np.ascontiguousarray(wr), "ident": consts["ident"]}
            for s in x_fm_shards]
    res = run_bass_kernel_spmd(nc, maps, core_ids=list(range(NCORES)))
    if debug:
        return res
    return np.concatenate([r["sc"] for r in res.results], axis=0)





def build_moe(debug=False):
    nc = bass.Bass("TRN2", target_bir_lowering=False, debug=False)
    x = nc.dram_tensor("x", [D, TL], F32, kind="ExternalInput")
    sbis_in = nc.dram_tensor("sbis", [128, NTOK // 16], F32, kind="ExternalInput")
    slocT_in = nc.dram_tensor("slocT", [E, TL], F32, kind="ExternalInput")
    w1_in = nc.dram_tensor("w1", [E, D, DI], F32, kind="ExternalInput")
    w2_in = nc.dram_tensor("w2", [E, DI, D], F32, kind="ExternalInput")
    b1f_in = nc.dram_tensor("b1f", [E, 128, DI // 128], F32, kind="ExternalInput")
    b2_in = nc.dram_tensor("b2", [E, D], F32, kind="ExternalInput")
    g16_in = nc.dram_tensor("g16", [128, 8], F32, kind="ExternalInput")
    lohi_in = nc.dram_tensor("lohi", [8, 2], F32, kind="ExternalInput")
    r16_in = nc.dram_tensor("r16", [8, 128], F32, kind="ExternalInput")
    xo = nc.dram_tensor("xo", [D, TL], F32, kind="ExternalOutput")
    if debug:
        dbg_t = nc.dram_tensor("dbg_t", [8, 1], F32, kind="ExternalOutput")
        dbg_m = nc.dram_tensor("dbg_m", [E, TL], F32, kind="ExternalOutput")
        dbg_it = nc.dram_tensor("dbg_it", [8, 12], F32, kind="ExternalOutput")
    mdram = nc.dram_tensor("mdram", [E, TL], F32)
    lndram = nc.dram_tensor("lndram", [D, TL], BF16)

    NHC = DI // 128
    with ExitStack() as ctx:
        lo8 = sbuf(ctx, nc, "lo8", [8, 1], F32)
        hi8 = sbuf(ctx, nc, "hi8", [8, 1], F32)
        mid8 = sbuf(ctx, nc, "mid8", [8, 1], F32)
        gef = sbuf(ctx, nc, "gef", [8, 1], F32)
        d8 = sbuf(ctx, nc, "d8", [8, 1], F32)
        d8b = sbuf(ctx, nc, "d8b", [8, 1], F32)
        t_col = sbuf(ctx, nc, "tcol", [8, 1], F32)
        itdbg = sbuf(ctx, nc, "itdbg", [8, 12], F32)
        ones_col = sbuf(ctx, nc, "onec", [128, 1], F32)
        ones_row = sbuf(ctx, nc, "oner", [1, 128], F32)
        eps1 = sbuf(ctx, nc, "eps1", [1, 1], F32)
        g16 = sbuf(ctx, nc, "g16s", [128, 8], F32)
        r16 = sbuf(ctx, nc, "r16s", [8, 128], F32)
        midbc = sbuf(ctx, nc, "midbc", [128, 1], F32)
        t_row = sbuf(ctx, nc, "trow", [1, E], F32)
        cnt = sbuf(ctx, nc, "cnt", [128, 1], F32)
        lnt2 = [sbuf(ctx, nc, f"lnt{i}", [128, DC * TT], BF16) for i in range(2)]
        mrow_cur = sbuf(ctx, nc, "mrowc", [1, TL], F32)
        mbc_t = sbuf(ctx, nc, "mbct", [128, TT], F32)
        gtmp = [sbuf(ctx, nc, f"gtmp{i}", [128, TT], F32) for i in range(2)]
        osb = [sbuf(ctx, nc, f"osb{i}", [128, TT], F32) for i in range(2)]
        b1sb = sbuf(ctx, nc, "b1sb", [128, DI // 128], F32)
        ps_h = [psum(ctx, nc, f"psh{i}", [128, TT]) for i in range(4)]
        ps_o = [psum(ctx, nc, f"pso{i}", [128, TT]) for i in range(2)]
        ps_g = psum(ctx, nc, "psg", [128, TT])
        ps_h2 = psum(ctx, nc, "psmisc2", [128, TT])
        ld = sem(ctx, nc, "ld")
        lnld = sem(ctx, nc, "lnld")
        wsem = sem(ctx, nc, "wsem")
        dvs = sem(ctx, nc, "dvs")
        mms = sem(ctx, nc, "mms")
        acs = sem(ctx, nc, "acs")
        gps = sem(ctx, nc, "gps")
        sto = sem(ctx, nc, "sto")

        with blk(nc, "m_p0") as b:
            @b.sync
            def _(s):
                ld.inc(s.dma_start(out=g16[:, :], in_=g16_in.ap()), 16)
                ld.inc(s.dma_start(out=r16[:, :], in_=r16_in.ap()), 16)
                s.wait_ge(ld.h, ld.n)

            @b.gpsimd
            def _(g):
                g.memset(ones_col[:, :], 1.0)
                g.memset(ones_row[:, :], 1.0)
                g.memset(eps1[:, :], LN_EPS)

            @b.sync
            def _(s2):
                ld.inc(s2.dma_start(out=lo8[:, :],
                                    in_=bass.AP(lohi_in, 0, [[2, 8], [1, 1]]),
                                    allow_slow_non_contiguous=True), 16)
                ld.inc(s2.dma_start(out=hi8[:, :],
                                    in_=bass.AP(lohi_in, 1, [[2, 8], [1, 1]]),
                                    allow_slow_non_contiguous=True), 16)
                s2.wait_ge(ld.h, ld.n)

        with ExitStack() as c1:
            sbis = sbuf(c1, nc, "sbiss", [128, NTOK // 16], F32)
            junk = sbuf(c1, nc, "junk", [128, NTOK // 16], BF16)
            with blk(nc, "m_p1l") as b:
                @b.sync
                def _(s):
                    ld.inc(s.dma_start(out=sbis[:, :], in_=sbis_in.ap()), 16)
                    s.wait_ge(ld.h, ld.n)

            for i in range(BIS_ITERS):
                with blk(nc, "bisa") as b:
                    @b.vector
                    def _(v):
                        v.tensor_scalar(mid8[:, :], lo8[:, :], hi8[:, 0:1], 0.5,
                                        ALU.add, ALU.mult)

                with blk(nc, "bisb") as b:
                    @b.tensor
                    def _(t):
                        t.matmul(ps_g[:, 0:1], r16[:, :], mid8[:, :],
                                 start=True, stop=True)

                with blk(nc, "bisc") as b:
                    @b.vector
                    def _(v):
                        v.tensor_copy(midbc[:, :], ps_g[:, 0:1])

                with blk(nc, "bisd") as b:
                    @b.vector
                    def _(v):
                        v.tensor_scalar(junk[:, :], sbis[:, :], midbc[:, 0:1],
                                        None, ALU.is_ge)
                        v.drain()
                        v.reduce_sum(cnt[:, 0:1], junk[:, :], AX.X)

                with blk(nc, "bise") as b:
                    @b.tensor
                    def _(t):
                        t.matmul(ps_h2[0:8, 0:1], g16[:, :], cnt[:, :],
                                 start=True, stop=True)

                with blk(nc, "bisf") as b:
                    @b.vector
                    def _(v):
                        v.tensor_scalar(gef[:, :], ps_h2[0:8, 0:1],
                                        float(K_SEL) - 0.5, None, ALU.is_ge)

                if debug and i < 4:
                    with blk(nc, "bisdbg") as b:
                        @b.vector
                        def _(v):
                            v.tensor_copy(itdbg[:, 3 * i:3 * i + 1], mid8[:, :])
                            v.tensor_copy(itdbg[:, 3 * i + 1:3 * i + 2], ps_h2[0:8, 0:1])
                            v.tensor_copy(itdbg[:, 3 * i + 2:3 * i + 3], gef[:, :])

                with blk(nc, "bisg") as b:
                    @b.vector
                    def _(v):
                        v.tensor_sub(d8[:, :], mid8[:, :], lo8[:, :])
                        v.tensor_sub(d8b[:, :], hi8[:, :], mid8[:, :])
                        v.drain()
                        v.scalar_tensor_tensor(lo8[:, :], d8[:, :], gef[:, 0:1],
                                               lo8[:, :], ALU.mult, ALU.add)
                        v.scalar_tensor_tensor(hi8[:, :], d8b[:, :], gef[:, 0:1],
                                               mid8[:, :], ALU.mult, ALU.add)

            with blk(nc, "m_p1f") as b:
                @b.vector
                def _(v):
                    dvs.inc(v.tensor_copy(t_col[:, :], lo8[:, :]))

        c2 = ctx.enter_context(ExitStack())
        srow = sbuf(c2, nc, "srow", [1, TL], F32)
        mrow_w = sbuf(c2, nc, "mroww", [1, TL], F32)
        tdram = nc.dram_tensor("tdram", [8, 1], F32)
        with blk(nc, "m_p2a") as b:
            @b.sync
            def _(s):
                sto.inc(s.dma_start(out=tdram.ap(), in_=t_col[:, :]), 16)
                s.wait_ge(sto.h, sto.n)
                sto.inc(s.dma_start(out=t_row[0:1, :],
                                    in_=bass.AP(tdram, 0, [[8, 1], [1, 8]])), 16)
                s.wait_ge(sto.h, sto.n)
                if debug:
                    sto.inc(s.dma_start(out=dbg_t.ap(), in_=t_col[:, :]), 16)
                    sto.inc(s.dma_start(out=dbg_it.ap(), in_=itdbg[:, :]), 16)
                    s.wait_ge(sto.h, sto.n)

        for ee in range(E):
            with blk(nc, f"m_p2r{ee}") as b:
                @b.sync
                def _(s):
                    sto.inc(s.dma_start(
                        out=srow[0:1, :],
                        in_=bass.AP(slocT_in, ee * TL, [[TL, 1], [1, TL]])), 16)
                    s.wait_ge(sto.h, sto.n)

            with blk(nc, f"m_p2m{ee}") as b:
                @b.vector
                def _(v):
                    v.tensor_scalar(mrow_w[0:1, :], srow[0:1, :],
                                    t_row[0:1, ee:ee + 1], None, ALU.is_ge)

            with blk(nc, f"m_p2n{ee}") as b:
                @b.vector
                def _(v):
                    v.scalar_tensor_tensor(mrow_w[0:1, :], mrow_w[0:1, :], 0.5,
                                           srow[0:1, :], ALU.mult, ALU.mult)

            with blk(nc, f"m_p2s{ee}") as b:
                @b.sync
                def _(s):
                    sto.inc(s.dma_start(
                        out=bass.AP(mdram, ee * TL, [[TL, 1], [1, TL]]),
                        in_=mrow_w[0:1, :]), 16)
                    s.wait_ge(sto.h, sto.n)
                    if debug:
                        sto.inc(s.dma_start(
                            out=bass.AP(dbg_m, ee * TL, [[TL, 1], [1, TL]]),
                            in_=mrow_w[0:1, :]), 16)
                        s.wait_ge(sto.h, sto.n)

        c2.close()

        with ExitStack() as c3:
            lbufs = {
                "xt": sbuf(c3, nc, "xt", [128, DC * TT], F32),
                "xsq": sbuf(c3, nc, "xsq", [128, DC * TT], F32),
                "ones_col": ones_col, "ones_row": ones_row, "eps1": eps1,
                "ps_sum": ps_g[0:1, 0:TT], "ps_sq": ps_h2[0:1, 0:TT],
                "ps_tr": ps_h[0][:, 0:8],
                "m_row": sbuf(c3, nc, "mrow", [1, TT], F32),
                "r_row": sbuf(c3, nc, "rrow", [1, TT], F32),
                "var_row": sbuf(c3, nc, "vrow", [1, TT], F32),
                "mcol": sbuf(c3, nc, "mcol", [128, 4], F32),
                "rcol": sbuf(c3, nc, "rcol", [128, 4], F32),
            }
            lnc_s = [sbuf(c3, nc, f"lnc{i}", [128, TT], BF16) for i in range(2)]
            tmpc = sbuf(c3, nc, "tmpc", [128, TT], F32)
            for t0 in range(NT):
                with blk(nc, f"m_p3l{t0}") as b:
                    @b.sync
                    def _(s):
                        s.wait_ge(sto.h, sto.n)
                        load_fm_tile(s, lbufs["xt"], x, t0 * TT, lnld)
                        s.wait_ge(lnld.h, lnld.n)
                        store_fm_tile(s, xo, lbufs["xt"], t0 * TT, sto)
                emit_ln_stats(nc, lbufs)
                with blk(nc, f"m_p3b{t0}") as b:
                    @b.tensor
                    def _(t):
                        mms.inc(t.matmul(ps_g[:, 0:TT], ones_row[:, :],
                                         lbufs["m_row"][0:1, :], start=True, stop=True))
                        mms.inc(t.matmul(ps_h2[:, 0:TT], ones_row[:, :],
                                         lbufs["r_row"][0:1, :], start=True, stop=True))

                with blk(nc, f"m_p3c{t0}") as b:
                    dv0, st0 = dvs.n, sto.n

                    @b.vector
                    def _(v):
                        for c in range(DC):
                            if c >= 2:
                                v.wait_ge(sto.h, st0 + 16 * (c - 1))
                            v.tensor_sub(tmpc[:, :], lbufs["xt"][:, c * TT:(c + 1) * TT],
                                         ps_g[:, 0:TT])
                            dvs.inc(v.tensor_mul(lnc_s[c % 2][:, :], tmpc[:, :],
                                                 ps_h2[:, 0:TT]))

                    @b.sync
                    def _(s):
                        for c in range(DC):
                            s.wait_ge(dvs.h, dv0 + c + 1)
                            ap_out = bass.AP(lndram, t0 * TT + c * 128 * TL,
                                             [[TL, 128], [1, TT]])
                            sto.inc(s.dma_start(out=ap_out, in_=lnc_s[c % 2][:, :]), 16)

        with blk(nc, "m_p3fin") as b:
            @b.sync
            def _(s):
                s.wait_ge(sto.h, sto.n)

        with ExitStack() as c5:
            w1sb = sbuf(c5, nc, "w1sb", [128, DC * DI], BF16)
            w2sb = sbuf(c5, nc, "w2sb", [128, NHC * D], BF16)
            h_sb = sbuf(c5, nc, "hsb", [128, NHC * TT], BF16)

            with blk(nc, "m_pre") as b:
                @b.sync
                def _(s):
                    ap_in = bass.AP(lndram, 0,
                                    [[TL, 128], [TL * 128, DC], [1, TT]])
                    ap_o = bass.AP(lnt2[0], 0, [[DC * TT, 128], [TT, DC], [1, TT]])
                    lnld.inc(s.dma_start(out=ap_o, in_=ap_in), 16)

            seq_i = [0]
            for e in range(E):
                with blk(nc, f"m_w{e}") as b:
                    @b.gpsimd
                    def _(g):
                        for kc in range(DC):
                            ap_in = bass.AP(w1_in, e * D * DI + kc * 128 * DI,
                                            [[DI, 128], [1, DI]])
                            wsem.inc(g.dma_start(
                                out=w1sb[:, kc * DI:(kc + 1) * DI], in_=ap_in), 16)
                        for kc in range(NHC):
                            ap_in = bass.AP(w2_in, e * DI * D + kc * 128 * D,
                                            [[D, 128], [1, D]])
                            wsem.inc(g.dma_start(
                                out=w2sb[:, kc * D:(kc + 1) * D], in_=ap_in), 16)
                        wsem.inc(g.dma_start(
                            out=b1sb[:, :],
                            in_=bass.AP(b1f_in, e * 128 * NHC, [[NHC, 128], [1, NHC]])), 16)
                        wsem.inc(g.dma_start(
                            out=mrow_cur[0:1, :],
                            in_=bass.AP(mdram, e * TL, [[TL, 1], [1, TL]])), 16)

                for t0 in range(NT):
                    with blk(nc, f"m_e{e}t{t0}") as b:
                        mm0, ac0, dv0, gp0 = mms.n, acs.n, dvs.n, gps.n
                        i_seq = seq_i[0]
                        seq_i[0] += 1
                        lnt = lnt2[i_seq % 2]
                        my_lnld = 16 * (i_seq + 1)
                        nt0 = (t0 + 1) % NT

                        @b.sync
                        def _(s):
                            if not (e == E - 1 and t0 == NT - 1):
                                ap_in = bass.AP(lndram, nt0 * TT,
                                                [[TL, 128], [TL * 128, DC], [1, TT]])
                                ap_o = bass.AP(lnt2[(i_seq + 1) % 2], 0,
                                               [[DC * TT, 128], [TT, DC], [1, TT]])
                                lnld.inc(s.dma_start(out=ap_o, in_=ap_in), 16)

                        @b.tensor
                        def _(t):
                            t.wait_ge(wsem.h, wsem.n)
                            t.wait_ge(lnld.h, my_lnld)
                            mms.inc(t.matmul(
                                ps_g[:, 0:TT], ones_row[:, :],
                                mrow_cur[0:1, t0 * TT:(t0 + 1) * TT],
                                start=True, stop=True))
                            for hc in range(NHC):
                                if hc >= 4:
                                    t.wait_ge(acs.h, ac0 + hc - 3)
                                for kc in range(DC):
                                    ins = t.matmul(
                                        ps_h[hc % 4][:, :],
                                        w1sb[:, kc * DI + hc * 128: kc * DI + hc * 128 + 128],
                                        lnt[:, kc * TT:(kc + 1) * TT],
                                        start=(kc == 0), stop=(kc == DC - 1))
                                mms.inc(ins)
                            t.wait_ge(dvs.h, dv0 + 1 + NHC)
                            for oc in range(DC):
                                if oc >= 2:
                                    t.wait_ge(acs.h, ac0 + NHC + oc - 1)
                                for hc in range(NHC):
                                    ins = t.matmul(
                                        ps_o[oc % 2][:, :],
                                        w2sb[:, hc * D + oc * 128: hc * D + oc * 128 + 128],
                                        h_sb[:, hc * TT:(hc + 1) * TT],
                                        start=(hc == 0), stop=(hc == NHC - 1))
                                mms.inc(ins)

                        @b.vector
                        def _(v):
                            v.wait_ge(mms.h, mm0 + 1)
                            dvs.inc(v.tensor_copy(mbc_t[:, :], ps_g[:, 0:TT]))
                            for hc in range(NHC):
                                v.wait_ge(acs.h, ac0 + hc + 1)
                                dvs.inc(v.tensor_mul(h_sb[:, hc * TT:(hc + 1) * TT],
                                                     gtmp[hc % 2][:, :], mbc_t[:, :]))

                        @b.scalar
                        def _(sc):
                            sc.wait_ge(wsem.h, wsem.n)
                            for hc in range(NHC):
                                sc.wait_ge(mms.h, mm0 + 1 + hc + 1)
                                if hc >= 2:
                                    sc.wait_ge(dvs.h, dv0 + 1 + hc - 1)
                                acs.inc(sc.activation(gtmp[hc % 2][:, :],
                                                      ps_h[hc % 4][:, :], AF.Gelu,
                                                      bias=b1sb[:, hc:hc + 1]))
                            for oc in range(DC):
                                sc.wait_ge(mms.h, mm0 + 1 + NHC + oc + 1)
                                if oc >= 2:
                                    sc.wait_ge(gps.h, gp0 + 16 * (oc - 1))
                                acs.inc(sc.activation(osb[oc % 2][:, :],
                                                      ps_o[oc % 2][:, :], AF.Copy))

                        @b.gpsimd
                        def _(g):
                            for oc in range(DC):
                                g.wait_ge(acs.h, ac0 + NHC + oc + 1)
                                ap_out = bass.AP(xo, oc * 128 * TL + t0 * TT,
                                                 [[TL, 128], [1, TT]])
                                gps.inc(g.dma_start(out=ap_out, in_=osb[oc % 2][:, :],
                                                    accum_op=ALU.add), 16)

        b2sb = sbuf(ctx, nc, "b2sb", [8, D], F32)
        mt8 = sbuf(ctx, nc, "mt8", [8, TT], F32)
        with blk(nc, "m_p6l") as b:
            @b.sync
            def _(s):
                ld.inc(s.dma_start(out=b2sb[:, :], in_=b2_in.ap()), 16)
                s.wait_ge(ld.h, ld.n)
        for t0 in range(NT):
            with blk(nc, f"m_p6_{t0}") as b:
                mm0, ac0, gp0 = mms.n, acs.n, gps.n

                @b.sync
                def _(s):
                    lnld.inc(s.dma_start(
                        out=mt8[:, :],
                        in_=bass.AP(mdram, t0 * TT, [[TL, E], [1, TT]])), 16)

                @b.tensor
                def _(t):
                    t.wait_ge(lnld.h, lnld.n)
                    for oc in range(DC):
                        if oc >= 2:
                            t.wait_ge(acs.h, ac0 + oc - 1)
                        mms.inc(t.matmul(ps_o[oc % 2][:, :],
                                         b2sb[:, oc * 128:(oc + 1) * 128],
                                         mt8[:, :], start=True, stop=True))

                @b.scalar
                def _(sc):
                    for oc in range(DC):
                        sc.wait_ge(mms.h, mm0 + oc + 1)
                        if oc >= 2:
                            sc.wait_ge(gps.h, gp0 + 16 * (oc - 1))
                        acs.inc(sc.activation(osb[oc % 2][:, :], ps_o[oc % 2][:, :],
                                              AF.Copy))

                @b.gpsimd
                def _(g):
                    for oc in range(DC):
                        g.wait_ge(acs.h, ac0 + oc + 1)
                        ap_out = bass.AP(xo, oc * 128 * TL + t0 * TT,
                                         [[TL, 128], [1, TT]])
                        gps.inc(g.dma_start(out=ap_out, in_=osb[oc % 2][:, :],
                                            accum_op=ALU.add), 16)

        with blk(nc, "m_fin") as b:
            @b.sync
            def _(s):
                s.wait_ge(gps.h, gps.n)
                s.wait_ge(sto.h, sto.n)

    return nc


def moe_host_inputs(scores_full, core, ins_w, x_fm):
    sT = np.ascontiguousarray(scores_full.T)
    sbis = np.ascontiguousarray(sT.reshape(128, NTOK // 16))
    slocT = np.ascontiguousarray(sT[:, core * TL:(core + 1) * TL])
    g16 = np.zeros((128, 8), np.float32)
    for r in range(128):
        g16[r, r // 16] = 1.0
    r16 = np.zeros((8, 128), np.float32)
    for r in range(128):
        r16[r // 16, r] = 1.0
    b1f = np.ascontiguousarray(
        ins_w["L0_ffn_b1"].reshape(E, DI // 128, 128).transpose(0, 2, 1))
    return {
        "x": x_fm, "sbis": sbis, "slocT": slocT,
        "w1": ins_w["L0_ffn_W1"], "w2": ins_w["L0_ffn_W2"],
        "b1f": b1f, "b2": ins_w["L0_ffn_b2"],
        "g16": g16, "r16": r16,
        "lohi": np.stack([np.zeros(8, np.float32), np.ones(8, np.float32)], 1),
    }


def emit_ln_stats2(nc, bufs, W=TT):
    xt = bufs["xt"]; xsqc = bufs["xsqc"]
    ps_sum = bufs["ps_sum"]; ps_sq = bufs["ps_sq"]; ps_tr = bufs["ps_tr"]
    m_row = bufs["m_row"]; r_row = bufs["r_row"]; var_row = bufs["var_row"]
    mcol = bufs["mcol"]; rcol = bufs["rcol"]
    nsub = W // 128
    mms2 = bufs["_mms"]; acs2 = bufs["_acs"]

    with blk(nc, "ln2a") as b:
        mm0, ac0 = mms2.n, acs2.n

        @b.scalar
        def _(sc):
            for c in range(DC):
                if c >= 2:
                    sc.wait_ge(mms2.h, mm0 + c - 1)
                acs2.inc(sc.activation(xsqc[c % 2][:, :], xt[:, c * W:(c + 1) * W],
                                       AF.Square))

        @b.tensor
        def _(t):
            for c in range(DC):
                t.matmul(ps_sum[0:1, :W], bufs["ones_col"][:, :],
                         xt[:, c * W:(c + 1) * W],
                         start=(c == 0), stop=(c == DC - 1))
            for c in range(DC):
                t.wait_ge(acs2.h, ac0 + c + 1)
                ins = t.matmul(ps_sq[0:1, :W], bufs["ones_col"][:, :],
                               xsqc[c % 2][:, :],
                               start=(c == 0), stop=(c == DC - 1))
                mms2.inc(ins)

    with blk(nc, "ln2b") as b:
        @b.vector
        def _(v):
            v.tensor_scalar_mul(m_row[0:1, :W], ps_sum[0:1, :W], 1.0 / D)
            v.tensor_scalar_mul(var_row[0:1, :W], ps_sq[0:1, :W], 1.0 / D)

    with blk(nc, "ln2c") as b:
        @b.scalar
        def _(sc):
            sc.activation(r_row[0:1, :W], m_row[0:1, :W], AF.Square)

    with blk(nc, "ln2d") as b:
        @b.vector
        def _(v):
            v.tensor_sub(var_row[0:1, :W], var_row[0:1, :W], r_row[0:1, :W])

    with blk(nc, "ln2e") as b:
        @b.scalar
        def _(sc):
            sc.activation(var_row[0:1, :W], var_row[0:1, :W], AF.Sqrt,
                          bias=bufs["eps1"][0:1, 0:1])

    with blk(nc, "ln2f") as b:
        @b.vector
        def _(v):
            v.reciprocal(r_row[0:1, :W], var_row[0:1, :W])

    with blk(nc, "ln2g") as b:
        @b.tensor
        def _(t):
            one11 = bufs["ones_row"][0:1, 0:1]
            for s in range(nsub):
                t.matmul(ps_tr[:, s:s + 1], m_row[0:1, s * 128:(s + 1) * 128],
                         one11, start=True, stop=True)
            for s in range(nsub):
                t.matmul(ps_tr[:, nsub + s:nsub + s + 1],
                         r_row[0:1, s * 128:(s + 1) * 128], one11,
                         start=True, stop=True)

    with blk(nc, "ln2h") as b:
        @b.vector
        def _(v):
            v.tensor_copy(mcol[:, 0:nsub], ps_tr[:, 0:nsub])
            v.tensor_copy(rcol[:, 0:nsub], ps_tr[:, nsub:2 * nsub])




def build_attn0():
    nc = bass.Bass("TRN2", target_bir_lowering=False, debug=False)
    x = nc.dram_tensor("x", [D, TL], F32, kind="ExternalInput")
    wq_in = nc.dram_tensor("wq", [D, D], F32, kind="ExternalInput")
    wk_in = nc.dram_tensor("wk", [D, D], F32, kind="ExternalInput")
    wv_in = nc.dram_tensor("wv", [D, D], F32, kind="ExternalInput")
    wo_in = nc.dram_tensor("wo", [D, D], F32, kind="ExternalInput")
    wr_in = nc.dram_tensor("wr", [D, E], F32, kind="ExternalInput")
    xo = nc.dram_tensor("xo", [D, TL], F32, kind="ExternalOutput")
    sc_out = nc.dram_tensor("sc", [TL, E], F32, kind="ExternalOutput")
    lnf = nc.dram_tensor("lnf", [D, TL], F32)

    NKC = SEQ // 128
    NQT = SEQ // TT

    with ExitStack() as ctx:
        ones_col = sbuf(ctx, nc, "onec", [128, 1], F32)
        ones_row = sbuf(ctx, nc, "oner", [1, 128], F32)
        eps1 = sbuf(ctx, nc, "eps1", [1, 1], F32)
        c20n = sbuf(ctx, nc, "c20n", [128, 1], F32)
        rzA = sbuf(ctx, nc, "rzA", [1, TT], F32)
        rzB = sbuf(ctx, nc, "rzB", [1, TT], F32)
        psSA = [psum(ctx, nc, f"psSA{i}", [128, TT]) for i in range(2)]
        psSB = [psum(ctx, nc, f"psSB{i}", [128, TT]) for i in range(2)]
        psZA = psum(ctx, nc, "psZA", [128, TT])
        psZB = psum(ctx, nc, "psZB", [128, TT])
        psRBC = psum(ctx, nc, "psRBC", [128, TT])
        psO = psum(ctx, nc, "psO", [128, TT])

        ld = sem(ctx, nc, "ld")
        wsem = sem(ctx, nc, "wsem")
        mms = sem(ctx, nc, "mms")
        acs = sem(ctx, nc, "acs")
        dvs = sem(ctx, nc, "dvs")
        sto = sem(ctx, nc, "sto")

        actx = ctx.enter_context(ExitStack())
        w0 = sbuf(actx, nc, "w0", [128, DC * D], BF16)
        w1 = sbuf(actx, nc, "w1b", [128, DC * D], BF16)
        w2 = sbuf(actx, nc, "w2b", [128, DC * D], BF16)
        q_sb = sbuf(actx, nc, "q_sb", [128, DC * SEQ], BF16)
        k_sb = sbuf(actx, nc, "k_sb", [128, DC * SEQ], BF16)
        v_sb = sbuf(actx, nc, "v_sb", [128, 16 * 1040], BF16)
        znorm = sbuf(actx, nc, "znorm", [128, DC * TT], BF16)
        zsb = sbuf(actx, nc, "zsb", [128, TT], F32)
        pA = [sbuf(actx, nc, f"pA{i}", [128, TT], BF16) for i in range(2)]
        pB = [sbuf(actx, nc, f"pB{i}", [128, TT], BF16) for i in range(2)]
        osb = [sbuf(actx, nc, f"osb{i}", [128, TT], F32) for i in range(2)]
        xc1 = sbuf(actx, nc, "xc1", [128, TT], F32)
        lc1 = sbuf(actx, nc, "lc1", [128, TT], F32)

        def load_w(q, dst, src, sem_):
            for kc in range(DC):
                ap_in = bass.AP(src, kc * 128 * D, [[D, 128], [1, D]])
                sem_.inc(q.dma_start(out=dst[:, kc * D:(kc + 1) * D], in_=ap_in), 16)

        with blk(nc, "a_p0") as b:
            @b.gpsimd
            def _(g):
                load_w(g, w0, wq_in, wsem)
                load_w(g, w1, wk_in, wsem)
                load_w(g, w2, wv_in, wsem)

            @b.vector
            def _(v):
                v.memset(ones_col[:, :], 1.0)
                v.memset(ones_row[:, :], 1.0)
                v.memset(eps1[:, :], LN_EPS)
                v.memset(c20n[:, :], -20.0)
                ap = bass.AP(v_sb, 64, [[16 * 1040, 128], [1040, 16], [65, 16], [1, 1]])
                v.memset(ap, 1.0)

        lbufs = {
            "xt": sbuf(actx, nc, "xt", [128, DC * TT], F32),
            "xsqc": [sbuf(actx, nc, f"xsqc{i}", [128, TT], F32) for i in range(2)],
            "_mms": mms, "_acs": acs,
            "ones_col": ones_col, "ones_row": ones_row, "eps1": eps1,
            "ps_sum": psZA[0:1, 0:TT], "ps_sq": psZB[0:1, 0:TT],
            "ps_tr": psRBC[:, 0:8],
            "m_row": sbuf(actx, nc, "mrow", [1, TT], F32),
            "r_row": sbuf(actx, nc, "rrow", [1, TT], F32),
            "var_row": sbuf(actx, nc, "vrow", [1, TT], F32),
            "mcol": sbuf(actx, nc, "mcol", [128, 4], F32),
            "rcol": sbuf(actx, nc, "rcol", [128, 4], F32),
        }
        lnt = sbuf(actx, nc, "lnta", [128, DC * TT], BF16)
        tmpc = lbufs["xsqc"][0]
        lnfc = [lbufs["xsqc"][1], sbuf(actx, nc, "lnfc1", [128, TT], F32)]

        for bb_ in range(BL):
            if bb_ > 0:
                with blk(nc, f"a_wq{bb_}") as b:
                    @b.gpsimd
                    def _(g):
                        load_w(g, w0, wq_in, wsem)
            for t0 in range(NQT):
                gt = bb_ * NQT + t0
                with blk(nc, f"a_l{gt}") as b:
                    @b.sync
                    def _(s):
                        s.wait_ge(sto.h, sto.n)
                        load_fm_tile(s, lbufs["xt"], x, gt * TT, ld)
                        s.wait_ge(ld.h, ld.n)
                emit_ln_stats2(nc, lbufs)
                with blk(nc, f"a_lb{gt}") as b:
                    @b.tensor
                    def _(t):
                        mms.inc(t.matmul(psZA[:, 0:TT], ones_row[:, :],
                                         lbufs["m_row"][0:1, :], start=True, stop=True))
                        mms.inc(t.matmul(psZB[:, 0:TT], ones_row[:, :],
                                         lbufs["r_row"][0:1, :], start=True, stop=True))

                with blk(nc, f"a_lc{gt}") as b:
                    dv0, st0 = dvs.n, sto.n

                    @b.vector
                    def _(v):
                        for c in range(DC):
                            if c >= 2:
                                v.wait_ge(sto.h, st0 + 16 * (c - 1))
                            v.tensor_sub(tmpc[:, :], lbufs["xt"][:, c * TT:(c + 1) * TT],
                                         psZA[:, 0:TT])
                            v.tensor_mul(lnfc[c % 2][:, :], tmpc[:, :], psZB[:, 0:TT])
                            v.drain()
                            dvs.inc(v.tensor_copy(lnt[:, c * TT:(c + 1) * TT],
                                                  lnfc[c % 2][:, :]))

                    @b.sync
                    def _(s):
                        for c in range(DC):
                            s.wait_ge(dvs.h, dv0 + c + 1)
                            ap_out = bass.AP(lnf, gt * TT + c * 128 * TL,
                                             [[TL, 128], [1, TT]])
                            sto.inc(s.dma_start(out=ap_out, in_=lnfc[c % 2][:, :]), 16)

                with blk(nc, f"a_pr{gt}") as b:
                    mm0, ac0 = mms.n, acs.n

                    @b.tensor
                    def _(t):
                        t.wait_ge(wsem.h, wsem.n)
                        n_ev = 0
                        for oc in range(DC):
                            for (wsb, dst_q) in ((w0, True), (w1, False)):
                                if n_ev >= 2:
                                    t.wait_ge(acs.h, ac0 + n_ev - 1)
                                bank = psSA[n_ev % 2]
                                for kc in range(DC):
                                    ins = t.matmul(
                                        bank[:, :],
                                        wsb[:, kc * D + oc * 128: kc * D + oc * 128 + 128],
                                        lnt[:, kc * TT:(kc + 1) * TT],
                                        start=(kc == 0), stop=(kc == DC - 1))
                                mms.inc(ins)
                                n_ev += 1
                        for sub in range(TT // 128):
                            s4 = t0 * 4 + sub
                            for g2 in range(2):
                                if n_ev >= 2:
                                    t.wait_ge(acs.h, ac0 + n_ev - 1)
                                bank = psSB[n_ev % 2]
                                for kc in range(DC):
                                    ins = t.matmul(
                                        bank[:, :],
                                        lnt[:, kc * TT + sub * 128: kc * TT + sub * 128 + 128],
                                        w2[:, kc * D + g2 * 512: kc * D + g2 * 512 + 512],
                                        start=(kc == 0), stop=(kc == DC - 1))
                                mms.inc(ins)
                                n_ev += 1

                    @b.scalar
                    def _(sc):
                        n_ev = 0
                        for oc in range(DC):
                            for dst_q in (True, False):
                                sc.wait_ge(mms.h, mm0 + n_ev + 1)
                                dst = q_sb if dst_q else k_sb
                                acs.inc(sc.activation(
                                    dst[:, oc * SEQ + t0 * TT: oc * SEQ + (t0 + 1) * TT],
                                    psSA[n_ev % 2][:, :], AF.Copy))
                                n_ev += 1
                        for sub in range(TT // 128):
                            s4 = t0 * 4 + sub
                            for g2 in range(2):
                                sc.wait_ge(mms.h, mm0 + n_ev + 1)
                                ap_out = bass.AP(
                                    v_sb, s4 * 1040 + g2 * 8 * 65,
                                    [[16 * 1040, 128], [65, 8], [1, 64]])
                                acs.inc(sc.activation(ap_out, psSB[n_ev % 2][:, :],
                                                      AF.Copy))
                                n_ev += 1

            with blk(nc, f"a_wo{bb_}") as b:
                @b.gpsimd
                def _(g):
                    load_w(g, w0, wo_in, wsem)

            for qt in range(NQT):
                gq = bb_ * NQT + qt
                for hp in range(DC):
                    with blk(nc, f"a_at{gq}_{hp}") as b:
                        mm0, ac0, dv0 = mms.n, acs.n, dvs.n

                        @b.tensor
                        def _(t):
                            t.wait_ge(wsem.h, wsem.n)
                            for kc in range(NKC):
                                if kc >= 2:
                                    t.wait_ge(acs.h, ac0 + kc - 1)
                                t.matmul(psSA[kc % 2][:, :],
                                         k_sb[0:64, hp * SEQ + kc * 128: hp * SEQ + kc * 128 + 128],
                                         q_sb[0:64, hp * SEQ + qt * TT: hp * SEQ + (qt + 1) * TT],
                                         start=True, stop=True)
                                mms.inc(t.matmul(
                                    psSB[kc % 2][:, :],
                                    k_sb[64:128, hp * SEQ + kc * 128: hp * SEQ + kc * 128 + 128],
                                    q_sb[64:128, hp * SEQ + qt * TT: hp * SEQ + (qt + 1) * TT],
                                    start=True, stop=True))
                                if kc >= 1:
                                    t.wait_ge(acs.h, ac0 + kc)
                                    t.matmul(psZA[0:65, :],
                                             v_sb[:, (kc - 1) * 1040 + (2 * hp) * 65: (kc - 1) * 1040 + (2 * hp) * 65 + 65],
                                             pA[(kc - 1) % 2][:, :],
                                             start=(kc == 1), stop=(kc == NKC))
                                    mms.inc(t.matmul(
                                        psZB[0:65, :],
                                        v_sb[:, (kc - 1) * 1040 + (2 * hp + 1) * 65: (kc - 1) * 1040 + (2 * hp + 1) * 65 + 65],
                                        pB[(kc - 1) % 2][:, :],
                                        start=(kc == 1), stop=(kc == NKC)))
                            kc = NKC
                            t.wait_ge(acs.h, ac0 + kc)
                            t.matmul(psZA[0:65, :],
                                     v_sb[:, (kc - 1) * 1040 + (2 * hp) * 65: (kc - 1) * 1040 + (2 * hp) * 65 + 65],
                                     pA[(kc - 1) % 2][:, :], start=False, stop=True)
                            mms.inc(t.matmul(
                                psZB[0:65, :],
                                v_sb[:, (kc - 1) * 1040 + (2 * hp + 1) * 65: (kc - 1) * 1040 + (2 * hp + 1) * 65 + 65],
                                pB[(kc - 1) % 2][:, :], start=False, stop=True))
                            t.wait_ge(dvs.h, dv0 + 1)
                            t.matmul(psRBC[0:64, :], ones_row[0:1, 0:64], rzA[0:1, :],
                                     start=True, stop=True)
                            mms.inc(t.matmul(psRBC[64:128, :], ones_row[0:1, 0:64],
                                             rzB[0:1, :], start=True, stop=True,
                                             tile_position=(0, 64)))

                        @b.scalar
                        def _(sc):
                            for kc in range(NKC):
                                sc.wait_ge(mms.h, mm0 + (2 * kc if kc else 1))
                                sc.activation(pA[kc % 2][:, :], psSA[kc % 2][:, :],
                                              AF.Exp, bias=c20n[:, 0:1], scale=0.125)
                                acs.inc(sc.activation(pB[kc % 2][:, :], psSB[kc % 2][:, :],
                                                      AF.Exp, bias=c20n[:, 0:1],
                                                      scale=0.125))
                            sc.wait_ge(mms.h, mm0 + 2 * NKC)
                            sc.activation(zsb[0:64, :], psZA[0:64, :], AF.Copy)
                            acs.inc(sc.activation(zsb[64:128, :], psZB[0:64, :], AF.Copy))

                        @b.vector
                        def _(v):
                            v.wait_ge(mms.h, mm0 + 2 * NKC)
                            v.reciprocal(rzA[0:1, :], psZA[64:65, :])
                            v.reciprocal(rzB[0:1, :], psZB[64:65, :])
                            v.drain()
                            dvs.inc(v.engine_nop())
                            v.wait_ge(mms.h, mm0 + 2 * NKC + 1)
                            v.wait_ge(acs.h, ac0 + NKC + 1)
                            v.tensor_mul(znorm[:, hp * TT:(hp + 1) * TT],
                                         zsb[:, :], psRBC[:, :])

                with blk(nc, f"a_op{gq}") as b:
                    mm0, ac0, dv0, st0 = mms.n, acs.n, dvs.n, sto.n

                    @b.sync
                    def _(s):
                        for oc in range(DC):
                            ap_x = bass.AP(x, oc * 128 * TL + gq * TT, [[TL, 128], [1, TT]])
                            ld.inc(s.dma_start(out=xc1[:, :], in_=ap_x), 16)
                            ap_l = bass.AP(lnf, oc * 128 * TL + gq * TT, [[TL, 128], [1, TT]])
                            ld.inc(s.dma_start(out=lc1[:, :], in_=ap_l), 16)
                            s.wait_ge(dvs.h, dv0 + oc + 1)
                            ap_o = bass.AP(xo, oc * 128 * TL + gq * TT, [[TL, 128], [1, TT]])
                            sto.inc(s.dma_start(out=ap_o, in_=osb[oc % 2][:, :]), 16)
                        s.wait_ge(sto.h, sto.n)

                    @b.tensor
                    def _(t):
                        for oc in range(DC):
                            if oc >= 1:
                                t.wait_ge(acs.h, ac0 + oc)
                            for hp in range(DC):
                                ins = t.matmul(
                                    psO[:, :],
                                    w0[:, hp * D + oc * 128: hp * D + oc * 128 + 128],
                                    znorm[:, hp * TT:(hp + 1) * TT],
                                    start=(hp == 0), stop=(hp == DC - 1))
                            mms.inc(ins)

                    @b.scalar
                    def _(sc):
                        for oc in range(DC):
                            sc.wait_ge(mms.h, mm0 + oc + 1)
                            if oc >= 2:
                                sc.wait_ge(dvs.h, dv0 + oc - 1)
                                sc.wait_ge(sto.h, st0 + 16 * (oc - 1))
                            acs.inc(sc.activation(osb[oc % 2][:, :], psO[:, :], AF.Copy))

                    @b.vector
                    def _(v):
                        for oc in range(DC):
                            v.wait_ge(acs.h, ac0 + oc + 1)
                            v.wait_ge(ld.h, st0 * 0 + mm0 * 0 + ld.n - 16 * (2 * (DC - 1 - oc)))
                            v.tensor_add(osb[oc % 2][:, :], osb[oc % 2][:, :],
                                         xc1[:, :])
                            v.drain()
                            dvs.inc(v.tensor_add(osb[oc % 2][:, :], osb[oc % 2][:, :],
                                                 lc1[:, :]))

        actx.close()
        sbufs = alloc_scores_bufs(ctx, nc, "s3", ps={
            "ps_sum": psSA[0][0:1, 0:TT], "ps_sq": psSA[1][0:1, 0:TT],
            "ps_tr": psSB[0][:, 0:8], "ps_raw": psSB[1][:, 0:32],
            "ps_bc": psZA[:, 0:8]})
        with blk(nc, "a_id") as b:
            @b.vector
            def _(v):
                v.memset(sbufs["ones_col"][:, :], 1.0)
                v.memset(sbufs["ones_row"][:, :], 1.0)
                v.memset(sbufs["eps1"][:, :], LN_EPS)
        emit_scores_prep(nc, sbufs, wr_in, ld)
        st2 = sem(ctx, nc, "st2")
        for t0 in range(NT):
            with blk(nc, f"a_sl{t0}") as b:
                @b.sync
                def _(s):
                    s.wait_ge(st2.h, st2.n)
                    load_fm_tile(s, sbufs["xt"], xo, t0 * TT, ld)
                    s.wait_ge(ld.h, ld.n)
            emit_ln_stats(nc, sbufs)
            emit_scores_tile(nc, sbufs)
            with blk(nc, f"a_ss{t0}") as b:
                @b.sync
                def _(s):
                    for sub in range(TT // 128):
                        row0 = t0 * TT + sub * 128
                        st2.inc(s.dma_start(
                            out=bass.AP(sc_out, row0 * E, [[E, 128], [1, E]]),
                            in_=sbufs["sc_t"][:, sub * 8:(sub + 1) * 8]), 16)
                    s.wait_ge(st2.h, st2.n)

        with blk(nc, "a_fin") as b:
            @b.sync
            def _(s):
                s.wait_ge(sto.h, sto.n)
                s.wait_ge(st2.h, st2.n)

    return nc



def build_l1ffn(nmat=9, skip=()):
    nc = bass.Bass("TRN2", target_bir_lowering=False, debug=False)
    x = nc.dram_tensor("x", [D, TL], F32, kind="ExternalInput")
    gwT_in = nc.dram_tensor("gwT", [D, E], F32, kind="ExternalInput")
    sg_in = nc.dram_tensor("sg", [D, D], F32, kind="ExternalInput")
    su_in = nc.dram_tensor("su", [D, D], F32, kind="ExternalInput")
    sd_in = nc.dram_tensor("sd", [D, D], F32, kind="ExternalInput")
    eg_in = nc.dram_tensor("eg", [E, D, D], F32, kind="ExternalInput")
    eu_in = nc.dram_tensor("eu", [E, D, D], F32, kind="ExternalInput")
    ed_in = nc.dram_tensor("ed", [E, D, D], F32, kind="ExternalInput")
    ident_in = nc.dram_tensor("ident", [128, 128], F32, kind="ExternalInput")
    xo = nc.dram_tensor("xo", [D, TL], F32, kind="ExternalOutput")
    lndram = nc.dram_tensor("lndram", [D, TL], BF16)
    cwdram = nc.dram_tensor("cwdram", [E, TL], F32)

    with ExitStack() as ctx:
        ld = sem(ctx, nc, "ld")
        lnld = sem(ctx, nc, "lnld")
        wsem = sem(ctx, nc, "wsem")
        dvs = sem(ctx, nc, "dvs")
        mms = sem(ctx, nc, "mms")
        acs = sem(ctx, nc, "acs")
        gps = sem(ctx, nc, "gps")
        sto = sem(ctx, nc, "sto")

        ps = [psum(ctx, nc, f"f_ps{i}", [128, TT]) for i in range(8)]
        ps_g = [ps[0], ps[1]]
        ps_u = [ps[2], ps[3]]
        ps_o = [ps[4], ps[5]]
        ps_m1 = ps[6]
        ps_m2 = ps[7]

        sbufs = alloc_scores_bufs(ctx, nc, "f", ps={
            "ps_sum": ps_m1[0:1, 0:TT], "ps_sq": ps_m2[0:1, 0:TT],
            "ps_tr": ps[0][:, 0:8], "ps_raw": ps[1][:, 0:32],
            "ps_bc": ps[2][:, 0:8]})
        ident = sbuf(ctx, nc, "identb", [128, 128], F32)
        lnc_s = [sbuf(ctx, nc, f"lnc{i}", [128, TT], BF16) for i in range(2)]
        cwsb = sbuf(ctx, nc, "cwsb", [8, TL], F32)
        cw_t = sbuf(ctx, nc, "cw_t", [128, 32], F32)
        m2c = sbuf(ctx, nc, "m2c", [128, 4], F32)
        eqm = sbuf(ctx, nc, "eqm", [128, 32], F32)
        l2 = sbuf(ctx, nc, "l2", [128, 32], F32)
        pind = sbuf(ctx, nc, "pind", [128, 32], F32)
        s2c = sbuf(ctx, nc, "s2c", [128, 4], F32)
        denc = sbuf(ctx, nc, "denc", [128, 4], F32)
        mxp = sbuf(ctx, nc, "mxp", [128, 4], F32)

        with blk(nc, "f_p0") as b:
            @b.sync
            def _(s):
                ld.inc(s.dma_start(out=ident[:, :], in_=ident_in.ap()), 16)
                s.wait_ge(ld.h, ld.n)

            @b.vector
            def _(v):
                v.memset(sbufs["ones_col"][:, :], 1.0)
                v.memset(sbufs["ones_row"][:, :], 1.0)
                v.memset(sbufs["eps1"][:, :], LN_EPS)

        emit_scores_prep(nc, sbufs, gwT_in, ld)

        for t0 in range(NT):
            with blk(nc, f"f_l{t0}") as b:
                @b.sync
                def _(s):
                    s.wait_ge(sto.h, sto.n)
                    load_fm_tile(s, sbufs["xt"], x, t0 * TT, lnld)
                    s.wait_ge(lnld.h, lnld.n)
                    store_fm_tile(s, xo, sbufs["xt"], t0 * TT, sto)
            emit_ln_stats(nc, sbufs)
            emit_scores_tile(nc, sbufs)
            with blk(nc, f"f_cw{t0}") as b:
                @b.vector
                def _(v):
                    pre = sbufs["pre"]; exps = sbufs["exps"]
                    v.tensor_reduce(mxp[:, 0:4].unsqueeze(2),
                                    pre[:, 0:32].rearrange("p (s e) -> p s e", e=8),
                                    AX.X, ALU.max)
                    v.drain()
                    for s in range(4):
                        v.tensor_scalar(eqm[:, s * 8:(s + 1) * 8],
                                        pre[:, s * 8:(s + 1) * 8],
                                        mxp[:, s:s + 1], None, ALU.is_ge)
                    v.drain()
                    for s in range(4):
                        v.scalar_tensor_tensor(l2[:, s * 8:(s + 1) * 8],
                                               eqm[:, s * 8:(s + 1) * 8], -1e30,
                                               pre[:, s * 8:(s + 1) * 8],
                                               ALU.mult, ALU.add)
                    v.drain()
                    v.tensor_reduce(m2c[:, 0:4].unsqueeze(2),
                                    l2[:, 0:32].rearrange("p (s e) -> p s e", e=8),
                                    AX.X, ALU.max)
                    v.drain()
                    for s in range(4):
                        v.tensor_scalar(eqm[:, s * 8:(s + 1) * 8],
                                        pre[:, s * 8:(s + 1) * 8],
                                        m2c[:, s:s + 1], None, ALU.is_ge)
                    v.drain()
                    v.tensor_mul(pind[:, 0:32], exps[:, 0:32], eqm[:, 0:32])
                    v.drain()
                    v.tensor_reduce(s2c[:, 0:4].unsqueeze(2),
                                    pind[:, 0:32].rearrange("p (s e) -> p s e", e=8),
                                    AX.X, ALU.add)
                    v.drain()
                    v.scalar_tensor_tensor(denc[:, 0:4], sbufs["zrow"][:, 0:4], 1e-8,
                                           s2c[:, 0:4], ALU.mult, ALU.add)
                    v.drain()
                    v.reciprocal(denc[:, 0:4], denc[:, 0:4])
                    v.drain()
                    for s in range(4):
                        dvs.inc(v.tensor_scalar(cw_t[:, s * 8:(s + 1) * 8],
                                                pind[:, s * 8:(s + 1) * 8],
                                                denc[:, s:s + 1], 0.5,
                                                ALU.mult, ALU.mult))

            with blk(nc, f"f_cwt{t0}") as b:
                @b.tensor
                def _(t):
                    for s in range(4):
                        t.transpose(ps_m1[0:8, s * 128:(s + 1) * 128],
                                    cw_t[:, s * 8:(s + 1) * 8], ident[:, :])

            with blk(nc, f"f_cwe{t0}") as b:
                @b.vector
                def _(v):
                    v.tensor_copy(cwsb[0:8, t0 * TT:(t0 + 1) * TT],
                                  ps_m1[0:8, 0:TT])

            with blk(nc, f"f_lb{t0}") as b:
                @b.tensor
                def _(t):
                    t.matmul(ps_m1[:, 0:TT], sbufs["ones_row"][:, :],
                             sbufs["m_row"][0:1, :], start=True, stop=True)
                    t.matmul(ps_m2[:, 0:TT], sbufs["ones_row"][:, :],
                             sbufs["r_row"][0:1, :], start=True, stop=True)

            with blk(nc, f"f_lc{t0}") as b:
                dv0, st0 = dvs.n, sto.n

                @b.vector
                def _(v):
                    for c in range(DC):
                        if c >= 2:
                            v.wait_ge(sto.h, st0 + 16 * (c - 1))
                        v.tensor_sub(sbufs["xsq"][:, 0:TT],
                                     sbufs["xt"][:, c * TT:(c + 1) * TT],
                                     ps_m1[:, 0:TT])
                        dvs.inc(v.tensor_mul(lnc_s[c % 2][:, :],
                                             sbufs["xsq"][:, 0:TT], ps_m2[:, 0:TT]))

                @b.sync
                def _(s):
                    for c in range(DC):
                        s.wait_ge(dvs.h, dv0 + c + 1)
                        ap_out = bass.AP(lndram, t0 * TT + c * 128 * TL,
                                         [[TL, 128], [1, TT]])
                        sto.inc(s.dma_start(out=ap_out, in_=lnc_s[c % 2][:, :]), 16)

        with blk(nc, "f_cwd") as b:
            @b.sync
            def _(s):
                sto.inc(s.dma_start(out=cwdram.ap(), in_=cwsb[:, :]), 16)
                s.wait_ge(sto.h, sto.n)

        with ExitStack() as c5:
            gg = sbuf(c5, nc, "gg", [128, DC * D], BF16)
            gu = sbuf(c5, nc, "gu", [128, DC * D], BF16)
            dt_ = sbuf(c5, nc, "dt", [128, DC * D], BF16)
            lnt = sbuf(c5, nc, "lnt5", [128, DC * TT], BF16)
            t_sb = sbuf(c5, nc, "tsb", [128, DC * TT], BF16)
            sg_s = [sbuf(c5, nc, f"sgs{i}", [128, TT], F32) for i in range(2)]
            u_s = [sbuf(c5, nc, f"us{i}", [128, TT], F32) for i in range(2)]
            t1f = sbuf(c5, nc, "t1f", [128, TT], F32)
            wbc = sbuf(c5, nc, "wbc", [128, TT], F32)
            osb = [sbuf(c5, nc, f"osb5{i}", [128, TT], F32) for i in range(2)]
            wrow = sbuf(c5, nc, "wrow", [1, TL], F32)

            for mat in range(nmat):
                if "fm_only" in skip:
                    pass
                with blk(nc, f"f_w{mat}") as b:
                    @b.gpsimd
                    def _(g):
                        for kc in range(DC):
                            if mat == 0:
                                srcs = [(sg_in, 0), (su_in, 0), (sd_in, 0)]
                            else:
                                ee = mat - 1
                                srcs = [(eg_in, ee * D * D), (eu_in, ee * D * D),
                                        (ed_in, ee * D * D)]
                            for (srct, base), dst in zip(srcs, (gg, gu, dt_)):
                                ap_in = bass.AP(srct, base + kc * 128 * D,
                                                [[D, 128], [1, D]])
                                wsem.inc(g.dma_start(
                                    out=dst[:, kc * D:(kc + 1) * D], in_=ap_in), 16)
                        if mat > 0:
                            wsem.inc(g.dma_start(
                                out=wrow[0:1, :],
                                in_=bass.AP(cwdram, (mat - 1) * TL,
                                            [[TL, 1], [1, TL]])), 16)

                    if mat == 0:
                        @b.vector
                        def _(v):
                            v.memset(wbc[:, :], 0.5)

                for t0 in range(NT if "fm" not in skip else 0):
                    with blk(nc, f"f_m{mat}t{t0}") as b:
                        mm0, ac0, dv0, gp0 = mms.n, acs.n, dvs.n, gps.n
                        off = 1 if mat > 0 else 0

                        @b.sync
                        def _(s):
                            ap_in = bass.AP(lndram, t0 * TT,
                                            [[TL, 128], [TL * 128, DC], [1, TT]])
                            ap_o = bass.AP(lnt, 0, [[DC * TT, 128], [TT, DC], [1, TT]])
                            lnld.inc(s.dma_start(out=ap_o, in_=ap_in), 16)

                        @b.tensor
                        def _(t):
                            t.wait_ge(wsem.h, wsem.n)
                            t.wait_ge(lnld.h, lnld.n)
                            if mat > 0:
                                mms.inc(t.matmul(ps_m1[:, 0:TT], sbufs["ones_row"][:, :],
                                                 wrow[0:1, t0 * TT:(t0 + 1) * TT],
                                                 start=True, stop=True))
                            for fc in range(DC):
                                if fc >= 2:
                                    t.wait_ge(acs.h, ac0 + 2 * (fc - 2) + 2)
                                for kc in range(DC):
                                    i1 = t.matmul(
                                        ps_g[fc % 2][:, :],
                                        gg[:, kc * D + fc * 128: kc * D + fc * 128 + 128],
                                        lnt[:, kc * TT:(kc + 1) * TT],
                                        start=(kc == 0), stop=(kc == DC - 1))
                                mms.inc(i1)
                                for kc in range(DC):
                                    i2 = t.matmul(
                                        ps_u[fc % 2][:, :],
                                        gu[:, kc * D + fc * 128: kc * D + fc * 128 + 128],
                                        lnt[:, kc * TT:(kc + 1) * TT],
                                        start=(kc == 0), stop=(kc == DC - 1))
                                mms.inc(i2)
                            if "down" not in skip:
                                t.wait_ge(dvs.h, dv0 + off + 2 * DC)
                                for oc in range(DC):
                                    if oc >= 2:
                                        t.wait_ge(acs.h, ac0 + 2 * DC + oc - 1)
                                    for ic in range(DC):
                                        i3 = t.matmul(
                                            ps_o[oc % 2][:, :],
                                            dt_[:, ic * D + oc * 128: ic * D + oc * 128 + 128],
                                            t_sb[:, ic * TT:(ic + 1) * TT],
                                            start=(ic == 0), stop=(ic == DC - 1))
                                    mms.inc(i3)

                        @b.scalar
                        def _(sc):
                            for fc in range(DC):
                                sc.wait_ge(mms.h, mm0 + off + 2 * fc + 1)
                                if fc >= 2:
                                    sc.wait_ge(dvs.h, dv0 + off + 2 * (fc - 2) + 2)
                                acs.inc(sc.activation(sg_s[fc % 2][:, :],
                                                      ps_g[fc % 2][:, :],
                                                      AF.Copy if "silu" in skip else AF.Silu))
                                sc.wait_ge(mms.h, mm0 + off + 2 * fc + 2)
                                acs.inc(sc.activation(u_s[fc % 2][:, :],
                                                      ps_u[fc % 2][:, :], AF.Copy))
                            for oc in range(DC if "down" not in skip else 0):
                                sc.wait_ge(mms.h, mm0 + off + 2 * DC + oc + 1)
                                if oc >= 2:
                                    sc.wait_ge(gps.h, gp0 + 16 * (oc - 1))
                                acs.inc(sc.activation(osb[oc % 2][:, :],
                                                      ps_o[oc % 2][:, :], AF.Copy))

                        @b.vector
                        def _(v):
                            if mat > 0:
                                v.wait_ge(mms.h, mm0 + 1)
                                dvs.inc(v.tensor_copy(wbc[:, :], ps_m1[:, 0:TT]))
                            for fc in range(DC):
                                v.wait_ge(acs.h, ac0 + 2 * fc + 2)
                                dvs.inc(v.tensor_mul(t1f[:, :], sg_s[fc % 2][:, :],
                                                     u_s[fc % 2][:, :]))
                                dvs.inc(v.tensor_mul(t_sb[:, fc * TT:(fc + 1) * TT],
                                                     t1f[:, :], wbc[:, :]))

                        @b.gpsimd
                        def _(g):
                            for oc in range(DC if "down" not in skip else 0):
                                g.wait_ge(acs.h, ac0 + 2 * DC + oc + 1)
                                ap_out = bass.AP(xo, oc * 128 * TL + t0 * TT,
                                                 [[TL, 128], [1, TT]])
                                kw = {} if "add" in skip else {"accum_op": ALU.add}
                                gps.inc(g.dma_start(out=ap_out, in_=osb[oc % 2][:, :],
                                                    **kw), 16)

        with blk(nc, "f_fin") as b:
            @b.sync
            def _(s):
                s.wait_ge(gps.h, gps.n)
                s.wait_ge(sto.h, sto.n)

    return nc


def l1ffn_host_inputs(ins_w, x_fm):
    def padT(a, rows=None):
        aT = np.ascontiguousarray(a.T.astype(np.float32))
        out = np.zeros((D, D), np.float32)
        out[:aT.shape[0], :aT.shape[1]] = aT
        return out

    sg = ins_w["L1_ffn_shared_gate_up"][:I1]
    su = ins_w["L1_ffn_shared_gate_up"][I1:2 * I1]
    sd = ins_w["L1_ffn_shared_down"]
    eg = np.stack([padT(ins_w["L1_ffn_experts_gate_up"][e][:I1]) for e in range(E)])
    eu = np.stack([padT(ins_w["L1_ffn_experts_gate_up"][e][I1:2 * I1]) for e in range(E)])
    ed = np.stack([padT(ins_w["L1_ffn_experts_down"][e]) for e in range(E)])
    return {
        "x": x_fm,
        "gwT": np.ascontiguousarray(ins_w["L1_ffn_gate_weight"].T.astype(np.float32)),
        "sg": padT(sg), "su": padT(su), "sd": padT(sd),
        "eg": eg, "eu": eu, "ed": ed,
        "ident": np.eye(128, dtype=np.float32),
    }



def build_attn1():
    nc = bass.Bass("TRN2", target_bir_lowering=False, debug=False)
    x = nc.dram_tensor("x", [D, TL], F32, kind="ExternalInput")
    wqkv_in = nc.dram_tensor("wqkv", [D, 3 * D], F32, kind="ExternalInput")
    wo_in = nc.dram_tensor("wo", [D, D], F32, kind="ExternalInput")
    ctab_in = nc.dram_tensor("ctab", [128, SEQ], F32, kind="ExternalInput")
    stab_in = nc.dram_tensor("stab", [128, SEQ], F32, kind="ExternalInput")
    p32_in = nc.dram_tensor("p32", [128, 128], F32, kind="ExternalInput")
    tri_in = nc.dram_tensor("tri", [128, 896], F32, kind="ExternalInput")
    xo = nc.dram_tensor("xo", [D, TL], F32, kind="ExternalOutput")

    NKC = SEQ // 128
    NQT = SEQ // TT

    with ExitStack() as ctx:
        ones_col = sbuf(ctx, nc, "onec", [128, 1], F32)
        ones_row = sbuf(ctx, nc, "oner", [1, 128], F32)
        eps1 = sbuf(ctx, nc, "eps1", [1, 1], F32)
        c20n = sbuf(ctx, nc, "c20n", [128, 1], F32)
        rzB = sbuf(ctx, nc, "rzB", [1, TT], F32)
        psSA = [psum(ctx, nc, f"psSA{i}", [128, TT]) for i in range(2)]
        psSB = [psum(ctx, nc, f"psSB{i}", [128, TT]) for i in range(2)]
        psZA = psum(ctx, nc, "psZA", [128, TT])
        psZB = psum(ctx, nc, "psZB", [128, TT])
        psRBC = psum(ctx, nc, "psRBC", [128, TT])
        psO = psum(ctx, nc, "psO", [128, TT])

        ld = sem(ctx, nc, "ld")
        wsem = sem(ctx, nc, "wsem")
        mms = sem(ctx, nc, "mms")
        acs = sem(ctx, nc, "acs")
        dvs = sem(ctx, nc, "dvs")
        sto = sem(ctx, nc, "sto")

        wq = sbuf(ctx, nc, "wq1", [128, DC * D], BF16)
        wk = sbuf(ctx, nc, "wk1", [128, DC * D], BF16)
        wv = sbuf(ctx, nc, "wv1", [128, DC * D], BF16)
        wo = sbuf(ctx, nc, "wo1", [128, DC * D], BF16)
        q_sb = sbuf(ctx, nc, "q_sb1", [128, DC * SEQ], BF16)
        k_sb = sbuf(ctx, nc, "k_sb1", [128, DC * SEQ], BF16)
        v_sb = sbuf(ctx, nc, "v_sb1", [128, 16 * 1040], BF16)
        ctab = sbuf(ctx, nc, "ctab_s", [128, SEQ], BF16)
        stab = sbuf(ctx, nc, "stab_s", [128, SEQ], BF16)
        p32 = sbuf(ctx, nc, "p32s", [128, 128], BF16)
        tri = sbuf(ctx, nc, "tris", [128, 896], BF16)

        xt1h = sbuf(ctx, nc, "xt1", [128, DC * TT], F32)
        znorm = xt1h.bitcast(BF16)
        lbufs = {
            "xt": xt1h,
            "xsqc": [sbuf(ctx, nc, f"xs1{i}", [128, TT], F32) for i in range(2)],
            "_mms": mms, "_acs": acs,
            "ones_col": ones_col, "ones_row": ones_row, "eps1": eps1,
            "ps_sum": psZA[0:1, 0:TT], "ps_sq": psZB[0:1, 0:TT],
            "ps_tr": psRBC[:, 0:8],
            "m_row": sbuf(ctx, nc, "mrow1", [1, TT], F32),
            "r_row": sbuf(ctx, nc, "rrow1", [1, TT], F32),
            "var_row": sbuf(ctx, nc, "vrow1", [1, TT], F32),
            "mcol": sbuf(ctx, nc, "mcol1", [128, 4], F32),
            "rcol": sbuf(ctx, nc, "rcol1", [128, 4], F32),
        }
        lnt = sbuf(ctx, nc, "lnt1", [128, DC * TT], BF16)
        pf32 = lnt.bitcast(F32)
        pA = [lnt[:, 0:TT], lnt[:, 0:TT]]
        pB = [lnt[:, TT:2 * TT], lnt[:, TT:2 * TT]]
        zsb = pf32[:, TT:2 * TT]
        osb = [pf32[:, 2 * TT:3 * TT], pf32[:, 2 * TT:3 * TT]]
        xc1 = zsb
        rzA = pf32[0:1, 3 * TT:3 * TT + TT]
        rtmp = lbufs["xsqc"][0]
        rtmp2 = lbufs["xsqc"][1]

        def load_wslice(g, dst, col0):
            for kc in range(DC):
                ap_in = bass.AP(wqkv_in, kc * 128 * 3 * D + col0,
                                [[3 * D, 128], [1, D]])
                wsem.inc(g.dma_start(out=dst[:, kc * D:(kc + 1) * D], in_=ap_in), 16)

        with blk(nc, "b_p0") as b:
            @b.gpsimd
            def _(g):
                load_wslice(g, wq, 0)
                load_wslice(g, wk, D)
                load_wslice(g, wv, 2 * D)
                for kc in range(DC):
                    ap_in = bass.AP(wo_in, kc * 128 * D, [[D, 128], [1, D]])
                    wsem.inc(g.dma_start(out=wo[:, kc * D:(kc + 1) * D], in_=ap_in), 16)
                wsem.inc(g.dma_start(out=ctab[:, :], in_=ctab_in.ap()), 16)
                wsem.inc(g.dma_start(out=stab[:, :], in_=stab_in.ap()), 16)
                wsem.inc(g.dma_start(out=p32[:, :], in_=p32_in.ap()), 16)
                wsem.inc(g.dma_start(out=tri[:, :], in_=tri_in.ap()), 16)

            @b.vector
            def _(v):
                v.memset(ones_col[:, :], 1.0)
                v.memset(ones_row[:, :], 1.0)
                v.memset(eps1[:, :], LN_EPS)
                v.memset(c20n[:, :], -20.0)
                ap = bass.AP(v_sb, 64, [[16 * 1040, 128], [1040, 16], [65, 16], [1, 1]])
                v.memset(ap, 1.0)

        for bb_ in range(BL):
            for t0 in range(NQT):
                gt = bb_ * NQT + t0
                with blk(nc, f"b_l{gt}") as b:
                    @b.sync
                    def _(s):
                        s.wait_ge(sto.h, sto.n)
                        load_fm_tile(s, lbufs["xt"], x, gt * TT, ld)
                        s.wait_ge(ld.h, ld.n)
                emit_ln_stats2(nc, lbufs)
                with blk(nc, f"b_lb{gt}") as b:
                    @b.tensor
                    def _(t):
                        mms.inc(t.matmul(psZA[:, 0:TT], ones_row[:, :],
                                         lbufs["m_row"][0:1, :], start=True, stop=True))
                        mms.inc(t.matmul(psZB[:, 0:TT], ones_row[:, :],
                                         lbufs["r_row"][0:1, :], start=True, stop=True))

                with blk(nc, f"b_lc{gt}") as b:
                    @b.vector
                    def _(v):
                        for c in range(DC):
                            v.tensor_sub(rtmp[:, :], lbufs["xt"][:, c * TT:(c + 1) * TT],
                                         psZA[:, 0:TT])
                            v.tensor_mul(lnt[:, c * TT:(c + 1) * TT], rtmp[:, :],
                                         psZB[:, 0:TT])

                with blk(nc, f"b_pr{gt}") as b:
                    mm0, ac0 = mms.n, acs.n

                    @b.tensor
                    def _(t):
                        t.wait_ge(wsem.h, wsem.n)
                        n_ev = 0
                        for oc in range(DC):
                            for wsb in (wq, wk):
                                if n_ev >= 2:
                                    t.wait_ge(acs.h, ac0 + n_ev - 1)
                                bank = psSA[n_ev % 2]
                                for kc in range(DC):
                                    ins = t.matmul(
                                        bank[:, :],
                                        wsb[:, kc * D + oc * 128: kc * D + oc * 128 + 128],
                                        lnt[:, kc * TT:(kc + 1) * TT],
                                        start=(kc == 0), stop=(kc == DC - 1))
                                mms.inc(ins)
                                n_ev += 1
                        for sub in range(TT // 128):
                            for g2 in range(2):
                                if n_ev >= 2:
                                    t.wait_ge(acs.h, ac0 + n_ev - 1)
                                bank = psSB[n_ev % 2]
                                for kc in range(DC):
                                    ins = t.matmul(
                                        bank[:, :],
                                        lnt[:, kc * TT + sub * 128: kc * TT + sub * 128 + 128],
                                        wv[:, kc * D + g2 * 512: kc * D + g2 * 512 + 512],
                                        start=(kc == 0), stop=(kc == DC - 1))
                                mms.inc(ins)
                                n_ev += 1

                    @b.scalar
                    def _(sc):
                        n_ev = 0
                        for oc in range(DC):
                            for dst in (q_sb, k_sb):
                                sc.wait_ge(mms.h, mm0 + n_ev + 1)
                                acs.inc(sc.activation(
                                    dst[:, oc * SEQ + t0 * TT: oc * SEQ + (t0 + 1) * TT],
                                    psSA[n_ev % 2][:, :], AF.Copy))
                                n_ev += 1
                        for sub in range(TT // 128):
                            s4 = t0 * 4 + sub
                            for g2 in range(2):
                                sc.wait_ge(mms.h, mm0 + n_ev + 1)
                                ap_out = bass.AP(
                                    v_sb, s4 * 1040 + g2 * 8 * 65,
                                    [[16 * 1040, 128], [65, 8], [1, 64]])
                                acs.inc(sc.activation(ap_out, psSB[n_ev % 2][:, :],
                                                      AF.Copy))
                                n_ev += 1

            for qk, buf in (("q", q_sb), ("k", k_sb)):
                for hp in range(DC):
                    with blk(nc, f"b_ro{bb_}{qk}{hp}") as b:
                        mm0 = mms.n

                        @b.tensor
                        def _(t):
                            for qt in range(NQT):
                                bank = (psSA + psSB)[qt]
                                mms.inc(t.matmul(
                                    bank[:, :], p32[:, :],
                                    buf[:, hp * SEQ + qt * TT: hp * SEQ + (qt + 1) * TT],
                                    start=True, stop=True))

                        @b.vector
                        def _(v):
                            for qt in range(NQT):
                                bank = (psSA + psSB)[qt]
                                v.wait_ge(mms.h, mm0 + qt + 1)
                                v.tensor_mul(rtmp[:, :], bank[:, :],
                                             stab[:, qt * TT:(qt + 1) * TT])
                                v.tensor_mul(rtmp2[:, :],
                                             buf[:, hp * SEQ + qt * TT: hp * SEQ + (qt + 1) * TT],
                                             ctab[:, qt * TT:(qt + 1) * TT])
                                v.tensor_add(
                                    buf[:, hp * SEQ + qt * TT: hp * SEQ + (qt + 1) * TT],
                                    rtmp[:, :], rtmp2[:, :])

            for qt in range(NQT):
                gq = bb_ * NQT + qt
                nkc = 4 * (qt + 1)
                for hp in range(DC):
                    with blk(nc, f"b_at{gq}_{hp}") as b:
                        mm0, ac0, dv0 = mms.n, acs.n, dvs.n
                        zw = []

                        @b.scalar
                        def _(sc):
                            nonlocal zw
                            for kc in range(nkc):
                                sc.wait_ge(mms.h, mm0 + (2 * kc + 1 if kc else 1))
                                sc.activation(pA[kc % 2][:, :], psSA[kc % 2][:, :],
                                              AF.Exp, bias=c20n[:, 0:1], scale=0.125)
                                acs.inc(sc.activation(pB[kc % 2][:, :], psSB[kc % 2][:, :],
                                                      AF.Exp, bias=c20n[:, 0:1],
                                                      scale=0.125))
                            sc.wait_ge(mms.h, mm0 + 2 * nkc)
                            sc.activation(zsb[0:64, :], psZA[0:64, :], AF.Copy)
                            acs.inc(sc.activation(zsb[64:128, :], psZB[0:64, :], AF.Copy))

                        @b.vector
                        def _(v):
                            for kc in range(nkc):
                                if kc >= 4 * qt:
                                    j = kc - 4 * qt
                                    v.wait_ge(acs.h, ac0 + kc + 1)
                                    o_j = 384 - 128 * j
                                    v.tensor_mul(pA[kc % 2][:, :], pA[kc % 2][:, :],
                                                 tri[:, o_j:o_j + TT])
                                    dvs.inc(v.tensor_mul(pB[kc % 2][:, :], pB[kc % 2][:, :],
                                                         tri[:, o_j:o_j + TT]))
                            v.wait_ge(mms.h, mm0 + 2 * nkc)
                            v.reciprocal(rzA[0:1, :], psZA[64:65, :])
                            v.reciprocal(rzB[0:1, :], psZB[64:65, :])
                            v.drain()
                            dvs.inc(v.engine_nop())
                            v.wait_ge(mms.h, mm0 + 2 * nkc + 1)
                            v.wait_ge(acs.h, ac0 + nkc + 1)
                            v.tensor_mul(znorm[:, hp * TT:(hp + 1) * TT],
                                         zsb[:, :], psRBC[:, :])

                        @b.tensor
                        def _(t):
                            ndv = 0
                            for kc in range(nkc):
                                if kc >= 2:
                                    t.wait_ge(acs.h, ac0 + kc - 1)
                                t.matmul(psSA[kc % 2][:, :],
                                         k_sb[0:64, hp * SEQ + kc * 128: hp * SEQ + kc * 128 + 128],
                                         q_sb[0:64, hp * SEQ + qt * TT: hp * SEQ + (qt + 1) * TT],
                                         start=True, stop=True)
                                mms.inc(t.matmul(
                                    psSB[kc % 2][:, :],
                                    k_sb[64:128, hp * SEQ + kc * 128: hp * SEQ + kc * 128 + 128],
                                    q_sb[64:128, hp * SEQ + qt * TT: hp * SEQ + (qt + 1) * TT],
                                    start=True, stop=True))
                                if kc >= 1:
                                    kp = kc - 1
                                    if kp >= 4 * qt:
                                        ndv += 1
                                        t.wait_ge(dvs.h, dv0 + ndv)
                                    else:
                                        t.wait_ge(acs.h, ac0 + kc)
                                    t.matmul(psZA[0:65, :],
                                             v_sb[:, kp * 1040 + (2 * hp) * 65: kp * 1040 + (2 * hp) * 65 + 65],
                                             pA[kp % 2][:, :],
                                             start=(kp == 0), stop=False)
                                    mms.inc(t.matmul(
                                        psZB[0:65, :],
                                        v_sb[:, kp * 1040 + (2 * hp + 1) * 65: kp * 1040 + (2 * hp + 1) * 65 + 65],
                                        pB[kp % 2][:, :],
                                        start=(kp == 0), stop=False))
                            kp = nkc - 1
                            ndv += 1
                            t.wait_ge(dvs.h, dv0 + ndv)
                            t.matmul(psZA[0:65, :],
                                     v_sb[:, kp * 1040 + (2 * hp) * 65: kp * 1040 + (2 * hp) * 65 + 65],
                                     pA[kp % 2][:, :], start=(kp == 0), stop=True)
                            mms.inc(t.matmul(
                                psZB[0:65, :],
                                v_sb[:, kp * 1040 + (2 * hp + 1) * 65: kp * 1040 + (2 * hp + 1) * 65 + 65],
                                pB[kp % 2][:, :], start=(kp == 0), stop=True))
                            t.wait_ge(dvs.h, dv0 + ndv + 1)
                            t.matmul(psRBC[0:64, :], ones_row[0:1, 0:64], rzA[0:1, :],
                                     start=True, stop=True)
                            mms.inc(t.matmul(psRBC[64:128, :], ones_row[0:1, 0:64],
                                             rzB[0:1, :], start=True, stop=True,
                                             tile_position=(0, 64)))

                with blk(nc, f"b_op{gq}") as b:
                    mm0, ac0, dv0, st0 = mms.n, acs.n, dvs.n, sto.n

                    @b.sync
                    def _(s):
                        for oc in range(DC):
                            ap_x = bass.AP(x, oc * 128 * TL + gq * TT, [[TL, 128], [1, TT]])
                            ld.inc(s.dma_start(out=xc1[:, :], in_=ap_x), 16)
                            s.wait_ge(dvs.h, dv0 + oc + 1)
                            ap_o = bass.AP(xo, oc * 128 * TL + gq * TT, [[TL, 128], [1, TT]])
                            sto.inc(s.dma_start(out=ap_o, in_=osb[oc % 2][:, :]), 16)
                        s.wait_ge(sto.h, sto.n)

                    @b.tensor
                    def _(t):
                        for oc in range(DC):
                            if oc >= 1:
                                t.wait_ge(acs.h, ac0 + oc)
                            for hp in range(DC):
                                ins = t.matmul(
                                    psO[:, :],
                                    wo[:, hp * D + oc * 128: hp * D + oc * 128 + 128],
                                    znorm[:, hp * TT:(hp + 1) * TT],
                                    start=(hp == 0), stop=(hp == DC - 1))
                            mms.inc(ins)

                    @b.scalar
                    def _(sc):
                        for oc in range(DC):
                            sc.wait_ge(mms.h, mm0 + oc + 1)
                            if oc >= 1:
                                sc.wait_ge(dvs.h, dv0 + oc)
                                sc.wait_ge(sto.h, st0 + 16 * oc)
                            acs.inc(sc.activation(osb[oc % 2][:, :], psO[:, :], AF.Copy))

                    @b.vector
                    def _(v):
                        for oc in range(DC):
                            v.wait_ge(acs.h, ac0 + oc + 1)
                            v.wait_ge(ld.h, ld.n - 16 * (DC - 1 - oc))
                            dvs.inc(v.tensor_add(osb[oc % 2][:, :], osb[oc % 2][:, :],
                                                 xc1[:, :]))

        with blk(nc, "b_fin") as b:
            @b.sync
            def _(s):
                s.wait_ge(sto.h, sto.n)

    return nc


def attn1_host_inputs(ins_w, x_fm):
    half = DH // 2
    freqs = np.arange(half, dtype=np.float32) / half
    theta = 1.0 / 10000.0 ** freqs
    t = np.arange(SEQ, dtype=np.float32)
    ang = t[:, None] * theta[None, :]
    cos, sin = np.cos(ang), np.sin(ang)
    ctab = np.zeros((128, SEQ), np.float32)
    stab = np.zeros((128, SEQ), np.float32)
    for r in range(128):
        i = r % half
        ctab[r] = cos[:, i]
        stab[r] = sin[:, i] * (1.0 if (r % DH) >= half else -1.0)
    p32 = np.zeros((128, 128), np.float32)
    for k in range(128):
        p32[k, k ^ 32] = 1.0
    tri = np.zeros((128, 896), np.float32)
    for p in range(128):
        tri[p, p + 384:] = 1.0
    return {
        "x": x_fm, "wqkv": ins_w["L1_attn_W_qkv"], "wo": ins_w["L1_attn_W_o"],
        "ctab": ctab, "stab": stab, "p32": p32, "tri": tri,
    }


def _run(nc, maps):
    res = run_bass_kernel_spmd(nc, maps, core_ids=list(range(NCORES)))
    return res.results


def kernel(**inputs):
    ins = {k: np.asarray(v) for k, v in inputs.items()}
    x_fm = _fm_shards(ins["L0_attn_x"].astype(np.float32, copy=False))
    ident = np.eye(128, dtype=np.float32)
    wr = np.ascontiguousarray(ins["L0_ffn_Wr"].astype(np.float32, copy=False))

    nc_sc = build_scores()
    r = _run(nc_sc, [{"x": s, "wr": wr, "ident": ident} for s in x_fm])
    scores1 = np.concatenate([rr["sc"] for rr in r], axis=0)

    nc_moe = build_moe()
    r = _run(nc_moe, [moe_host_inputs(scores1, c, ins, x_fm[c])
                      for c in range(NCORES)])
    x1 = [rr["xo"] for rr in r]

    nc_a0 = build_attn0()
    r = _run(nc_a0, [{"x": x1[c], "wq": ins["L0_attn_Wq"], "wk": ins["L0_attn_Wk"],
                      "wv": ins["L0_attn_Wv"], "wo": ins["L0_attn_Wout"], "wr": wr}
                     for c in range(NCORES)])
    x2 = [rr["xo"] for rr in r]
    scores3 = np.concatenate([rr["sc"] for rr in r], axis=0)

    r = _run(nc_moe, [moe_host_inputs(scores3, c, ins, x2[c])
                      for c in range(NCORES)])
    x3 = [rr["xo"] for rr in r]

    nc_f1 = build_l1ffn()
    r = _run(nc_f1, [l1ffn_host_inputs(ins, x3[c]) for c in range(NCORES)])
    x4 = [rr["xo"] for rr in r]

    nc_a1 = build_attn1()
    r = _run(nc_a1, [attn1_host_inputs(ins, x4[c]) for c in range(NCORES)])
    x5 = [rr["xo"] for rr in r]

    r = _run(nc_f1, [l1ffn_host_inputs(ins, x5[c]) for c in range(NCORES)])
    x6 = [rr["xo"] for rr in r]

    out = np.empty((16, SEQ, D), dtype=np.float32)
    for c in range(NCORES):
        out[BL * c:BL * c + BL] = x6[c].reshape(D, BL, SEQ).transpose(1, 2, 0)
    return out


# version 05
# speedup vs baseline: 1.0498x; 1.0424x over previous
import math
import numpy as np
import ml_dtypes
from contextlib import ExitStack

import concourse.bass as bass
import concourse.mybir as mybir
from concourse.bass_utils import run_bass_kernel_spmd

F32 = mybir.dt.float32
BF16 = mybir.dt.bfloat16
U32 = mybir.dt.uint32
AF = mybir.ActivationFunctionType
ALU = mybir.AluOpType
AX = mybir.AxisListType

NCORES = 8
D = 1024
DC = D // 128
SEQ = 2048
BL = 2
TL = BL * SEQ
TT = 512
NT = TL // TT
E = 8
K_SEL = 8192
NTOK = 16 * SEQ
DI = 4096
NH = 16
DH = 64
I1 = 938
LN_EPS = 1e-5
BIS_ITERS = 42


class Sem:
    def __init__(self, h):
        self.h = h
        self.n = 0

    def inc(self, ins, v=1):
        self.n += v
        ins.then_inc(self.h, v)
        return self.n



_BLK = [0]


def blk(nc, name):
    _BLK[0] += 1
    return nc.Block(f"{name}_{_BLK[0]}")

def sem(ctx, nc, name):
    return Sem(ctx.enter_context(nc.semaphore(name)))


def sbuf(ctx, nc, name, shape, dt):
    return ctx.enter_context(nc.sbuf_tensor(name, shape, dt))


def psum(ctx, nc, name, shape, dt=F32):
    return ctx.enter_context(nc.psum_tensor(name, shape, dt))


def load_fm_tile(q, dst, src_hbm, t0, sem_, width=TT, nchunk=DC, src_cols=TL):
    ap_in = bass.AP(src_hbm, t0,
                    [[src_cols, 128], [src_cols * 128, nchunk], [1, width]])
    ap_out = bass.AP(dst, 0, [[nchunk * width, 128], [width, nchunk], [1, width]])
    sem_.inc(q.dma_start(out=ap_out, in_=ap_in), 16)


def store_fm_tile(q, dst_hbm, src, t0, sem_, width=TT, nchunk=DC, dst_cols=TL,
                  accum=False):
    ap_out = bass.AP(dst_hbm, t0,
                     [[dst_cols, 128], [dst_cols * 128, nchunk], [1, width]])
    ap_in = bass.AP(src, 0, [[nchunk * width, 128], [width, nchunk], [1, width]])
    kw = {"accum_op": ALU.add} if accum else {}
    sem_.inc(q.dma_start(out=ap_out, in_=ap_in, **kw), 16)



def emit_ln_stats(nc, bufs, W=TT):
    xt = bufs["xt"]; xsq = bufs["xsq"]
    ps_sum = bufs["ps_sum"]; ps_sq = bufs["ps_sq"]; ps_tr = bufs["ps_tr"]
    m_row = bufs["m_row"]; r_row = bufs["r_row"]; var_row = bufs["var_row"]
    mcol = bufs["mcol"]; rcol = bufs["rcol"]
    nsub = W // 128

    with blk(nc, "lnsq") as b:
        @b.scalar
        def _(sc):
            sc.activation(xsq[:, :DC * W], xt[:, :DC * W], AF.Square)

    with blk(nc, "lnmm") as b:
        @b.tensor
        def _(t):
            for c in range(DC):
                t.matmul(ps_sum[0:1, :W], bufs["ones_col"][:, :],
                         xt[:, c * W:(c + 1) * W],
                         start=(c == 0), stop=(c == DC - 1))
            for c in range(DC):
                t.matmul(ps_sq[0:1, :W], bufs["ones_col"][:, :],
                         xsq[:, c * W:(c + 1) * W],
                         start=(c == 0), stop=(c == DC - 1))

    with blk(nc, "lnst1") as b:
        @b.vector
        def _(v):
            v.tensor_scalar_mul(m_row[0:1, :W], ps_sum[0:1, :W], 1.0 / D)
            v.tensor_scalar_mul(var_row[0:1, :W], ps_sq[0:1, :W], 1.0 / D)

    with blk(nc, "lnst2") as b:
        @b.scalar
        def _(sc):
            sc.activation(r_row[0:1, :W], m_row[0:1, :W], AF.Square)

    with blk(nc, "lnst3") as b:
        @b.vector
        def _(v):
            v.tensor_sub(var_row[0:1, :W], var_row[0:1, :W], r_row[0:1, :W])

    with blk(nc, "lnst4") as b:
        @b.scalar
        def _(sc):
            sc.activation(var_row[0:1, :W], var_row[0:1, :W], AF.Sqrt,
                          bias=bufs["eps1"][0:1, 0:1])

    with blk(nc, "lnst5") as b:
        @b.vector
        def _(v):
            v.reciprocal(r_row[0:1, :W], var_row[0:1, :W])

    with blk(nc, "lntr") as b:
        @b.tensor
        def _(t):
            one11 = bufs["ones_row"][0:1, 0:1]
            for s in range(nsub):
                t.matmul(ps_tr[:, s:s + 1], m_row[0:1, s * 128:(s + 1) * 128],
                         one11, start=True, stop=True)
            for s in range(nsub):
                t.matmul(ps_tr[:, nsub + s:nsub + s + 1],
                         r_row[0:1, s * 128:(s + 1) * 128], one11,
                         start=True, stop=True)

    with blk(nc, "lnev") as b:
        @b.vector
        def _(v):
            v.tensor_copy(mcol[:, 0:nsub], ps_tr[:, 0:nsub])
            v.tensor_copy(rcol[:, 0:nsub], ps_tr[:, nsub:2 * nsub])



def emit_scores_tile(nc, bufs, W=TT):
    xt = bufs["xt"]; wr_sb = bufs["wr_sb"]; csw_bc = bufs["csw_bc"]
    ps_raw = bufs["ps_raw"]
    mcol = bufs["mcol"]; rcol = bufs["rcol"]
    neg2 = bufs["neg2"]; pre = bufs["pre"]
    mx = bufs["mx"]; zrow = bufs["zrow"]
    exps = bufs["exps"]; sc_t = bufs["sc_t"]
    nsub = W // 128

    with blk(nc, "scraw") as b:
        @b.tensor
        def _(t):
            for s in range(nsub):
                for c in range(DC):
                    t.matmul(ps_raw[:, s * 8:(s + 1) * 8],
                             xt[:, c * W + s * 128: c * W + s * 128 + 128],
                             wr_sb[:, c * 8:(c + 1) * 8],
                             start=(c == 0), stop=(c == DC - 1))

    with blk(nc, "scadj0") as b:
        @b.vector
        def _(v):
            v.tensor_scalar_mul(bufs["rneg"][:, 0:nsub], rcol[:, 0:nsub], -1.0)

    with blk(nc, "scadj") as b:
        @b.vector
        def _(v):
            for s in range(nsub):
                v.scalar_tensor_tensor(neg2[:, s * 8:(s + 1) * 8], csw_bc[:, 0:8],
                                       mcol[:, s:s + 1], ps_raw[:, s * 8:(s + 1) * 8],
                                       ALU.mult, ALU.subtract)
            for s in range(nsub):
                v.tensor_scalar(pre[:, s * 8:(s + 1) * 8], neg2[:, s * 8:(s + 1) * 8],
                                bufs["rneg"][:, s:s + 1], None, ALU.mult)
            v.tensor_reduce(mx[:, 0:nsub].unsqueeze(2),
                            pre[:, 0:nsub * 8].rearrange("p (s e) -> p s e", e=8),
                            AX.X, ALU.max, negate=True)

    with blk(nc, "scexp") as b:
        @b.scalar
        def _(sc):
            for s in range(nsub):
                sc.activation(exps[:, s * 8:(s + 1) * 8], pre[:, s * 8:(s + 1) * 8],
                              AF.Exp, bias=mx[:, s:s + 1],
                              accum_out=zrow[:, s:s + 1])

    with blk(nc, "scdiv") as b:
        @b.vector
        def _(v):
            v.reciprocal(bufs["rz"][:, 0:nsub], zrow[:, 0:nsub])

    with blk(nc, "scdiv2") as b:
        @b.vector
        def _(v):
            for s in range(nsub):
                v.tensor_scalar(sc_t[:, s * 8:(s + 1) * 8], exps[:, s * 8:(s + 1) * 8],
                                bufs["rz"][:, s:s + 1], None, ALU.mult)


def alloc_scores_bufs(ctx, nc, tag, W=TT, ps=None):
    nsub = W // 128
    if ps is None:
        ps = {
            "ps_sum": psum(ctx, nc, f"{tag}_pssum", [1, W]),
            "ps_sq": psum(ctx, nc, f"{tag}_pssq", [1, W]),
            "ps_tr": psum(ctx, nc, f"{tag}_pstr", [128, 2 * nsub]),
            "ps_raw": psum(ctx, nc, f"{tag}_psraw", [128, nsub * 8]),
            "ps_bc": psum(ctx, nc, f"{tag}_psbc", [128, 8]),
        }
    return {
        **ps,
        "xt": sbuf(ctx, nc, f"{tag}_xt", [128, DC * W], F32),
        "xsq": sbuf(ctx, nc, f"{tag}_xsq", [128, DC * W], F32),
        "ones_col": sbuf(ctx, nc, f"{tag}_onec", [128, 1], F32),
        "eps1": sbuf(ctx, nc, f"{tag}_eps1", [1, 1], F32),
        "ones_row": sbuf(ctx, nc, f"{tag}_oner", [1, 128], F32),
        "ident_f32": sbuf(ctx, nc, f"{tag}_id", [128, 128], F32),
        "m_row": sbuf(ctx, nc, f"{tag}_mrow", [1, W], F32),
        "r_row": sbuf(ctx, nc, f"{tag}_rrow", [1, W], F32),
        "var_row": sbuf(ctx, nc, f"{tag}_vrow", [1, W], F32),
        "mcol": sbuf(ctx, nc, f"{tag}_mcol", [128, nsub], F32),
        "rcol": sbuf(ctx, nc, f"{tag}_rcol", [128, nsub], F32),
        "rneg": sbuf(ctx, nc, f"{tag}_rneg", [128, nsub], F32),
        "wr_sb": sbuf(ctx, nc, f"{tag}_wr", [128, DC * 8], F32),
        "csw_bc": sbuf(ctx, nc, f"{tag}_csw", [128, 8], F32),
        "neg2": sbuf(ctx, nc, f"{tag}_neg2", [128, nsub * 8], F32),
        "pre": sbuf(ctx, nc, f"{tag}_pre", [128, nsub * 8], F32),
        "mx": sbuf(ctx, nc, f"{tag}_mx", [128, nsub], F32),
        "zrow": sbuf(ctx, nc, f"{tag}_zrow", [128, nsub], F32),
        "rz": sbuf(ctx, nc, f"{tag}_rz", [128, nsub], F32),
        "exps": sbuf(ctx, nc, f"{tag}_exps", [128, nsub * 8], F32),
        "sc_t": sbuf(ctx, nc, f"{tag}_sct", [128, nsub * 8], F32),
    }


def emit_scores_prep(nc, bufs, wr_hbm, ld):
    with blk(nc, "scp1") as b:
        @b.sync
        def _(s):
            ap_in = bass.AP(wr_hbm, 0, [[8, 128], [8 * 128, DC], [1, 8]])
            ap_out = bass.AP(bufs["wr_sb"], 0, [[DC * 8, 128], [8, DC], [1, 8]])
            ld.inc(s.dma_start(out=ap_out, in_=ap_in), 16)
            s.wait_ge(ld.h, ld.n)

        @b.gpsimd
        def _(g):
            g.memset(bufs["ones_col"][:, :], 1.0)
            g.memset(bufs["ones_row"][:, :], 1.0)
            g.memset(bufs["eps1"][:, :], LN_EPS)

    with blk(nc, "scp2") as b:
        @b.tensor
        def _(t):
            for c in range(DC):
                t.matmul(bufs["ps_bc"][0:1, 0:8], bufs["ones_col"][:, :],
                         bufs["wr_sb"][:, c * 8:(c + 1) * 8],
                         start=(c == 0), stop=(c == DC - 1))

    with blk(nc, "scp3") as b:
        @b.vector
        def _(v):
            v.tensor_copy(bufs["csw_bc"][0:1, 0:8], bufs["ps_bc"][0:1, 0:8])

    with blk(nc, "scp4") as b:
        @b.tensor
        def _(t):
            t.matmul(bufs["ps_bc"][:, 0:8], bufs["ones_row"][:, :],
                     bufs["csw_bc"][0:1, 0:8], start=True, stop=True)

    with blk(nc, "scp5") as b:
        @b.vector
        def _(v):
            v.tensor_copy(bufs["csw_bc"][:, 0:8], bufs["ps_bc"][:, 0:8])


def build_scores(debug=False):
    nc = bass.Bass("TRN2", target_bir_lowering=False, debug=False)
    x = nc.dram_tensor("x", [D, TL], F32, kind="ExternalInput")
    wr = nc.dram_tensor("wr", [D, E], F32, kind="ExternalInput")
    ident_in = nc.dram_tensor("ident", [128, 128], F32, kind="ExternalInput")
    sc_out = nc.dram_tensor("sc", [TL, E], F32, kind="ExternalOutput")
    if debug:
        dbg_m = nc.dram_tensor("dbg_m", [1, TT], F32, kind="ExternalOutput")
        dbg_r = nc.dram_tensor("dbg_r", [1, TT], F32, kind="ExternalOutput")
        dbg_pre = nc.dram_tensor("dbg_pre", [128, 32], F32, kind="ExternalOutput")
        dbg_csw = nc.dram_tensor("dbg_csw", [128, 8], F32, kind="ExternalOutput")
        dbg_exps = nc.dram_tensor("dbg_exps", [128, 32], F32, kind="ExternalOutput")
        dbg_z = nc.dram_tensor("dbg_z", [128, 4], F32, kind="ExternalOutput")
        dbg_sct = nc.dram_tensor("dbg_sct", [128, 32], F32, kind="ExternalOutput")
        dbg_mx = nc.dram_tensor("dbg_mx", [128, 4], F32, kind="ExternalOutput")

    with ExitStack() as ctx:
        bufs = alloc_scores_bufs(ctx, nc, "s")
        ld = sem(ctx, nc, "ld")
        st = sem(ctx, nc, "st")

        with blk(nc, "id") as b:
            @b.sync
            def _(s):
                ld.inc(s.dma_start(out=bufs["ident_f32"][:, :],
                                   in_=ident_in.ap()), 16)
                s.wait_ge(ld.h, ld.n)

        emit_scores_prep(nc, bufs, wr, ld)

        for t0 in range(NT):
            with blk(nc, f"xl{t0}") as b:
                @b.sync
                def _(s):
                    load_fm_tile(s, bufs["xt"], x, t0 * TT, ld)
                    s.wait_ge(ld.h, ld.n)
            emit_ln_stats(nc, bufs)
            emit_scores_tile(nc, bufs)
            if debug and t0 == 0:
                with blk(nc, "dbg") as b:
                    @b.sync
                    def _(s):
                        st.inc(s.dma_start(out=dbg_m.ap(), in_=bufs["m_row"][0:1, :]), 16)
                        st.inc(s.dma_start(out=dbg_r.ap(), in_=bufs["r_row"][0:1, :]), 16)
                        st.inc(s.dma_start(out=dbg_pre.ap(), in_=bufs["pre"][:, :]), 16)
                        st.inc(s.dma_start(out=dbg_csw.ap(), in_=bufs["csw_bc"][:, :]), 16)
                        st.inc(s.dma_start(out=dbg_exps.ap(), in_=bufs["exps"][:, :]), 16)
                        st.inc(s.dma_start(out=dbg_z.ap(), in_=bufs["zrow"][:, :]), 16)
                        st.inc(s.dma_start(out=dbg_sct.ap(), in_=bufs["sc_t"][:, :]), 16)
                        st.inc(s.dma_start(out=dbg_mx.ap(), in_=bufs["mx"][:, :]), 16)
            with blk(nc, f"xs{t0}") as b:
                @b.sync
                def _(s):
                    for sub in range(TT // 128):
                        row0 = t0 * TT + sub * 128
                        st.inc(s.dma_start(
                            out=bass.AP(sc_out, row0 * E, [[E, 128], [1, E]]),
                            in_=bufs["sc_t"][:, sub * 8:(sub + 1) * 8]), 16)
                    s.wait_ge(st.h, st.n)

        with blk(nc, "fin") as b:
            @b.sync
            def _(s):
                s.wait_ge(st.h, st.n)

    return nc



def _host_constants():
    return {"ident": np.eye(128, dtype=np.float32)}


def _fm_shards(x):
    out = []
    for c in range(NCORES):
        xc = x[BL * c:BL * c + BL]
        xc = np.ascontiguousarray(
            xc.transpose(2, 0, 1).reshape(D, TL)).astype(np.float32, copy=False)
        out.append(xc)
    return out


def run_scores(x_fm_shards, wr, debug=False):
    nc = build_scores(debug)
    consts = _host_constants()
    maps = [{"x": s, "wr": np.ascontiguousarray(wr), "ident": consts["ident"]}
            for s in x_fm_shards]
    res = run_bass_kernel_spmd(nc, maps, core_ids=list(range(NCORES)))
    if debug:
        return res
    return np.concatenate([r["sc"] for r in res.results], axis=0)





def build_moe(debug=False):
    nc = bass.Bass("TRN2", target_bir_lowering=False, debug=False)
    x = nc.dram_tensor("x", [D, TL], F32, kind="ExternalInput")
    sbis_in = nc.dram_tensor("sbis", [128, NTOK // 16], F32, kind="ExternalInput")
    slocT_in = nc.dram_tensor("slocT", [E, TL], F32, kind="ExternalInput")
    w1_in = nc.dram_tensor("w1", [E, D, DI], F32, kind="ExternalInput")
    w2_in = nc.dram_tensor("w2", [E, DI, D], F32, kind="ExternalInput")
    b1f_in = nc.dram_tensor("b1f", [E, 128, DI // 128], F32, kind="ExternalInput")
    b2_in = nc.dram_tensor("b2", [E, D], F32, kind="ExternalInput")
    g16_in = nc.dram_tensor("g16", [128, 8], F32, kind="ExternalInput")
    lohi_in = nc.dram_tensor("lohi", [8, 2], F32, kind="ExternalInput")
    r16_in = nc.dram_tensor("r16", [8, 128], F32, kind="ExternalInput")
    xo = nc.dram_tensor("xo", [D, TL], F32, kind="ExternalOutput")
    if debug:
        dbg_t = nc.dram_tensor("dbg_t", [8, 1], F32, kind="ExternalOutput")
        dbg_m = nc.dram_tensor("dbg_m", [E, TL], F32, kind="ExternalOutput")
        dbg_it = nc.dram_tensor("dbg_it", [8, 12], F32, kind="ExternalOutput")
    mdram = nc.dram_tensor("mdram", [E, TL], F32)
    lndram = nc.dram_tensor("lndram", [D, TL], BF16)

    NHC = DI // 128
    with ExitStack() as ctx:
        lo8 = sbuf(ctx, nc, "lo8", [8, 1], F32)
        hi8 = sbuf(ctx, nc, "hi8", [8, 1], F32)
        mid8 = sbuf(ctx, nc, "mid8", [8, 1], F32)
        gef = sbuf(ctx, nc, "gef", [8, 1], F32)
        d8 = sbuf(ctx, nc, "d8", [8, 1], F32)
        d8b = sbuf(ctx, nc, "d8b", [8, 1], F32)
        t_col = sbuf(ctx, nc, "tcol", [8, 1], F32)
        itdbg = sbuf(ctx, nc, "itdbg", [8, 12], F32)
        ones_col = sbuf(ctx, nc, "onec", [128, 1], F32)
        ones_row = sbuf(ctx, nc, "oner", [1, 128], F32)
        eps1 = sbuf(ctx, nc, "eps1", [1, 1], F32)
        g16 = sbuf(ctx, nc, "g16s", [128, 8], F32)
        r16 = sbuf(ctx, nc, "r16s", [8, 128], F32)
        midbc = sbuf(ctx, nc, "midbc", [128, 1], F32)
        t_row = sbuf(ctx, nc, "trow", [1, E], F32)
        cnt = sbuf(ctx, nc, "cnt", [128, 1], F32)
        lnt2 = [sbuf(ctx, nc, f"lnt{i}", [128, DC * TT], BF16) for i in range(2)]
        mrow_cur = sbuf(ctx, nc, "mrowc", [1, TL], F32)
        mbc_t = sbuf(ctx, nc, "mbct", [128, TT], F32)
        gtmp = [sbuf(ctx, nc, f"gtmp{i}", [128, TT], F32) for i in range(2)]
        osb = [sbuf(ctx, nc, f"osb{i}", [128, TT], F32) for i in range(2)]
        b1sb = sbuf(ctx, nc, "b1sb", [128, DI // 128], F32)
        ps_h = [psum(ctx, nc, f"psh{i}", [128, TT]) for i in range(4)]
        ps_o = [psum(ctx, nc, f"pso{i}", [128, TT]) for i in range(2)]
        ps_g = psum(ctx, nc, "psg", [128, TT])
        ps_h2 = psum(ctx, nc, "psmisc2", [128, TT])
        ld = sem(ctx, nc, "ld")
        lnld = sem(ctx, nc, "lnld")
        wsem = sem(ctx, nc, "wsem")
        dvs = sem(ctx, nc, "dvs")
        mms = sem(ctx, nc, "mms")
        acs = sem(ctx, nc, "acs")
        gps = sem(ctx, nc, "gps")
        sto = sem(ctx, nc, "sto")

        with blk(nc, "m_p0") as b:
            @b.sync
            def _(s):
                ld.inc(s.dma_start(out=g16[:, :], in_=g16_in.ap()), 16)
                ld.inc(s.dma_start(out=r16[:, :], in_=r16_in.ap()), 16)
                s.wait_ge(ld.h, ld.n)

            @b.gpsimd
            def _(g):
                g.memset(ones_col[:, :], 1.0)
                g.memset(ones_row[:, :], 1.0)
                g.memset(eps1[:, :], LN_EPS)

            @b.sync
            def _(s2):
                ld.inc(s2.dma_start(out=lo8[:, :],
                                    in_=bass.AP(lohi_in, 0, [[2, 8], [1, 1]]),
                                    allow_slow_non_contiguous=True), 16)
                ld.inc(s2.dma_start(out=hi8[:, :],
                                    in_=bass.AP(lohi_in, 1, [[2, 8], [1, 1]]),
                                    allow_slow_non_contiguous=True), 16)
                s2.wait_ge(ld.h, ld.n)

        with ExitStack() as c1:
            sbis = sbuf(c1, nc, "sbiss", [128, NTOK // 16], F32)
            junk = sbuf(c1, nc, "junk", [128, NTOK // 16], BF16)
            with blk(nc, "m_p1l") as b:
                @b.sync
                def _(s):
                    ld.inc(s.dma_start(out=sbis[:, :], in_=sbis_in.ap()), 16)
                    s.wait_ge(ld.h, ld.n)

            for i in range(BIS_ITERS):
                with blk(nc, "bisa") as b:
                    @b.vector
                    def _(v):
                        v.tensor_scalar(mid8[:, :], lo8[:, :], hi8[:, 0:1], 0.5,
                                        ALU.add, ALU.mult)

                with blk(nc, "bisb") as b:
                    @b.tensor
                    def _(t):
                        t.matmul(ps_g[:, 0:1], r16[:, :], mid8[:, :],
                                 start=True, stop=True)

                with blk(nc, "bisc") as b:
                    @b.vector
                    def _(v):
                        v.tensor_copy(midbc[:, :], ps_g[:, 0:1])

                with blk(nc, "bisd") as b:
                    @b.vector
                    def _(v):
                        v.tensor_scalar(junk[:, :], sbis[:, :], midbc[:, 0:1],
                                        None, ALU.is_ge)
                        v.drain()
                        v.reduce_sum(cnt[:, 0:1], junk[:, :], AX.X)

                with blk(nc, "bise") as b:
                    @b.tensor
                    def _(t):
                        t.matmul(ps_h2[0:8, 0:1], g16[:, :], cnt[:, :],
                                 start=True, stop=True)

                with blk(nc, "bisf") as b:
                    @b.vector
                    def _(v):
                        v.tensor_scalar(gef[:, :], ps_h2[0:8, 0:1],
                                        float(K_SEL) - 0.5, None, ALU.is_ge)

                if debug and i < 4:
                    with blk(nc, "bisdbg") as b:
                        @b.vector
                        def _(v):
                            v.tensor_copy(itdbg[:, 3 * i:3 * i + 1], mid8[:, :])
                            v.tensor_copy(itdbg[:, 3 * i + 1:3 * i + 2], ps_h2[0:8, 0:1])
                            v.tensor_copy(itdbg[:, 3 * i + 2:3 * i + 3], gef[:, :])

                with blk(nc, "bisg") as b:
                    @b.vector
                    def _(v):
                        v.tensor_sub(d8[:, :], mid8[:, :], lo8[:, :])
                        v.tensor_sub(d8b[:, :], hi8[:, :], mid8[:, :])
                        v.drain()
                        v.scalar_tensor_tensor(lo8[:, :], d8[:, :], gef[:, 0:1],
                                               lo8[:, :], ALU.mult, ALU.add)
                        v.scalar_tensor_tensor(hi8[:, :], d8b[:, :], gef[:, 0:1],
                                               mid8[:, :], ALU.mult, ALU.add)

            with blk(nc, "m_p1f") as b:
                @b.vector
                def _(v):
                    dvs.inc(v.tensor_copy(t_col[:, :], lo8[:, :]))

        c2 = ctx.enter_context(ExitStack())
        srow = sbuf(c2, nc, "srow", [1, TL], F32)
        mrow_w = sbuf(c2, nc, "mroww", [1, TL], F32)
        tdram = nc.dram_tensor("tdram", [8, 1], F32)
        with blk(nc, "m_p2a") as b:
            @b.sync
            def _(s):
                sto.inc(s.dma_start(out=tdram.ap(), in_=t_col[:, :]), 16)
                s.wait_ge(sto.h, sto.n)
                sto.inc(s.dma_start(out=t_row[0:1, :],
                                    in_=bass.AP(tdram, 0, [[8, 1], [1, 8]])), 16)
                s.wait_ge(sto.h, sto.n)
                if debug:
                    sto.inc(s.dma_start(out=dbg_t.ap(), in_=t_col[:, :]), 16)
                    sto.inc(s.dma_start(out=dbg_it.ap(), in_=itdbg[:, :]), 16)
                    s.wait_ge(sto.h, sto.n)

        for ee in range(E):
            with blk(nc, f"m_p2r{ee}") as b:
                @b.sync
                def _(s):
                    sto.inc(s.dma_start(
                        out=srow[0:1, :],
                        in_=bass.AP(slocT_in, ee * TL, [[TL, 1], [1, TL]])), 16)
                    s.wait_ge(sto.h, sto.n)

            with blk(nc, f"m_p2m{ee}") as b:
                @b.vector
                def _(v):
                    v.tensor_scalar(mrow_w[0:1, :], srow[0:1, :],
                                    t_row[0:1, ee:ee + 1], None, ALU.is_ge)

            with blk(nc, f"m_p2n{ee}") as b:
                @b.vector
                def _(v):
                    v.scalar_tensor_tensor(mrow_w[0:1, :], mrow_w[0:1, :], 0.5,
                                           srow[0:1, :], ALU.mult, ALU.mult)

            with blk(nc, f"m_p2s{ee}") as b:
                @b.sync
                def _(s):
                    sto.inc(s.dma_start(
                        out=bass.AP(mdram, ee * TL, [[TL, 1], [1, TL]]),
                        in_=mrow_w[0:1, :]), 16)
                    s.wait_ge(sto.h, sto.n)
                    if debug:
                        sto.inc(s.dma_start(
                            out=bass.AP(dbg_m, ee * TL, [[TL, 1], [1, TL]]),
                            in_=mrow_w[0:1, :]), 16)
                        s.wait_ge(sto.h, sto.n)

        c2.close()

        with ExitStack() as c3:
            lbufs = {
                "xt": sbuf(c3, nc, "xt", [128, DC * TT], F32),
                "xsq": sbuf(c3, nc, "xsq", [128, DC * TT], F32),
                "ones_col": ones_col, "ones_row": ones_row, "eps1": eps1,
                "ps_sum": ps_g[0:1, 0:TT], "ps_sq": ps_h2[0:1, 0:TT],
                "ps_tr": ps_h[0][:, 0:8],
                "m_row": sbuf(c3, nc, "mrow", [1, TT], F32),
                "r_row": sbuf(c3, nc, "rrow", [1, TT], F32),
                "var_row": sbuf(c3, nc, "vrow", [1, TT], F32),
                "mcol": sbuf(c3, nc, "mcol", [128, 4], F32),
                "rcol": sbuf(c3, nc, "rcol", [128, 4], F32),
            }
            lnc_s = [sbuf(c3, nc, f"lnc{i}", [128, TT], BF16) for i in range(2)]
            tmpc = sbuf(c3, nc, "tmpc", [128, TT], F32)
            for t0 in range(NT):
                with blk(nc, f"m_p3l{t0}") as b:
                    @b.sync
                    def _(s):
                        s.wait_ge(sto.h, sto.n)
                        load_fm_tile(s, lbufs["xt"], x, t0 * TT, lnld)
                        s.wait_ge(lnld.h, lnld.n)
                        store_fm_tile(s, xo, lbufs["xt"], t0 * TT, sto)
                emit_ln_stats(nc, lbufs)
                with blk(nc, f"m_p3b{t0}") as b:
                    @b.tensor
                    def _(t):
                        mms.inc(t.matmul(ps_g[:, 0:TT], ones_row[:, :],
                                         lbufs["m_row"][0:1, :], start=True, stop=True))
                        mms.inc(t.matmul(ps_h2[:, 0:TT], ones_row[:, :],
                                         lbufs["r_row"][0:1, :], start=True, stop=True))

                with blk(nc, f"m_p3c{t0}") as b:
                    dv0, st0 = dvs.n, sto.n

                    @b.vector
                    def _(v):
                        for c in range(DC):
                            if c >= 2:
                                v.wait_ge(sto.h, st0 + 16 * (c - 1))
                            v.tensor_sub(tmpc[:, :], lbufs["xt"][:, c * TT:(c + 1) * TT],
                                         ps_g[:, 0:TT])
                            dvs.inc(v.tensor_mul(lnc_s[c % 2][:, :], tmpc[:, :],
                                                 ps_h2[:, 0:TT]))

                    @b.sync
                    def _(s):
                        for c in range(DC):
                            s.wait_ge(dvs.h, dv0 + c + 1)
                            ap_out = bass.AP(lndram, t0 * TT + c * 128 * TL,
                                             [[TL, 128], [1, TT]])
                            sto.inc(s.dma_start(out=ap_out, in_=lnc_s[c % 2][:, :]), 16)

        with blk(nc, "m_p3fin") as b:
            @b.sync
            def _(s):
                s.wait_ge(sto.h, sto.n)

        with ExitStack() as c5:
            w1sb = sbuf(c5, nc, "w1sb", [128, DC * DI], BF16)
            w2sb = sbuf(c5, nc, "w2sb", [128, NHC * D], BF16)
            h_sb = sbuf(c5, nc, "hsb", [128, NHC * TT], BF16)

            ln_base_m = [lnld.n]
            with blk(nc, "m_pre") as b:
                @b.sync
                def _(s):
                    ap_in = bass.AP(lndram, 0,
                                    [[TL, 128], [TL * 128, DC], [1, TT]])
                    ap_o = bass.AP(lnt2[0], 0, [[DC * TT, 128], [TT, DC], [1, TT]])
                    lnld.inc(s.dma_start(out=ap_o, in_=ap_in), 16)

            seq_i = [0]
            for e in range(E):
                with blk(nc, f"m_w{e}") as b:
                    @b.gpsimd
                    def _(g):
                        for kc in range(DC):
                            ap_in = bass.AP(w1_in, e * D * DI + kc * 128 * DI,
                                            [[DI, 128], [1, DI]])
                            wsem.inc(g.dma_start(
                                out=w1sb[:, kc * DI:(kc + 1) * DI], in_=ap_in), 16)
                        for kc in range(NHC):
                            ap_in = bass.AP(w2_in, e * DI * D + kc * 128 * D,
                                            [[D, 128], [1, D]])
                            wsem.inc(g.dma_start(
                                out=w2sb[:, kc * D:(kc + 1) * D], in_=ap_in), 16)
                        wsem.inc(g.dma_start(
                            out=b1sb[:, :],
                            in_=bass.AP(b1f_in, e * 128 * NHC, [[NHC, 128], [1, NHC]])), 16)
                        wsem.inc(g.dma_start(
                            out=mrow_cur[0:1, :],
                            in_=bass.AP(mdram, e * TL, [[TL, 1], [1, TL]])), 16)

                for t0 in range(NT):
                    with blk(nc, f"m_e{e}t{t0}") as b:
                        mm0, ac0, dv0, gp0 = mms.n, acs.n, dvs.n, gps.n
                        i_seq = seq_i[0]
                        seq_i[0] += 1
                        lnt = lnt2[i_seq % 2]
                        my_lnld = ln_base_m[0] + 16 * (i_seq + 1)
                        nt0 = (t0 + 1) % NT

                        @b.sync
                        def _(s):
                            if not (e == E - 1 and t0 == NT - 1):
                                ap_in = bass.AP(lndram, nt0 * TT,
                                                [[TL, 128], [TL * 128, DC], [1, TT]])
                                ap_o = bass.AP(lnt2[(i_seq + 1) % 2], 0,
                                               [[DC * TT, 128], [TT, DC], [1, TT]])
                                lnld.inc(s.dma_start(out=ap_o, in_=ap_in), 16)

                        @b.tensor
                        def _(t):
                            t.wait_ge(wsem.h, wsem.n)
                            t.wait_ge(lnld.h, my_lnld)
                            mms.inc(t.matmul(
                                ps_g[:, 0:TT], ones_row[:, :],
                                mrow_cur[0:1, t0 * TT:(t0 + 1) * TT],
                                start=True, stop=True))
                            for hc in range(NHC):
                                if hc >= 4:
                                    t.wait_ge(acs.h, ac0 + hc - 3)
                                for kc in range(DC):
                                    ins = t.matmul(
                                        ps_h[hc % 4][:, :],
                                        w1sb[:, kc * DI + hc * 128: kc * DI + hc * 128 + 128],
                                        lnt[:, kc * TT:(kc + 1) * TT],
                                        start=(kc == 0), stop=(kc == DC - 1))
                                mms.inc(ins)
                            t.wait_ge(dvs.h, dv0 + 1 + NHC)
                            for oc in range(DC):
                                if oc >= 2:
                                    t.wait_ge(acs.h, ac0 + NHC + oc - 1)
                                for hc in range(NHC):
                                    ins = t.matmul(
                                        ps_o[oc % 2][:, :],
                                        w2sb[:, hc * D + oc * 128: hc * D + oc * 128 + 128],
                                        h_sb[:, hc * TT:(hc + 1) * TT],
                                        start=(hc == 0), stop=(hc == NHC - 1))
                                mms.inc(ins)

                        @b.vector
                        def _(v):
                            v.wait_ge(mms.h, mm0 + 1)
                            dvs.inc(v.tensor_copy(mbc_t[:, :], ps_g[:, 0:TT]))
                            for hc in range(NHC):
                                v.wait_ge(acs.h, ac0 + hc + 1)
                                dvs.inc(v.tensor_mul(h_sb[:, hc * TT:(hc + 1) * TT],
                                                     gtmp[hc % 2][:, :], mbc_t[:, :]))

                        @b.scalar
                        def _(sc):
                            sc.wait_ge(wsem.h, wsem.n)
                            for hc in range(NHC):
                                sc.wait_ge(mms.h, mm0 + 1 + hc + 1)
                                if hc >= 2:
                                    sc.wait_ge(dvs.h, dv0 + 1 + hc - 1)
                                acs.inc(sc.activation(gtmp[hc % 2][:, :],
                                                      ps_h[hc % 4][:, :], AF.Gelu,
                                                      bias=b1sb[:, hc:hc + 1]))
                            for oc in range(DC):
                                sc.wait_ge(mms.h, mm0 + 1 + NHC + oc + 1)
                                if oc >= 2:
                                    sc.wait_ge(gps.h, gp0 + 16 * (oc - 1))
                                acs.inc(sc.activation(osb[oc % 2][:, :],
                                                      ps_o[oc % 2][:, :], AF.Copy))

                        @b.gpsimd
                        def _(g):
                            for oc in range(DC):
                                g.wait_ge(acs.h, ac0 + NHC + oc + 1)
                                ap_out = bass.AP(xo, oc * 128 * TL + t0 * TT,
                                                 [[TL, 128], [1, TT]])
                                gps.inc(g.dma_start(out=ap_out, in_=osb[oc % 2][:, :],
                                                    accum_op=ALU.add), 16)

        b2sb = sbuf(ctx, nc, "b2sb", [8, D], F32)
        mt8 = sbuf(ctx, nc, "mt8", [8, TT], F32)
        with blk(nc, "m_p6l") as b:
            @b.sync
            def _(s):
                ld.inc(s.dma_start(out=b2sb[:, :], in_=b2_in.ap()), 16)
                s.wait_ge(ld.h, ld.n)
        for t0 in range(NT):
            with blk(nc, f"m_p6_{t0}") as b:
                mm0, ac0, gp0 = mms.n, acs.n, gps.n

                @b.sync
                def _(s):
                    lnld.inc(s.dma_start(
                        out=mt8[:, :],
                        in_=bass.AP(mdram, t0 * TT, [[TL, E], [1, TT]])), 16)

                @b.tensor
                def _(t):
                    t.wait_ge(lnld.h, lnld.n)
                    for oc in range(DC):
                        if oc >= 2:
                            t.wait_ge(acs.h, ac0 + oc - 1)
                        mms.inc(t.matmul(ps_o[oc % 2][:, :],
                                         b2sb[:, oc * 128:(oc + 1) * 128],
                                         mt8[:, :], start=True, stop=True))

                @b.scalar
                def _(sc):
                    for oc in range(DC):
                        sc.wait_ge(mms.h, mm0 + oc + 1)
                        if oc >= 2:
                            sc.wait_ge(gps.h, gp0 + 16 * (oc - 1))
                        acs.inc(sc.activation(osb[oc % 2][:, :], ps_o[oc % 2][:, :],
                                              AF.Copy))

                @b.gpsimd
                def _(g):
                    for oc in range(DC):
                        g.wait_ge(acs.h, ac0 + oc + 1)
                        ap_out = bass.AP(xo, oc * 128 * TL + t0 * TT,
                                         [[TL, 128], [1, TT]])
                        gps.inc(g.dma_start(out=ap_out, in_=osb[oc % 2][:, :],
                                            accum_op=ALU.add), 16)

        with blk(nc, "m_fin") as b:
            @b.sync
            def _(s):
                s.wait_ge(gps.h, gps.n)
                s.wait_ge(sto.h, sto.n)

    return nc


def moe_host_inputs(scores_full, core, ins_w, x_fm):
    sT = np.ascontiguousarray(scores_full.T)
    sbis = np.ascontiguousarray(sT.reshape(128, NTOK // 16))
    slocT = np.ascontiguousarray(sT[:, core * TL:(core + 1) * TL])
    g16 = np.zeros((128, 8), np.float32)
    for r in range(128):
        g16[r, r // 16] = 1.0
    r16 = np.zeros((8, 128), np.float32)
    for r in range(128):
        r16[r // 16, r] = 1.0
    b1f = np.ascontiguousarray(
        ins_w["L0_ffn_b1"].reshape(E, DI // 128, 128).transpose(0, 2, 1))
    return {
        "x": x_fm, "sbis": sbis, "slocT": slocT,
        "w1": ins_w["L0_ffn_W1"], "w2": ins_w["L0_ffn_W2"],
        "b1f": b1f, "b2": ins_w["L0_ffn_b2"],
        "g16": g16, "r16": r16,
        "lohi": np.stack([np.zeros(8, np.float32), np.ones(8, np.float32)], 1),
    }


def emit_ln_stats2(nc, bufs, W=TT):
    xt = bufs["xt"]; xsqc = bufs["xsqc"]
    ps_sum = bufs["ps_sum"]; ps_sq = bufs["ps_sq"]; ps_tr = bufs["ps_tr"]
    m_row = bufs["m_row"]; r_row = bufs["r_row"]; var_row = bufs["var_row"]
    mcol = bufs["mcol"]; rcol = bufs["rcol"]
    nsub = W // 128
    mms2 = bufs["_mms"]; acs2 = bufs["_acs"]

    with blk(nc, "ln2a") as b:
        mm0, ac0 = mms2.n, acs2.n

        @b.scalar
        def _(sc):
            for c in range(DC):
                if c >= 2:
                    sc.wait_ge(mms2.h, mm0 + c - 1)
                acs2.inc(sc.activation(xsqc[c % 2][:, :], xt[:, c * W:(c + 1) * W],
                                       AF.Square))

        @b.tensor
        def _(t):
            for c in range(DC):
                t.matmul(ps_sum[0:1, :W], bufs["ones_col"][:, :],
                         xt[:, c * W:(c + 1) * W],
                         start=(c == 0), stop=(c == DC - 1))
            for c in range(DC):
                t.wait_ge(acs2.h, ac0 + c + 1)
                ins = t.matmul(ps_sq[0:1, :W], bufs["ones_col"][:, :],
                               xsqc[c % 2][:, :],
                               start=(c == 0), stop=(c == DC - 1))
                mms2.inc(ins)

    with blk(nc, "ln2b") as b:
        @b.vector
        def _(v):
            v.tensor_scalar_mul(m_row[0:1, :W], ps_sum[0:1, :W], 1.0 / D)
            v.tensor_scalar_mul(var_row[0:1, :W], ps_sq[0:1, :W], 1.0 / D)

    with blk(nc, "ln2c") as b:
        @b.scalar
        def _(sc):
            sc.activation(r_row[0:1, :W], m_row[0:1, :W], AF.Square)

    with blk(nc, "ln2d") as b:
        @b.vector
        def _(v):
            v.tensor_sub(var_row[0:1, :W], var_row[0:1, :W], r_row[0:1, :W])

    with blk(nc, "ln2e") as b:
        @b.scalar
        def _(sc):
            sc.activation(var_row[0:1, :W], var_row[0:1, :W], AF.Sqrt,
                          bias=bufs["eps1"][0:1, 0:1])

    with blk(nc, "ln2f") as b:
        @b.vector
        def _(v):
            v.reciprocal(r_row[0:1, :W], var_row[0:1, :W])

    with blk(nc, "ln2g") as b:
        @b.tensor
        def _(t):
            one11 = bufs["ones_row"][0:1, 0:1]
            for s in range(nsub):
                t.matmul(ps_tr[:, s:s + 1], m_row[0:1, s * 128:(s + 1) * 128],
                         one11, start=True, stop=True)
            for s in range(nsub):
                t.matmul(ps_tr[:, nsub + s:nsub + s + 1],
                         r_row[0:1, s * 128:(s + 1) * 128], one11,
                         start=True, stop=True)

    with blk(nc, "ln2h") as b:
        @b.vector
        def _(v):
            v.tensor_copy(mcol[:, 0:nsub], ps_tr[:, 0:nsub])
            v.tensor_copy(rcol[:, 0:nsub], ps_tr[:, nsub:2 * nsub])




def build_attn0():
    nc = bass.Bass("TRN2", target_bir_lowering=False, debug=False)
    x = nc.dram_tensor("x", [D, TL], F32, kind="ExternalInput")
    wq_in = nc.dram_tensor("wq", [D, D], F32, kind="ExternalInput")
    wk_in = nc.dram_tensor("wk", [D, D], F32, kind="ExternalInput")
    wv_in = nc.dram_tensor("wv", [D, D], F32, kind="ExternalInput")
    wo_in = nc.dram_tensor("wo", [D, D], F32, kind="ExternalInput")
    wr_in = nc.dram_tensor("wr", [D, E], F32, kind="ExternalInput")
    xo = nc.dram_tensor("xo", [D, TL], F32, kind="ExternalOutput")
    sc_out = nc.dram_tensor("sc", [TL, E], F32, kind="ExternalOutput")
    lnf = nc.dram_tensor("lnf", [D, TL], F32)

    NKC = SEQ // 128
    NQT = SEQ // TT

    with ExitStack() as ctx:
        ones_col = sbuf(ctx, nc, "onec", [128, 1], F32)
        ones_row = sbuf(ctx, nc, "oner", [1, 128], F32)
        eps1 = sbuf(ctx, nc, "eps1", [1, 1], F32)
        c20n = sbuf(ctx, nc, "c20n", [128, 1], F32)
        rzA = sbuf(ctx, nc, "rzA", [1, TT], F32)
        rzB = sbuf(ctx, nc, "rzB", [1, TT], F32)
        psSA = [psum(ctx, nc, f"psSA{i}", [128, TT]) for i in range(2)]
        psSB = [psum(ctx, nc, f"psSB{i}", [128, TT]) for i in range(2)]
        psZA = psum(ctx, nc, "psZA", [128, TT])
        psZB = psum(ctx, nc, "psZB", [128, TT])
        psRBC = psum(ctx, nc, "psRBC", [128, TT])
        psO = psum(ctx, nc, "psO", [128, TT])

        ld = sem(ctx, nc, "ld")
        wsem = sem(ctx, nc, "wsem")
        mms = sem(ctx, nc, "mms")
        acs = sem(ctx, nc, "acs")
        dvs = sem(ctx, nc, "dvs")
        sto = sem(ctx, nc, "sto")

        actx = ctx.enter_context(ExitStack())
        w0 = sbuf(actx, nc, "w0", [128, DC * D], BF16)
        w1 = sbuf(actx, nc, "w1b", [128, DC * D], BF16)
        w2 = sbuf(actx, nc, "w2b", [128, DC * D], BF16)
        q_sb = sbuf(actx, nc, "q_sb", [128, DC * SEQ], BF16)
        k_sb = sbuf(actx, nc, "k_sb", [128, DC * SEQ], BF16)
        v_sb = sbuf(actx, nc, "v_sb", [128, 16 * 1040], BF16)
        znorm = sbuf(actx, nc, "znorm", [128, DC * TT], BF16)
        zsb = sbuf(actx, nc, "zsb", [128, TT], F32)
        pA = [sbuf(actx, nc, f"pA{i}", [128, TT], BF16) for i in range(2)]
        pB = [sbuf(actx, nc, f"pB{i}", [128, TT], BF16) for i in range(2)]
        osb = [sbuf(actx, nc, f"osb{i}", [128, TT], F32) for i in range(2)]
        xc1 = sbuf(actx, nc, "xc1", [128, TT], F32)
        lc1 = sbuf(actx, nc, "lc1", [128, TT], F32)

        def load_w(q, dst, src, sem_):
            for kc in range(DC):
                ap_in = bass.AP(src, kc * 128 * D, [[D, 128], [1, D]])
                sem_.inc(q.dma_start(out=dst[:, kc * D:(kc + 1) * D], in_=ap_in), 16)

        with blk(nc, "a_p0") as b:
            @b.gpsimd
            def _(g):
                load_w(g, w0, wq_in, wsem)
                load_w(g, w1, wk_in, wsem)
                load_w(g, w2, wv_in, wsem)

            @b.vector
            def _(v):
                v.memset(ones_col[:, :], 1.0)
                v.memset(ones_row[:, :], 1.0)
                v.memset(eps1[:, :], LN_EPS)
                v.memset(c20n[:, :], -20.0)
                ap = bass.AP(v_sb, 64, [[16 * 1040, 128], [1040, 16], [65, 16], [1, 1]])
                v.memset(ap, 1.0)

        lbufs = {
            "xt": sbuf(actx, nc, "xt", [128, DC * TT], F32),
            "xsqc": [sbuf(actx, nc, f"xsqc{i}", [128, TT], F32) for i in range(2)],
            "_mms": mms, "_acs": acs,
            "ones_col": ones_col, "ones_row": ones_row, "eps1": eps1,
            "ps_sum": psZA[0:1, 0:TT], "ps_sq": psZB[0:1, 0:TT],
            "ps_tr": psRBC[:, 0:8],
            "m_row": sbuf(actx, nc, "mrow", [1, TT], F32),
            "r_row": sbuf(actx, nc, "rrow", [1, TT], F32),
            "var_row": sbuf(actx, nc, "vrow", [1, TT], F32),
            "mcol": sbuf(actx, nc, "mcol", [128, 4], F32),
            "rcol": sbuf(actx, nc, "rcol", [128, 4], F32),
        }
        lnt = sbuf(actx, nc, "lnta", [128, DC * TT], BF16)
        tmpc = lbufs["xsqc"][0]
        lnfc = [lbufs["xsqc"][1], sbuf(actx, nc, "lnfc1", [128, TT], F32)]

        for bb_ in range(BL):
            if bb_ > 0:
                with blk(nc, f"a_wq{bb_}") as b:
                    @b.gpsimd
                    def _(g):
                        load_w(g, w0, wq_in, wsem)
            for t0 in range(NQT):
                gt = bb_ * NQT + t0
                with blk(nc, f"a_l{gt}") as b:
                    @b.sync
                    def _(s):
                        s.wait_ge(sto.h, sto.n)
                        load_fm_tile(s, lbufs["xt"], x, gt * TT, ld)
                        s.wait_ge(ld.h, ld.n)
                emit_ln_stats2(nc, lbufs)
                with blk(nc, f"a_lb{gt}") as b:
                    @b.tensor
                    def _(t):
                        mms.inc(t.matmul(psZA[:, 0:TT], ones_row[:, :],
                                         lbufs["m_row"][0:1, :], start=True, stop=True))
                        mms.inc(t.matmul(psZB[:, 0:TT], ones_row[:, :],
                                         lbufs["r_row"][0:1, :], start=True, stop=True))

                with blk(nc, f"a_lc{gt}") as b:
                    dv0, st0 = dvs.n, sto.n

                    @b.vector
                    def _(v):
                        for c in range(DC):
                            if c >= 2:
                                v.wait_ge(sto.h, st0 + 16 * (c - 1))
                            v.tensor_sub(tmpc[:, :], lbufs["xt"][:, c * TT:(c + 1) * TT],
                                         psZA[:, 0:TT])
                            v.tensor_mul(lnfc[c % 2][:, :], tmpc[:, :], psZB[:, 0:TT])
                            v.drain()
                            dvs.inc(v.tensor_copy(lnt[:, c * TT:(c + 1) * TT],
                                                  lnfc[c % 2][:, :]))

                    @b.sync
                    def _(s):
                        for c in range(DC):
                            s.wait_ge(dvs.h, dv0 + c + 1)
                            ap_out = bass.AP(lnf, gt * TT + c * 128 * TL,
                                             [[TL, 128], [1, TT]])
                            sto.inc(s.dma_start(out=ap_out, in_=lnfc[c % 2][:, :]), 16)

                with blk(nc, f"a_pr{gt}") as b:
                    mm0, ac0 = mms.n, acs.n

                    @b.tensor
                    def _(t):
                        t.wait_ge(wsem.h, wsem.n)
                        n_ev = 0
                        for oc in range(DC):
                            for (wsb, dst_q) in ((w0, True), (w1, False)):
                                if n_ev >= 2:
                                    t.wait_ge(acs.h, ac0 + n_ev - 1)
                                bank = psSA[n_ev % 2]
                                for kc in range(DC):
                                    ins = t.matmul(
                                        bank[:, :],
                                        wsb[:, kc * D + oc * 128: kc * D + oc * 128 + 128],
                                        lnt[:, kc * TT:(kc + 1) * TT],
                                        start=(kc == 0), stop=(kc == DC - 1))
                                mms.inc(ins)
                                n_ev += 1
                        for sub in range(TT // 128):
                            s4 = t0 * 4 + sub
                            for g2 in range(2):
                                if n_ev >= 2:
                                    t.wait_ge(acs.h, ac0 + n_ev - 1)
                                bank = psSB[n_ev % 2]
                                for kc in range(DC):
                                    ins = t.matmul(
                                        bank[:, :],
                                        lnt[:, kc * TT + sub * 128: kc * TT + sub * 128 + 128],
                                        w2[:, kc * D + g2 * 512: kc * D + g2 * 512 + 512],
                                        start=(kc == 0), stop=(kc == DC - 1))
                                mms.inc(ins)
                                n_ev += 1

                    @b.scalar
                    def _(sc):
                        n_ev = 0
                        for oc in range(DC):
                            for dst_q in (True, False):
                                sc.wait_ge(mms.h, mm0 + n_ev + 1)
                                dst = q_sb if dst_q else k_sb
                                acs.inc(sc.activation(
                                    dst[:, oc * SEQ + t0 * TT: oc * SEQ + (t0 + 1) * TT],
                                    psSA[n_ev % 2][:, :], AF.Copy))
                                n_ev += 1
                        for sub in range(TT // 128):
                            s4 = t0 * 4 + sub
                            for g2 in range(2):
                                sc.wait_ge(mms.h, mm0 + n_ev + 1)
                                ap_out = bass.AP(
                                    v_sb, s4 * 1040 + g2 * 8 * 65,
                                    [[16 * 1040, 128], [65, 8], [1, 64]])
                                acs.inc(sc.activation(ap_out, psSB[n_ev % 2][:, :],
                                                      AF.Copy))
                                n_ev += 1

            with blk(nc, f"a_wo{bb_}") as b:
                @b.gpsimd
                def _(g):
                    load_w(g, w0, wo_in, wsem)

            for qt in range(NQT):
                gq = bb_ * NQT + qt
                for hp in range(DC):
                    with blk(nc, f"a_at{gq}_{hp}") as b:
                        mm0, ac0, dv0 = mms.n, acs.n, dvs.n

                        @b.tensor
                        def _(t):
                            t.wait_ge(wsem.h, wsem.n)
                            for kc in range(NKC):
                                if kc >= 2:
                                    t.wait_ge(acs.h, ac0 + kc - 1)
                                t.matmul(psSA[kc % 2][:, :],
                                         k_sb[0:64, hp * SEQ + kc * 128: hp * SEQ + kc * 128 + 128],
                                         q_sb[0:64, hp * SEQ + qt * TT: hp * SEQ + (qt + 1) * TT],
                                         start=True, stop=True)
                                mms.inc(t.matmul(
                                    psSB[kc % 2][:, :],
                                    k_sb[64:128, hp * SEQ + kc * 128: hp * SEQ + kc * 128 + 128],
                                    q_sb[64:128, hp * SEQ + qt * TT: hp * SEQ + (qt + 1) * TT],
                                    start=True, stop=True))
                                if kc >= 1:
                                    t.wait_ge(acs.h, ac0 + kc)
                                    t.matmul(psZA[0:65, :],
                                             v_sb[:, (kc - 1) * 1040 + (2 * hp) * 65: (kc - 1) * 1040 + (2 * hp) * 65 + 65],
                                             pA[(kc - 1) % 2][:, :],
                                             start=(kc == 1), stop=(kc == NKC))
                                    mms.inc(t.matmul(
                                        psZB[0:65, :],
                                        v_sb[:, (kc - 1) * 1040 + (2 * hp + 1) * 65: (kc - 1) * 1040 + (2 * hp + 1) * 65 + 65],
                                        pB[(kc - 1) % 2][:, :],
                                        start=(kc == 1), stop=(kc == NKC)))
                            kc = NKC
                            t.wait_ge(acs.h, ac0 + kc)
                            t.matmul(psZA[0:65, :],
                                     v_sb[:, (kc - 1) * 1040 + (2 * hp) * 65: (kc - 1) * 1040 + (2 * hp) * 65 + 65],
                                     pA[(kc - 1) % 2][:, :], start=False, stop=True)
                            mms.inc(t.matmul(
                                psZB[0:65, :],
                                v_sb[:, (kc - 1) * 1040 + (2 * hp + 1) * 65: (kc - 1) * 1040 + (2 * hp + 1) * 65 + 65],
                                pB[(kc - 1) % 2][:, :], start=False, stop=True))
                            t.wait_ge(dvs.h, dv0 + 1)
                            t.matmul(psRBC[0:64, :], ones_row[0:1, 0:64], rzA[0:1, :],
                                     start=True, stop=True)
                            mms.inc(t.matmul(psRBC[64:128, :], ones_row[0:1, 0:64],
                                             rzB[0:1, :], start=True, stop=True,
                                             tile_position=(0, 64)))

                        @b.scalar
                        def _(sc):
                            for kc in range(NKC):
                                sc.wait_ge(mms.h, mm0 + (2 * kc if kc else 1))
                                sc.activation(pA[kc % 2][:, :], psSA[kc % 2][:, :],
                                              AF.Exp, bias=c20n[:, 0:1], scale=0.125)
                                acs.inc(sc.activation(pB[kc % 2][:, :], psSB[kc % 2][:, :],
                                                      AF.Exp, bias=c20n[:, 0:1],
                                                      scale=0.125))
                            sc.wait_ge(mms.h, mm0 + 2 * NKC)
                            sc.activation(zsb[0:64, :], psZA[0:64, :], AF.Copy)
                            acs.inc(sc.activation(zsb[64:128, :], psZB[0:64, :], AF.Copy))

                        @b.vector
                        def _(v):
                            v.wait_ge(mms.h, mm0 + 2 * NKC)
                            v.reciprocal(rzA[0:1, :], psZA[64:65, :])
                            v.reciprocal(rzB[0:1, :], psZB[64:65, :])
                            v.drain()
                            dvs.inc(v.engine_nop())
                            v.wait_ge(mms.h, mm0 + 2 * NKC + 1)
                            v.wait_ge(acs.h, ac0 + NKC + 1)
                            v.tensor_mul(znorm[:, hp * TT:(hp + 1) * TT],
                                         zsb[:, :], psRBC[:, :])

                with blk(nc, f"a_op{gq}") as b:
                    mm0, ac0, dv0, st0 = mms.n, acs.n, dvs.n, sto.n

                    @b.sync
                    def _(s):
                        for oc in range(DC):
                            ap_x = bass.AP(x, oc * 128 * TL + gq * TT, [[TL, 128], [1, TT]])
                            ld.inc(s.dma_start(out=xc1[:, :], in_=ap_x), 16)
                            ap_l = bass.AP(lnf, oc * 128 * TL + gq * TT, [[TL, 128], [1, TT]])
                            ld.inc(s.dma_start(out=lc1[:, :], in_=ap_l), 16)
                            s.wait_ge(dvs.h, dv0 + oc + 1)
                            ap_o = bass.AP(xo, oc * 128 * TL + gq * TT, [[TL, 128], [1, TT]])
                            sto.inc(s.dma_start(out=ap_o, in_=osb[oc % 2][:, :]), 16)
                        s.wait_ge(sto.h, sto.n)

                    @b.tensor
                    def _(t):
                        for oc in range(DC):
                            if oc >= 1:
                                t.wait_ge(acs.h, ac0 + oc)
                            for hp in range(DC):
                                ins = t.matmul(
                                    psO[:, :],
                                    w0[:, hp * D + oc * 128: hp * D + oc * 128 + 128],
                                    znorm[:, hp * TT:(hp + 1) * TT],
                                    start=(hp == 0), stop=(hp == DC - 1))
                            mms.inc(ins)

                    @b.scalar
                    def _(sc):
                        for oc in range(DC):
                            sc.wait_ge(mms.h, mm0 + oc + 1)
                            if oc >= 2:
                                sc.wait_ge(dvs.h, dv0 + oc - 1)
                                sc.wait_ge(sto.h, st0 + 16 * (oc - 1))
                            acs.inc(sc.activation(osb[oc % 2][:, :], psO[:, :], AF.Copy))

                    @b.vector
                    def _(v):
                        for oc in range(DC):
                            v.wait_ge(acs.h, ac0 + oc + 1)
                            v.wait_ge(ld.h, st0 * 0 + mm0 * 0 + ld.n - 16 * (2 * (DC - 1 - oc)))
                            v.tensor_add(osb[oc % 2][:, :], osb[oc % 2][:, :],
                                         xc1[:, :])
                            v.drain()
                            dvs.inc(v.tensor_add(osb[oc % 2][:, :], osb[oc % 2][:, :],
                                                 lc1[:, :]))

        actx.close()
        sbufs = alloc_scores_bufs(ctx, nc, "s3", ps={
            "ps_sum": psSA[0][0:1, 0:TT], "ps_sq": psSA[1][0:1, 0:TT],
            "ps_tr": psSB[0][:, 0:8], "ps_raw": psSB[1][:, 0:32],
            "ps_bc": psZA[:, 0:8]})
        with blk(nc, "a_id") as b:
            @b.vector
            def _(v):
                v.memset(sbufs["ones_col"][:, :], 1.0)
                v.memset(sbufs["ones_row"][:, :], 1.0)
                v.memset(sbufs["eps1"][:, :], LN_EPS)
        emit_scores_prep(nc, sbufs, wr_in, ld)
        st2 = sem(ctx, nc, "st2")
        for t0 in range(NT):
            with blk(nc, f"a_sl{t0}") as b:
                @b.sync
                def _(s):
                    s.wait_ge(st2.h, st2.n)
                    load_fm_tile(s, sbufs["xt"], xo, t0 * TT, ld)
                    s.wait_ge(ld.h, ld.n)
            emit_ln_stats(nc, sbufs)
            emit_scores_tile(nc, sbufs)
            with blk(nc, f"a_ss{t0}") as b:
                @b.sync
                def _(s):
                    for sub in range(TT // 128):
                        row0 = t0 * TT + sub * 128
                        st2.inc(s.dma_start(
                            out=bass.AP(sc_out, row0 * E, [[E, 128], [1, E]]),
                            in_=sbufs["sc_t"][:, sub * 8:(sub + 1) * 8]), 16)
                    s.wait_ge(st2.h, st2.n)

        with blk(nc, "a_fin") as b:
            @b.sync
            def _(s):
                s.wait_ge(sto.h, sto.n)
                s.wait_ge(st2.h, st2.n)

    return nc



def build_l1ffn(nmat=9, skip=()):
    nc = bass.Bass("TRN2", target_bir_lowering=False, debug=False)
    x = nc.dram_tensor("x", [D, TL], F32, kind="ExternalInput")
    gwT_in = nc.dram_tensor("gwT", [D, E], F32, kind="ExternalInput")
    sg_in = nc.dram_tensor("sg", [D, D], F32, kind="ExternalInput")
    su_in = nc.dram_tensor("su", [D, D], F32, kind="ExternalInput")
    sd_in = nc.dram_tensor("sd", [D, D], F32, kind="ExternalInput")
    eg_in = nc.dram_tensor("eg", [E, D, D], F32, kind="ExternalInput")
    eu_in = nc.dram_tensor("eu", [E, D, D], F32, kind="ExternalInput")
    ed_in = nc.dram_tensor("ed", [E, D, D], F32, kind="ExternalInput")
    ident_in = nc.dram_tensor("ident", [128, 128], F32, kind="ExternalInput")
    xo = nc.dram_tensor("xo", [D, TL], F32, kind="ExternalOutput")
    lndram = nc.dram_tensor("lndram", [D, TL], BF16)
    cwdram = nc.dram_tensor("cwdram", [E, TL], F32)

    with ExitStack() as ctx:
        ld = sem(ctx, nc, "ld")
        lnld = sem(ctx, nc, "lnld")
        wsem = sem(ctx, nc, "wsem")
        dvs = sem(ctx, nc, "dvs")
        mms = sem(ctx, nc, "mms")
        acs = sem(ctx, nc, "acs")
        gps = sem(ctx, nc, "gps")
        sto = sem(ctx, nc, "sto")

        ps = [psum(ctx, nc, f"f_ps{i}", [128, TT]) for i in range(8)]
        ps_g = [ps[0], ps[1]]
        ps_u = [ps[2], ps[3]]
        ps_o = [ps[4], ps[5]]
        ps_m1 = ps[6]
        ps_m2 = ps[7]

        sbufs = alloc_scores_bufs(ctx, nc, "f", ps={
            "ps_sum": ps_m1[0:1, 0:TT], "ps_sq": ps_m2[0:1, 0:TT],
            "ps_tr": ps[0][:, 0:8], "ps_raw": ps[1][:, 0:32],
            "ps_bc": ps[2][:, 0:8]})
        ident = sbuf(ctx, nc, "identb", [128, 128], F32)
        lnc_s = [sbuf(ctx, nc, f"lnc{i}", [128, TT], BF16) for i in range(2)]
        cwsb = sbuf(ctx, nc, "cwsb", [8, TL], F32)
        cw_t = sbuf(ctx, nc, "cw_t", [128, 32], F32)
        m2c = sbuf(ctx, nc, "m2c", [128, 4], F32)
        eqm = sbuf(ctx, nc, "eqm", [128, 32], F32)
        l2 = sbuf(ctx, nc, "l2", [128, 32], F32)
        pind = sbuf(ctx, nc, "pind", [128, 32], F32)
        s2c = sbuf(ctx, nc, "s2c", [128, 4], F32)
        denc = sbuf(ctx, nc, "denc", [128, 4], F32)
        mxp = sbuf(ctx, nc, "mxp", [128, 4], F32)

        with blk(nc, "f_p0") as b:
            @b.sync
            def _(s):
                ld.inc(s.dma_start(out=ident[:, :], in_=ident_in.ap()), 16)
                s.wait_ge(ld.h, ld.n)

            @b.vector
            def _(v):
                v.memset(sbufs["ones_col"][:, :], 1.0)
                v.memset(sbufs["ones_row"][:, :], 1.0)
                v.memset(sbufs["eps1"][:, :], LN_EPS)

        emit_scores_prep(nc, sbufs, gwT_in, ld)

        for t0 in range(NT):
            with blk(nc, f"f_l{t0}") as b:
                @b.sync
                def _(s):
                    s.wait_ge(sto.h, sto.n)
                    load_fm_tile(s, sbufs["xt"], x, t0 * TT, lnld)
                    s.wait_ge(lnld.h, lnld.n)
                    store_fm_tile(s, xo, sbufs["xt"], t0 * TT, sto)
            emit_ln_stats(nc, sbufs)
            emit_scores_tile(nc, sbufs)
            with blk(nc, f"f_cw{t0}") as b:
                @b.vector
                def _(v):
                    pre = sbufs["pre"]; exps = sbufs["exps"]
                    v.tensor_reduce(mxp[:, 0:4].unsqueeze(2),
                                    pre[:, 0:32].rearrange("p (s e) -> p s e", e=8),
                                    AX.X, ALU.max)
                    v.drain()
                    for s in range(4):
                        v.tensor_scalar(eqm[:, s * 8:(s + 1) * 8],
                                        pre[:, s * 8:(s + 1) * 8],
                                        mxp[:, s:s + 1], None, ALU.is_ge)
                    v.drain()
                    for s in range(4):
                        v.scalar_tensor_tensor(l2[:, s * 8:(s + 1) * 8],
                                               eqm[:, s * 8:(s + 1) * 8], -1e30,
                                               pre[:, s * 8:(s + 1) * 8],
                                               ALU.mult, ALU.add)
                    v.drain()
                    v.tensor_reduce(m2c[:, 0:4].unsqueeze(2),
                                    l2[:, 0:32].rearrange("p (s e) -> p s e", e=8),
                                    AX.X, ALU.max)
                    v.drain()
                    for s in range(4):
                        v.tensor_scalar(eqm[:, s * 8:(s + 1) * 8],
                                        pre[:, s * 8:(s + 1) * 8],
                                        m2c[:, s:s + 1], None, ALU.is_ge)
                    v.drain()
                    v.tensor_mul(pind[:, 0:32], exps[:, 0:32], eqm[:, 0:32])
                    v.drain()
                    v.tensor_reduce(s2c[:, 0:4].unsqueeze(2),
                                    pind[:, 0:32].rearrange("p (s e) -> p s e", e=8),
                                    AX.X, ALU.add)
                    v.drain()
                    v.scalar_tensor_tensor(denc[:, 0:4], sbufs["zrow"][:, 0:4], 1e-8,
                                           s2c[:, 0:4], ALU.mult, ALU.add)
                    v.drain()
                    v.reciprocal(denc[:, 0:4], denc[:, 0:4])
                    v.drain()
                    for s in range(4):
                        dvs.inc(v.tensor_scalar(cw_t[:, s * 8:(s + 1) * 8],
                                                pind[:, s * 8:(s + 1) * 8],
                                                denc[:, s:s + 1], 0.5,
                                                ALU.mult, ALU.mult))

            with blk(nc, f"f_cwt{t0}") as b:
                @b.tensor
                def _(t):
                    for s in range(4):
                        t.transpose(ps_m1[0:8, s * 128:(s + 1) * 128],
                                    cw_t[:, s * 8:(s + 1) * 8], ident[:, :])

            with blk(nc, f"f_cwe{t0}") as b:
                @b.vector
                def _(v):
                    v.tensor_copy(cwsb[0:8, t0 * TT:(t0 + 1) * TT],
                                  ps_m1[0:8, 0:TT])

            with blk(nc, f"f_lb{t0}") as b:
                @b.tensor
                def _(t):
                    t.matmul(ps_m1[:, 0:TT], sbufs["ones_row"][:, :],
                             sbufs["m_row"][0:1, :], start=True, stop=True)
                    t.matmul(ps_m2[:, 0:TT], sbufs["ones_row"][:, :],
                             sbufs["r_row"][0:1, :], start=True, stop=True)

            with blk(nc, f"f_lc{t0}") as b:
                dv0, st0 = dvs.n, sto.n

                @b.vector
                def _(v):
                    for c in range(DC):
                        if c >= 2:
                            v.wait_ge(sto.h, st0 + 16 * (c - 1))
                        v.tensor_sub(sbufs["xsq"][:, 0:TT],
                                     sbufs["xt"][:, c * TT:(c + 1) * TT],
                                     ps_m1[:, 0:TT])
                        dvs.inc(v.tensor_mul(lnc_s[c % 2][:, :],
                                             sbufs["xsq"][:, 0:TT], ps_m2[:, 0:TT]))

                @b.sync
                def _(s):
                    for c in range(DC):
                        s.wait_ge(dvs.h, dv0 + c + 1)
                        ap_out = bass.AP(lndram, t0 * TT + c * 128 * TL,
                                         [[TL, 128], [1, TT]])
                        sto.inc(s.dma_start(out=ap_out, in_=lnc_s[c % 2][:, :]), 16)

        with blk(nc, "f_cwd") as b:
            @b.sync
            def _(s):
                sto.inc(s.dma_start(out=cwdram.ap(), in_=cwsb[:, :]), 16)
                s.wait_ge(sto.h, sto.n)

        with ExitStack() as c5:
            gg = sbuf(c5, nc, "gg", [128, DC * D], BF16)
            gu = sbuf(c5, nc, "gu", [128, DC * D], BF16)
            dt_ = sbuf(c5, nc, "dt", [128, DC * D], BF16)
            lnt5b = [sbuf(c5, nc, f"lnt5{i}", [128, DC * TT], BF16) for i in range(2)]
            t_sb = sbuf(c5, nc, "tsb", [128, DC * TT], BF16)
            sg_s = [sbuf(c5, nc, f"sgs{i}", [128, TT], F32) for i in range(2)]
            u_s = [sbuf(c5, nc, f"us{i}", [128, TT], F32) for i in range(2)]
            t1f = sbuf(c5, nc, "t1f", [128, TT], F32)
            wbc = sbuf(c5, nc, "wbc", [128, TT], F32)
            osb = [sbuf(c5, nc, f"osb5{i}", [128, TT], F32) for i in range(2)]
            wrow = sbuf(c5, nc, "wrow", [1, TL], F32)

            ln_base_5 = [lnld.n]
            with blk(nc, "f_pre") as b:
                @b.sync
                def _(s):
                    ap_in = bass.AP(lndram, 0, [[TL, 128], [TL * 128, DC], [1, TT]])
                    ap_o = bass.AP(lnt5b[0], 0, [[DC * TT, 128], [TT, DC], [1, TT]])
                    lnld.inc(s.dma_start(out=ap_o, in_=ap_in), 16)

            seq5 = [0]
            for mat in range(nmat):
                with blk(nc, f"f_w{mat}") as b:
                    @b.gpsimd
                    def _(g):
                        for kc in range(DC):
                            if mat == 0:
                                srcs = [(sg_in, 0), (su_in, 0), (sd_in, 0)]
                            else:
                                ee = mat - 1
                                srcs = [(eg_in, ee * D * D), (eu_in, ee * D * D),
                                        (ed_in, ee * D * D)]
                            for (srct, base), dst in zip(srcs, (gg, gu, dt_)):
                                ap_in = bass.AP(srct, base + kc * 128 * D,
                                                [[D, 128], [1, D]])
                                wsem.inc(g.dma_start(
                                    out=dst[:, kc * D:(kc + 1) * D], in_=ap_in), 16)
                        if mat > 0:
                            wsem.inc(g.dma_start(
                                out=wrow[0:1, :],
                                in_=bass.AP(cwdram, (mat - 1) * TL,
                                            [[TL, 1], [1, TL]])), 16)

                    if mat == 0:
                        @b.vector
                        def _(v):
                            v.memset(wbc[:, :], 0.5)

                for t0 in range(NT if "fm" not in skip else 0):
                    with blk(nc, f"f_m{mat}t{t0}") as b:
                        mm0, ac0, dv0, gp0 = mms.n, acs.n, dvs.n, gps.n
                        off = 1 if mat > 0 else 0
                        i5 = seq5[0]
                        seq5[0] += 1
                        lnt = lnt5b[i5 % 2]
                        my_ln5 = ln_base_5[0] + 16 * (i5 + 1)
                        nt5 = (t0 + 1) % NT

                        @b.sync
                        def _(s):
                            if not (mat == nmat - 1 and t0 == NT - 1):
                                ap_in = bass.AP(lndram, nt5 * TT,
                                                [[TL, 128], [TL * 128, DC], [1, TT]])
                                ap_o = bass.AP(lnt5b[(i5 + 1) % 2], 0,
                                               [[DC * TT, 128], [TT, DC], [1, TT]])
                                lnld.inc(s.dma_start(out=ap_o, in_=ap_in), 16)

                        @b.tensor
                        def _(t):
                            t.wait_ge(wsem.h, wsem.n)
                            t.wait_ge(lnld.h, my_ln5)
                            if mat > 0:
                                mms.inc(t.matmul(ps_m1[:, 0:TT], sbufs["ones_row"][:, :],
                                                 wrow[0:1, t0 * TT:(t0 + 1) * TT],
                                                 start=True, stop=True))
                            for fc in range(DC):
                                if fc >= 2:
                                    t.wait_ge(acs.h, ac0 + 2 * (fc - 2) + 2)
                                for kc in range(DC):
                                    i1 = t.matmul(
                                        ps_g[fc % 2][:, :],
                                        gg[:, kc * D + fc * 128: kc * D + fc * 128 + 128],
                                        lnt[:, kc * TT:(kc + 1) * TT],
                                        start=(kc == 0), stop=(kc == DC - 1))
                                mms.inc(i1)
                                for kc in range(DC):
                                    i2 = t.matmul(
                                        ps_u[fc % 2][:, :],
                                        gu[:, kc * D + fc * 128: kc * D + fc * 128 + 128],
                                        lnt[:, kc * TT:(kc + 1) * TT],
                                        start=(kc == 0), stop=(kc == DC - 1))
                                mms.inc(i2)
                            if "down" not in skip:
                                t.wait_ge(dvs.h, dv0 + off + 2 * DC)
                                for oc in range(DC):
                                    if oc >= 2:
                                        t.wait_ge(acs.h, ac0 + 2 * DC + oc - 1)
                                    for ic in range(DC):
                                        i3 = t.matmul(
                                            ps_o[oc % 2][:, :],
                                            dt_[:, ic * D + oc * 128: ic * D + oc * 128 + 128],
                                            t_sb[:, ic * TT:(ic + 1) * TT],
                                            start=(ic == 0), stop=(ic == DC - 1))
                                    mms.inc(i3)

                        @b.scalar
                        def _(sc):
                            for fc in range(DC):
                                sc.wait_ge(mms.h, mm0 + off + 2 * fc + 1)
                                if fc >= 2:
                                    sc.wait_ge(dvs.h, dv0 + off + 2 * (fc - 2) + 2)
                                acs.inc(sc.activation(sg_s[fc % 2][:, :],
                                                      ps_g[fc % 2][:, :],
                                                      AF.Copy if "silu" in skip else AF.Silu))
                                sc.wait_ge(mms.h, mm0 + off + 2 * fc + 2)
                                acs.inc(sc.activation(u_s[fc % 2][:, :],
                                                      ps_u[fc % 2][:, :], AF.Copy))
                            for oc in range(DC if "down" not in skip else 0):
                                sc.wait_ge(mms.h, mm0 + off + 2 * DC + oc + 1)
                                if oc >= 2:
                                    sc.wait_ge(gps.h, gp0 + 16 * (oc - 1))
                                acs.inc(sc.activation(osb[oc % 2][:, :],
                                                      ps_o[oc % 2][:, :], AF.Copy))

                        @b.vector
                        def _(v):
                            if mat > 0:
                                v.wait_ge(mms.h, mm0 + 1)
                                dvs.inc(v.tensor_copy(wbc[:, :], ps_m1[:, 0:TT]))
                            for fc in range(DC):
                                v.wait_ge(acs.h, ac0 + 2 * fc + 2)
                                dvs.inc(v.tensor_mul(t1f[:, :], sg_s[fc % 2][:, :],
                                                     u_s[fc % 2][:, :]))
                                dvs.inc(v.tensor_mul(t_sb[:, fc * TT:(fc + 1) * TT],
                                                     t1f[:, :], wbc[:, :]))

                        @b.gpsimd
                        def _(g):
                            for oc in range(DC if "down" not in skip else 0):
                                g.wait_ge(acs.h, ac0 + 2 * DC + oc + 1)
                                ap_out = bass.AP(xo, oc * 128 * TL + t0 * TT,
                                                 [[TL, 128], [1, TT]])
                                kw = {} if "add" in skip else {"accum_op": ALU.add}
                                gps.inc(g.dma_start(out=ap_out, in_=osb[oc % 2][:, :],
                                                    **kw), 16)

        with blk(nc, "f_fin") as b:
            @b.sync
            def _(s):
                s.wait_ge(gps.h, gps.n)
                s.wait_ge(sto.h, sto.n)

    return nc


def l1ffn_host_inputs(ins_w, x_fm):
    def padT(a, rows=None):
        aT = np.ascontiguousarray(a.T.astype(np.float32))
        out = np.zeros((D, D), np.float32)
        out[:aT.shape[0], :aT.shape[1]] = aT
        return out

    sg = ins_w["L1_ffn_shared_gate_up"][:I1]
    su = ins_w["L1_ffn_shared_gate_up"][I1:2 * I1]
    sd = ins_w["L1_ffn_shared_down"]
    eg = np.stack([padT(ins_w["L1_ffn_experts_gate_up"][e][:I1]) for e in range(E)])
    eu = np.stack([padT(ins_w["L1_ffn_experts_gate_up"][e][I1:2 * I1]) for e in range(E)])
    ed = np.stack([padT(ins_w["L1_ffn_experts_down"][e]) for e in range(E)])
    return {
        "x": x_fm,
        "gwT": np.ascontiguousarray(ins_w["L1_ffn_gate_weight"].T.astype(np.float32)),
        "sg": padT(sg), "su": padT(su), "sd": padT(sd),
        "eg": eg, "eu": eu, "ed": ed,
        "ident": np.eye(128, dtype=np.float32),
    }



def build_attn1():
    nc = bass.Bass("TRN2", target_bir_lowering=False, debug=False)
    x = nc.dram_tensor("x", [D, TL], F32, kind="ExternalInput")
    wqkv_in = nc.dram_tensor("wqkv", [D, 3 * D], F32, kind="ExternalInput")
    wo_in = nc.dram_tensor("wo", [D, D], F32, kind="ExternalInput")
    ctab_in = nc.dram_tensor("ctab", [128, SEQ], F32, kind="ExternalInput")
    stab_in = nc.dram_tensor("stab", [128, SEQ], F32, kind="ExternalInput")
    p32_in = nc.dram_tensor("p32", [128, 128], F32, kind="ExternalInput")
    tri_in = nc.dram_tensor("tri", [128, 896], F32, kind="ExternalInput")
    xo = nc.dram_tensor("xo", [D, TL], F32, kind="ExternalOutput")

    NKC = SEQ // 128
    NQT = SEQ // TT

    with ExitStack() as ctx:
        ones_col = sbuf(ctx, nc, "onec", [128, 1], F32)
        ones_row = sbuf(ctx, nc, "oner", [1, 128], F32)
        eps1 = sbuf(ctx, nc, "eps1", [1, 1], F32)
        c20n = sbuf(ctx, nc, "c20n", [128, 1], F32)
        rzB = sbuf(ctx, nc, "rzB", [1, TT], F32)
        psSA = [psum(ctx, nc, f"psSA{i}", [128, TT]) for i in range(2)]
        psSB = [psum(ctx, nc, f"psSB{i}", [128, TT]) for i in range(2)]
        psZA = psum(ctx, nc, "psZA", [128, TT])
        psZB = psum(ctx, nc, "psZB", [128, TT])
        psRBC = psum(ctx, nc, "psRBC", [128, TT])
        psO = psum(ctx, nc, "psO", [128, TT])

        ld = sem(ctx, nc, "ld")
        wsem = sem(ctx, nc, "wsem")
        mms = sem(ctx, nc, "mms")
        acs = sem(ctx, nc, "acs")
        dvs = sem(ctx, nc, "dvs")
        sto = sem(ctx, nc, "sto")

        wq = sbuf(ctx, nc, "wq1", [128, DC * D], BF16)
        wk = sbuf(ctx, nc, "wk1", [128, DC * D], BF16)
        wv = sbuf(ctx, nc, "wv1", [128, DC * D], BF16)
        wo = sbuf(ctx, nc, "wo1", [128, DC * D], BF16)
        q_sb = sbuf(ctx, nc, "q_sb1", [128, DC * SEQ], BF16)
        k_sb = sbuf(ctx, nc, "k_sb1", [128, DC * SEQ], BF16)
        v_sb = sbuf(ctx, nc, "v_sb1", [128, 16 * 1040], BF16)
        ctab = sbuf(ctx, nc, "ctab_s", [128, SEQ], BF16)
        stab = sbuf(ctx, nc, "stab_s", [128, SEQ], BF16)
        p32 = sbuf(ctx, nc, "p32s", [128, 128], BF16)
        tri = sbuf(ctx, nc, "tris", [128, 896], BF16)

        xt1h = sbuf(ctx, nc, "xt1", [128, DC * TT], F32)
        znorm = xt1h.bitcast(BF16)
        lbufs = {
            "xt": xt1h,
            "xsqc": [sbuf(ctx, nc, f"xs1{i}", [128, TT], F32) for i in range(2)],
            "_mms": mms, "_acs": acs,
            "ones_col": ones_col, "ones_row": ones_row, "eps1": eps1,
            "ps_sum": psZA[0:1, 0:TT], "ps_sq": psZB[0:1, 0:TT],
            "ps_tr": psRBC[:, 0:8],
            "m_row": sbuf(ctx, nc, "mrow1", [1, TT], F32),
            "r_row": sbuf(ctx, nc, "rrow1", [1, TT], F32),
            "var_row": sbuf(ctx, nc, "vrow1", [1, TT], F32),
            "mcol": sbuf(ctx, nc, "mcol1", [128, 4], F32),
            "rcol": sbuf(ctx, nc, "rcol1", [128, 4], F32),
        }
        lnt = sbuf(ctx, nc, "lnt1", [128, DC * TT], BF16)
        pf32 = lnt.bitcast(F32)
        pA = [lnt[:, 0:TT], lnt[:, 0:TT]]
        pB = [lnt[:, TT:2 * TT], lnt[:, TT:2 * TT]]
        zsb = pf32[:, TT:2 * TT]
        osb = [pf32[:, 2 * TT:3 * TT], pf32[:, 2 * TT:3 * TT]]
        xc1 = zsb
        rzA = pf32[0:1, 3 * TT:3 * TT + TT]
        rtmp = lbufs["xsqc"][0]
        rtmp2 = lbufs["xsqc"][1]

        def load_wslice(g, dst, col0):
            for kc in range(DC):
                ap_in = bass.AP(wqkv_in, kc * 128 * 3 * D + col0,
                                [[3 * D, 128], [1, D]])
                wsem.inc(g.dma_start(out=dst[:, kc * D:(kc + 1) * D], in_=ap_in), 16)

        with blk(nc, "b_p0") as b:
            @b.gpsimd
            def _(g):
                load_wslice(g, wq, 0)
                load_wslice(g, wk, D)
                load_wslice(g, wv, 2 * D)
                for kc in range(DC):
                    ap_in = bass.AP(wo_in, kc * 128 * D, [[D, 128], [1, D]])
                    wsem.inc(g.dma_start(out=wo[:, kc * D:(kc + 1) * D], in_=ap_in), 16)
                wsem.inc(g.dma_start(out=ctab[:, :], in_=ctab_in.ap()), 16)
                wsem.inc(g.dma_start(out=stab[:, :], in_=stab_in.ap()), 16)
                wsem.inc(g.dma_start(out=p32[:, :], in_=p32_in.ap()), 16)
                wsem.inc(g.dma_start(out=tri[:, :], in_=tri_in.ap()), 16)

            @b.vector
            def _(v):
                v.memset(ones_col[:, :], 1.0)
                v.memset(ones_row[:, :], 1.0)
                v.memset(eps1[:, :], LN_EPS)
                v.memset(c20n[:, :], -20.0)
                ap = bass.AP(v_sb, 64, [[16 * 1040, 128], [1040, 16], [65, 16], [1, 1]])
                v.memset(ap, 1.0)

        for bb_ in range(BL):
            for t0 in range(NQT):
                gt = bb_ * NQT + t0
                with blk(nc, f"b_l{gt}") as b:
                    @b.sync
                    def _(s):
                        s.wait_ge(sto.h, sto.n)
                        load_fm_tile(s, lbufs["xt"], x, gt * TT, ld)
                        s.wait_ge(ld.h, ld.n)
                emit_ln_stats2(nc, lbufs)
                with blk(nc, f"b_lb{gt}") as b:
                    @b.tensor
                    def _(t):
                        mms.inc(t.matmul(psZA[:, 0:TT], ones_row[:, :],
                                         lbufs["m_row"][0:1, :], start=True, stop=True))
                        mms.inc(t.matmul(psZB[:, 0:TT], ones_row[:, :],
                                         lbufs["r_row"][0:1, :], start=True, stop=True))

                with blk(nc, f"b_lc{gt}") as b:
                    @b.vector
                    def _(v):
                        for c in range(DC):
                            v.tensor_sub(rtmp[:, :], lbufs["xt"][:, c * TT:(c + 1) * TT],
                                         psZA[:, 0:TT])
                            v.tensor_mul(lnt[:, c * TT:(c + 1) * TT], rtmp[:, :],
                                         psZB[:, 0:TT])

                with blk(nc, f"b_pr{gt}") as b:
                    mm0, ac0 = mms.n, acs.n

                    @b.tensor
                    def _(t):
                        t.wait_ge(wsem.h, wsem.n)
                        n_ev = 0
                        for oc in range(DC):
                            for wsb in (wq, wk):
                                if n_ev >= 2:
                                    t.wait_ge(acs.h, ac0 + n_ev - 1)
                                bank = psSA[n_ev % 2]
                                for kc in range(DC):
                                    ins = t.matmul(
                                        bank[:, :],
                                        wsb[:, kc * D + oc * 128: kc * D + oc * 128 + 128],
                                        lnt[:, kc * TT:(kc + 1) * TT],
                                        start=(kc == 0), stop=(kc == DC - 1))
                                mms.inc(ins)
                                n_ev += 1
                        for sub in range(TT // 128):
                            for g2 in range(2):
                                if n_ev >= 2:
                                    t.wait_ge(acs.h, ac0 + n_ev - 1)
                                bank = psSB[n_ev % 2]
                                for kc in range(DC):
                                    ins = t.matmul(
                                        bank[:, :],
                                        lnt[:, kc * TT + sub * 128: kc * TT + sub * 128 + 128],
                                        wv[:, kc * D + g2 * 512: kc * D + g2 * 512 + 512],
                                        start=(kc == 0), stop=(kc == DC - 1))
                                mms.inc(ins)
                                n_ev += 1

                    @b.scalar
                    def _(sc):
                        n_ev = 0
                        for oc in range(DC):
                            for dst in (q_sb, k_sb):
                                sc.wait_ge(mms.h, mm0 + n_ev + 1)
                                acs.inc(sc.activation(
                                    dst[:, oc * SEQ + t0 * TT: oc * SEQ + (t0 + 1) * TT],
                                    psSA[n_ev % 2][:, :], AF.Copy))
                                n_ev += 1
                        for sub in range(TT // 128):
                            s4 = t0 * 4 + sub
                            for g2 in range(2):
                                sc.wait_ge(mms.h, mm0 + n_ev + 1)
                                ap_out = bass.AP(
                                    v_sb, s4 * 1040 + g2 * 8 * 65,
                                    [[16 * 1040, 128], [65, 8], [1, 64]])
                                acs.inc(sc.activation(ap_out, psSB[n_ev % 2][:, :],
                                                      AF.Copy))
                                n_ev += 1

            for qk, buf in (("q", q_sb), ("k", k_sb)):
                for hp in range(DC):
                    with blk(nc, f"b_ro{bb_}{qk}{hp}") as b:
                        mm0 = mms.n

                        @b.tensor
                        def _(t):
                            for qt in range(NQT):
                                bank = (psSA + psSB)[qt]
                                mms.inc(t.matmul(
                                    bank[:, :], p32[:, :],
                                    buf[:, hp * SEQ + qt * TT: hp * SEQ + (qt + 1) * TT],
                                    start=True, stop=True))

                        @b.vector
                        def _(v):
                            for qt in range(NQT):
                                bank = (psSA + psSB)[qt]
                                v.wait_ge(mms.h, mm0 + qt + 1)
                                v.tensor_mul(rtmp[:, :], bank[:, :],
                                             stab[:, qt * TT:(qt + 1) * TT])
                                v.tensor_mul(rtmp2[:, :],
                                             buf[:, hp * SEQ + qt * TT: hp * SEQ + (qt + 1) * TT],
                                             ctab[:, qt * TT:(qt + 1) * TT])
                                v.tensor_add(
                                    buf[:, hp * SEQ + qt * TT: hp * SEQ + (qt + 1) * TT],
                                    rtmp[:, :], rtmp2[:, :])

            for qt in range(NQT):
                gq = bb_ * NQT + qt
                nkc = 4 * (qt + 1)
                for hp in range(DC):
                    with blk(nc, f"b_at{gq}_{hp}") as b:
                        mm0, ac0, dv0 = mms.n, acs.n, dvs.n
                        zw = []

                        @b.scalar
                        def _(sc):
                            nonlocal zw
                            for kc in range(nkc):
                                sc.wait_ge(mms.h, mm0 + (2 * kc + 1 if kc else 1))
                                sc.activation(pA[kc % 2][:, :], psSA[kc % 2][:, :],
                                              AF.Exp, bias=c20n[:, 0:1], scale=0.125)
                                acs.inc(sc.activation(pB[kc % 2][:, :], psSB[kc % 2][:, :],
                                                      AF.Exp, bias=c20n[:, 0:1],
                                                      scale=0.125))
                            sc.wait_ge(mms.h, mm0 + 2 * nkc)
                            sc.activation(zsb[0:64, :], psZA[0:64, :], AF.Copy)
                            acs.inc(sc.activation(zsb[64:128, :], psZB[0:64, :], AF.Copy))

                        @b.vector
                        def _(v):
                            for kc in range(nkc):
                                if kc >= 4 * qt:
                                    j = kc - 4 * qt
                                    v.wait_ge(acs.h, ac0 + kc + 1)
                                    o_j = 384 - 128 * j
                                    v.tensor_mul(pA[kc % 2][:, :], pA[kc % 2][:, :],
                                                 tri[:, o_j:o_j + TT])
                                    dvs.inc(v.tensor_mul(pB[kc % 2][:, :], pB[kc % 2][:, :],
                                                         tri[:, o_j:o_j + TT]))
                            v.wait_ge(mms.h, mm0 + 2 * nkc)
                            v.reciprocal(rzA[0:1, :], psZA[64:65, :])
                            v.reciprocal(rzB[0:1, :], psZB[64:65, :])
                            v.drain()
                            dvs.inc(v.engine_nop())
                            v.wait_ge(mms.h, mm0 + 2 * nkc + 1)
                            v.wait_ge(acs.h, ac0 + nkc + 1)
                            v.tensor_mul(znorm[:, hp * TT:(hp + 1) * TT],
                                         zsb[:, :], psRBC[:, :])

                        @b.tensor
                        def _(t):
                            ndv = 0
                            for kc in range(nkc):
                                if kc >= 2:
                                    t.wait_ge(acs.h, ac0 + kc - 1)
                                t.matmul(psSA[kc % 2][:, :],
                                         k_sb[0:64, hp * SEQ + kc * 128: hp * SEQ + kc * 128 + 128],
                                         q_sb[0:64, hp * SEQ + qt * TT: hp * SEQ + (qt + 1) * TT],
                                         start=True, stop=True)
                                mms.inc(t.matmul(
                                    psSB[kc % 2][:, :],
                                    k_sb[64:128, hp * SEQ + kc * 128: hp * SEQ + kc * 128 + 128],
                                    q_sb[64:128, hp * SEQ + qt * TT: hp * SEQ + (qt + 1) * TT],
                                    start=True, stop=True))
                                if kc >= 1:
                                    kp = kc - 1
                                    if kp >= 4 * qt:
                                        ndv += 1
                                        t.wait_ge(dvs.h, dv0 + ndv)
                                    else:
                                        t.wait_ge(acs.h, ac0 + kc)
                                    t.matmul(psZA[0:65, :],
                                             v_sb[:, kp * 1040 + (2 * hp) * 65: kp * 1040 + (2 * hp) * 65 + 65],
                                             pA[kp % 2][:, :],
                                             start=(kp == 0), stop=False)
                                    mms.inc(t.matmul(
                                        psZB[0:65, :],
                                        v_sb[:, kp * 1040 + (2 * hp + 1) * 65: kp * 1040 + (2 * hp + 1) * 65 + 65],
                                        pB[kp % 2][:, :],
                                        start=(kp == 0), stop=False))
                            kp = nkc - 1
                            ndv += 1
                            t.wait_ge(dvs.h, dv0 + ndv)
                            t.matmul(psZA[0:65, :],
                                     v_sb[:, kp * 1040 + (2 * hp) * 65: kp * 1040 + (2 * hp) * 65 + 65],
                                     pA[kp % 2][:, :], start=(kp == 0), stop=True)
                            mms.inc(t.matmul(
                                psZB[0:65, :],
                                v_sb[:, kp * 1040 + (2 * hp + 1) * 65: kp * 1040 + (2 * hp + 1) * 65 + 65],
                                pB[kp % 2][:, :], start=(kp == 0), stop=True))
                            t.wait_ge(dvs.h, dv0 + ndv + 1)
                            t.matmul(psRBC[0:64, :], ones_row[0:1, 0:64], rzA[0:1, :],
                                     start=True, stop=True)
                            mms.inc(t.matmul(psRBC[64:128, :], ones_row[0:1, 0:64],
                                             rzB[0:1, :], start=True, stop=True,
                                             tile_position=(0, 64)))

                with blk(nc, f"b_op{gq}") as b:
                    mm0, ac0, dv0, st0 = mms.n, acs.n, dvs.n, sto.n

                    @b.sync
                    def _(s):
                        for oc in range(DC):
                            ap_x = bass.AP(x, oc * 128 * TL + gq * TT, [[TL, 128], [1, TT]])
                            ld.inc(s.dma_start(out=xc1[:, :], in_=ap_x), 16)
                            s.wait_ge(dvs.h, dv0 + oc + 1)
                            ap_o = bass.AP(xo, oc * 128 * TL + gq * TT, [[TL, 128], [1, TT]])
                            sto.inc(s.dma_start(out=ap_o, in_=osb[oc % 2][:, :]), 16)
                        s.wait_ge(sto.h, sto.n)

                    @b.tensor
                    def _(t):
                        for oc in range(DC):
                            if oc >= 1:
                                t.wait_ge(acs.h, ac0 + oc)
                            for hp in range(DC):
                                ins = t.matmul(
                                    psO[:, :],
                                    wo[:, hp * D + oc * 128: hp * D + oc * 128 + 128],
                                    znorm[:, hp * TT:(hp + 1) * TT],
                                    start=(hp == 0), stop=(hp == DC - 1))
                            mms.inc(ins)

                    @b.scalar
                    def _(sc):
                        for oc in range(DC):
                            sc.wait_ge(mms.h, mm0 + oc + 1)
                            if oc >= 1:
                                sc.wait_ge(dvs.h, dv0 + oc)
                                sc.wait_ge(sto.h, st0 + 16 * oc)
                            acs.inc(sc.activation(osb[oc % 2][:, :], psO[:, :], AF.Copy))

                    @b.vector
                    def _(v):
                        for oc in range(DC):
                            v.wait_ge(acs.h, ac0 + oc + 1)
                            v.wait_ge(ld.h, ld.n - 16 * (DC - 1 - oc))
                            dvs.inc(v.tensor_add(osb[oc % 2][:, :], osb[oc % 2][:, :],
                                                 xc1[:, :]))

        with blk(nc, "b_fin") as b:
            @b.sync
            def _(s):
                s.wait_ge(sto.h, sto.n)

    return nc


def attn1_host_inputs(ins_w, x_fm):
    half = DH // 2
    freqs = np.arange(half, dtype=np.float32) / half
    theta = 1.0 / 10000.0 ** freqs
    t = np.arange(SEQ, dtype=np.float32)
    ang = t[:, None] * theta[None, :]
    cos, sin = np.cos(ang), np.sin(ang)
    ctab = np.zeros((128, SEQ), np.float32)
    stab = np.zeros((128, SEQ), np.float32)
    for r in range(128):
        i = r % half
        ctab[r] = cos[:, i]
        stab[r] = sin[:, i] * (1.0 if (r % DH) >= half else -1.0)
    p32 = np.zeros((128, 128), np.float32)
    for k in range(128):
        p32[k, k ^ 32] = 1.0
    tri = np.zeros((128, 896), np.float32)
    for p in range(128):
        tri[p, p + 384:] = 1.0
    return {
        "x": x_fm, "wqkv": ins_w["L1_attn_W_qkv"], "wo": ins_w["L1_attn_W_o"],
        "ctab": ctab, "stab": stab, "p32": p32, "tri": tri,
    }


def _run(nc, maps):
    res = run_bass_kernel_spmd(nc, maps, core_ids=list(range(NCORES)))
    return res.results


def kernel(**inputs):
    ins = {k: np.asarray(v) for k, v in inputs.items()}
    x_fm = _fm_shards(ins["L0_attn_x"].astype(np.float32, copy=False))
    ident = np.eye(128, dtype=np.float32)
    wr = np.ascontiguousarray(ins["L0_ffn_Wr"].astype(np.float32, copy=False))

    nc_sc = build_scores()
    r = _run(nc_sc, [{"x": s, "wr": wr, "ident": ident} for s in x_fm])
    scores1 = np.concatenate([rr["sc"] for rr in r], axis=0)

    nc_moe = build_moe()
    r = _run(nc_moe, [moe_host_inputs(scores1, c, ins, x_fm[c])
                      for c in range(NCORES)])
    x1 = [rr["xo"] for rr in r]

    nc_a0 = build_attn0()
    r = _run(nc_a0, [{"x": x1[c], "wq": ins["L0_attn_Wq"], "wk": ins["L0_attn_Wk"],
                      "wv": ins["L0_attn_Wv"], "wo": ins["L0_attn_Wout"], "wr": wr}
                     for c in range(NCORES)])
    x2 = [rr["xo"] for rr in r]
    scores3 = np.concatenate([rr["sc"] for rr in r], axis=0)

    r = _run(nc_moe, [moe_host_inputs(scores3, c, ins, x2[c])
                      for c in range(NCORES)])
    x3 = [rr["xo"] for rr in r]

    nc_f1 = build_l1ffn()
    r = _run(nc_f1, [l1ffn_host_inputs(ins, x3[c]) for c in range(NCORES)])
    x4 = [rr["xo"] for rr in r]

    nc_a1 = build_attn1()
    r = _run(nc_a1, [attn1_host_inputs(ins, x4[c]) for c in range(NCORES)])
    x5 = [rr["xo"] for rr in r]

    r = _run(nc_f1, [l1ffn_host_inputs(ins, x5[c]) for c in range(NCORES)])
    x6 = [rr["xo"] for rr in r]

    out = np.empty((16, SEQ, D), dtype=np.float32)
    for c in range(NCORES):
        out[BL * c:BL * c + BL] = x6[c].reshape(D, BL, SEQ).transpose(1, 2, 0)
    return out


# version 06 (final)
# speedup vs baseline: 1.0544x; 1.0044x over previous
import math
import numpy as np
import ml_dtypes
from contextlib import ExitStack

import concourse.bass as bass
import concourse.mybir as mybir
from concourse.bass_utils import run_bass_kernel_spmd

F32 = mybir.dt.float32
BF16 = mybir.dt.bfloat16
U32 = mybir.dt.uint32
AF = mybir.ActivationFunctionType
ALU = mybir.AluOpType
AX = mybir.AxisListType

NCORES = 8
D = 1024
DC = D // 128
SEQ = 2048
BL = 2
TL = BL * SEQ
TT = 512
NT = TL // TT
E = 8
K_SEL = 8192
NTOK = 16 * SEQ
DI = 4096
NH = 16
DH = 64
I1 = 938
LN_EPS = 1e-5
BIS_ITERS = 42


class Sem:
    def __init__(self, h):
        self.h = h
        self.n = 0

    def inc(self, ins, v=1):
        self.n += v
        ins.then_inc(self.h, v)
        return self.n



_BLK = [0]


def blk(nc, name):
    _BLK[0] += 1
    return nc.Block(f"{name}_{_BLK[0]}")

def sem(ctx, nc, name):
    return Sem(ctx.enter_context(nc.semaphore(name)))


def sbuf(ctx, nc, name, shape, dt):
    return ctx.enter_context(nc.sbuf_tensor(name, shape, dt))


def psum(ctx, nc, name, shape, dt=F32):
    return ctx.enter_context(nc.psum_tensor(name, shape, dt))


def load_fm_tile(q, dst, src_hbm, t0, sem_, width=TT, nchunk=DC, src_cols=TL):
    ap_in = bass.AP(src_hbm, t0,
                    [[src_cols, 128], [src_cols * 128, nchunk], [1, width]])
    ap_out = bass.AP(dst, 0, [[nchunk * width, 128], [width, nchunk], [1, width]])
    sem_.inc(q.dma_start(out=ap_out, in_=ap_in), 16)


def store_fm_tile(q, dst_hbm, src, t0, sem_, width=TT, nchunk=DC, dst_cols=TL,
                  accum=False):
    ap_out = bass.AP(dst_hbm, t0,
                     [[dst_cols, 128], [dst_cols * 128, nchunk], [1, width]])
    ap_in = bass.AP(src, 0, [[nchunk * width, 128], [width, nchunk], [1, width]])
    kw = {"accum_op": ALU.add} if accum else {}
    sem_.inc(q.dma_start(out=ap_out, in_=ap_in, **kw), 16)



def emit_ln_stats(nc, bufs, W=TT):
    xt = bufs["xt"]; xsq = bufs["xsq"]
    ps_sum = bufs["ps_sum"]; ps_sq = bufs["ps_sq"]; ps_tr = bufs["ps_tr"]
    m_row = bufs["m_row"]; r_row = bufs["r_row"]; var_row = bufs["var_row"]
    mcol = bufs["mcol"]; rcol = bufs["rcol"]
    nsub = W // 128

    with blk(nc, "lnsq") as b:
        @b.scalar
        def _(sc):
            sc.activation(xsq[:, :DC * W], xt[:, :DC * W], AF.Square)

    with blk(nc, "lnmm") as b:
        @b.tensor
        def _(t):
            for c in range(DC):
                t.matmul(ps_sum[0:1, :W], bufs["ones_col"][:, :],
                         xt[:, c * W:(c + 1) * W],
                         start=(c == 0), stop=(c == DC - 1))
            for c in range(DC):
                t.matmul(ps_sq[0:1, :W], bufs["ones_col"][:, :],
                         xsq[:, c * W:(c + 1) * W],
                         start=(c == 0), stop=(c == DC - 1))

    with blk(nc, "lnst1") as b:
        @b.vector
        def _(v):
            v.tensor_scalar_mul(m_row[0:1, :W], ps_sum[0:1, :W], 1.0 / D)
            v.tensor_scalar_mul(var_row[0:1, :W], ps_sq[0:1, :W], 1.0 / D)

    with blk(nc, "lnst2") as b:
        @b.scalar
        def _(sc):
            sc.activation(r_row[0:1, :W], m_row[0:1, :W], AF.Square)

    with blk(nc, "lnst3") as b:
        @b.vector
        def _(v):
            v.tensor_sub(var_row[0:1, :W], var_row[0:1, :W], r_row[0:1, :W])

    with blk(nc, "lnst4") as b:
        @b.scalar
        def _(sc):
            sc.activation(var_row[0:1, :W], var_row[0:1, :W], AF.Sqrt,
                          bias=bufs["eps1"][0:1, 0:1])

    with blk(nc, "lnst5") as b:
        @b.vector
        def _(v):
            v.reciprocal(r_row[0:1, :W], var_row[0:1, :W])

    with blk(nc, "lntr") as b:
        @b.tensor
        def _(t):
            one11 = bufs["ones_row"][0:1, 0:1]
            for s in range(nsub):
                t.matmul(ps_tr[:, s:s + 1], m_row[0:1, s * 128:(s + 1) * 128],
                         one11, start=True, stop=True)
            for s in range(nsub):
                t.matmul(ps_tr[:, nsub + s:nsub + s + 1],
                         r_row[0:1, s * 128:(s + 1) * 128], one11,
                         start=True, stop=True)

    with blk(nc, "lnev") as b:
        @b.vector
        def _(v):
            v.tensor_copy(mcol[:, 0:nsub], ps_tr[:, 0:nsub])
            v.tensor_copy(rcol[:, 0:nsub], ps_tr[:, nsub:2 * nsub])



def emit_scores_tile(nc, bufs, W=TT):
    xt = bufs["xt"]; wr_sb = bufs["wr_sb"]; csw_bc = bufs["csw_bc"]
    ps_raw = bufs["ps_raw"]
    mcol = bufs["mcol"]; rcol = bufs["rcol"]
    neg2 = bufs["neg2"]; pre = bufs["pre"]
    mx = bufs["mx"]; zrow = bufs["zrow"]
    exps = bufs["exps"]; sc_t = bufs["sc_t"]
    nsub = W // 128

    with blk(nc, "scraw") as b:
        @b.tensor
        def _(t):
            for s in range(nsub):
                for c in range(DC):
                    t.matmul(ps_raw[:, s * 8:(s + 1) * 8],
                             xt[:, c * W + s * 128: c * W + s * 128 + 128],
                             wr_sb[:, c * 8:(c + 1) * 8],
                             start=(c == 0), stop=(c == DC - 1))

    with blk(nc, "scadj0") as b:
        @b.vector
        def _(v):
            v.tensor_scalar_mul(bufs["rneg"][:, 0:nsub], rcol[:, 0:nsub], -1.0)

    with blk(nc, "scadj") as b:
        @b.vector
        def _(v):
            for s in range(nsub):
                v.scalar_tensor_tensor(neg2[:, s * 8:(s + 1) * 8], csw_bc[:, 0:8],
                                       mcol[:, s:s + 1], ps_raw[:, s * 8:(s + 1) * 8],
                                       ALU.mult, ALU.subtract)
            v.drain()
            for s in range(nsub):
                v.tensor_scalar(pre[:, s * 8:(s + 1) * 8], neg2[:, s * 8:(s + 1) * 8],
                                bufs["rneg"][:, s:s + 1], None, ALU.mult)
            v.drain()
            v.tensor_reduce(mx[:, 0:nsub].unsqueeze(2),
                            pre[:, 0:nsub * 8].rearrange("p (s e) -> p s e", e=8),
                            AX.X, ALU.max, negate=True)

    with blk(nc, "scexp") as b:
        @b.scalar
        def _(sc):
            for s in range(nsub):
                sc.activation(exps[:, s * 8:(s + 1) * 8], pre[:, s * 8:(s + 1) * 8],
                              AF.Exp, bias=mx[:, s:s + 1],
                              accum_out=zrow[:, s:s + 1])

    with blk(nc, "scdiv") as b:
        @b.vector
        def _(v):
            v.reciprocal(bufs["rz"][:, 0:nsub], zrow[:, 0:nsub])
            v.drain()

    with blk(nc, "scdiv2") as b:
        @b.vector
        def _(v):
            for s in range(nsub):
                v.tensor_scalar(sc_t[:, s * 8:(s + 1) * 8], exps[:, s * 8:(s + 1) * 8],
                                bufs["rz"][:, s:s + 1], None, ALU.mult)


def alloc_scores_bufs(ctx, nc, tag, W=TT, ps=None):
    nsub = W // 128
    if ps is None:
        ps = {
            "ps_sum": psum(ctx, nc, f"{tag}_pssum", [1, W]),
            "ps_sq": psum(ctx, nc, f"{tag}_pssq", [1, W]),
            "ps_tr": psum(ctx, nc, f"{tag}_pstr", [128, 2 * nsub]),
            "ps_raw": psum(ctx, nc, f"{tag}_psraw", [128, nsub * 8]),
            "ps_bc": psum(ctx, nc, f"{tag}_psbc", [128, 8]),
        }
    return {
        **ps,
        "xt": sbuf(ctx, nc, f"{tag}_xt", [128, DC * W], F32),
        "xsq": sbuf(ctx, nc, f"{tag}_xsq", [128, DC * W], F32),
        "ones_col": sbuf(ctx, nc, f"{tag}_onec", [128, 1], F32),
        "eps1": sbuf(ctx, nc, f"{tag}_eps1", [1, 1], F32),
        "ones_row": sbuf(ctx, nc, f"{tag}_oner", [1, 128], F32),
        "ident_f32": sbuf(ctx, nc, f"{tag}_id", [128, 128], F32),
        "m_row": sbuf(ctx, nc, f"{tag}_mrow", [1, W], F32),
        "r_row": sbuf(ctx, nc, f"{tag}_rrow", [1, W], F32),
        "var_row": sbuf(ctx, nc, f"{tag}_vrow", [1, W], F32),
        "mcol": sbuf(ctx, nc, f"{tag}_mcol", [128, nsub], F32),
        "rcol": sbuf(ctx, nc, f"{tag}_rcol", [128, nsub], F32),
        "rneg": sbuf(ctx, nc, f"{tag}_rneg", [128, nsub], F32),
        "wr_sb": sbuf(ctx, nc, f"{tag}_wr", [128, DC * 8], F32),
        "csw_bc": sbuf(ctx, nc, f"{tag}_csw", [128, 8], F32),
        "neg2": sbuf(ctx, nc, f"{tag}_neg2", [128, nsub * 8], F32),
        "pre": sbuf(ctx, nc, f"{tag}_pre", [128, nsub * 8], F32),
        "mx": sbuf(ctx, nc, f"{tag}_mx", [128, nsub], F32),
        "zrow": sbuf(ctx, nc, f"{tag}_zrow", [128, nsub], F32),
        "rz": sbuf(ctx, nc, f"{tag}_rz", [128, nsub], F32),
        "exps": sbuf(ctx, nc, f"{tag}_exps", [128, nsub * 8], F32),
        "sc_t": sbuf(ctx, nc, f"{tag}_sct", [128, nsub * 8], F32),
    }


def emit_scores_prep(nc, bufs, wr_hbm, ld):
    with blk(nc, "scp1") as b:
        @b.sync
        def _(s):
            ap_in = bass.AP(wr_hbm, 0, [[8, 128], [8 * 128, DC], [1, 8]])
            ap_out = bass.AP(bufs["wr_sb"], 0, [[DC * 8, 128], [8, DC], [1, 8]])
            ld.inc(s.dma_start(out=ap_out, in_=ap_in), 16)
            s.wait_ge(ld.h, ld.n)

        @b.gpsimd
        def _(g):
            g.memset(bufs["ones_col"][:, :], 1.0)
            g.memset(bufs["ones_row"][:, :], 1.0)
            g.memset(bufs["eps1"][:, :], LN_EPS)

    with blk(nc, "scp2") as b:
        @b.tensor
        def _(t):
            for c in range(DC):
                t.matmul(bufs["ps_bc"][0:1, 0:8], bufs["ones_col"][:, :],
                         bufs["wr_sb"][:, c * 8:(c + 1) * 8],
                         start=(c == 0), stop=(c == DC - 1))

    with blk(nc, "scp3") as b:
        @b.vector
        def _(v):
            v.tensor_copy(bufs["csw_bc"][0:1, 0:8], bufs["ps_bc"][0:1, 0:8])

    with blk(nc, "scp4") as b:
        @b.tensor
        def _(t):
            t.matmul(bufs["ps_bc"][:, 0:8], bufs["ones_row"][:, :],
                     bufs["csw_bc"][0:1, 0:8], start=True, stop=True)

    with blk(nc, "scp5") as b:
        @b.vector
        def _(v):
            v.tensor_copy(bufs["csw_bc"][:, 0:8], bufs["ps_bc"][:, 0:8])


def build_scores(debug=False):
    nc = bass.Bass("TRN2", target_bir_lowering=False, debug=False)
    x = nc.dram_tensor("x", [D, TL], F32, kind="ExternalInput")
    wr = nc.dram_tensor("wr", [D, E], F32, kind="ExternalInput")
    ident_in = nc.dram_tensor("ident", [128, 128], F32, kind="ExternalInput")
    sc_out = nc.dram_tensor("sc", [TL, E], F32, kind="ExternalOutput")
    if debug:
        dbg_m = nc.dram_tensor("dbg_m", [1, TT], F32, kind="ExternalOutput")
        dbg_r = nc.dram_tensor("dbg_r", [1, TT], F32, kind="ExternalOutput")
        dbg_pre = nc.dram_tensor("dbg_pre", [128, 32], F32, kind="ExternalOutput")
        dbg_csw = nc.dram_tensor("dbg_csw", [128, 8], F32, kind="ExternalOutput")
        dbg_exps = nc.dram_tensor("dbg_exps", [128, 32], F32, kind="ExternalOutput")
        dbg_z = nc.dram_tensor("dbg_z", [128, 4], F32, kind="ExternalOutput")
        dbg_sct = nc.dram_tensor("dbg_sct", [128, 32], F32, kind="ExternalOutput")
        dbg_mx = nc.dram_tensor("dbg_mx", [128, 4], F32, kind="ExternalOutput")

    with ExitStack() as ctx:
        bufs = alloc_scores_bufs(ctx, nc, "s")
        ld = sem(ctx, nc, "ld")
        st = sem(ctx, nc, "st")

        with blk(nc, "id") as b:
            @b.sync
            def _(s):
                ld.inc(s.dma_start(out=bufs["ident_f32"][:, :],
                                   in_=ident_in.ap()), 16)
                s.wait_ge(ld.h, ld.n)

        emit_scores_prep(nc, bufs, wr, ld)

        for t0 in range(NT):
            with blk(nc, f"xl{t0}") as b:
                @b.sync
                def _(s):
                    load_fm_tile(s, bufs["xt"], x, t0 * TT, ld)
                    s.wait_ge(ld.h, ld.n)
            emit_ln_stats(nc, bufs)
            emit_scores_tile(nc, bufs)
            if debug and t0 == 0:
                with blk(nc, "dbg") as b:
                    @b.sync
                    def _(s):
                        st.inc(s.dma_start(out=dbg_m.ap(), in_=bufs["m_row"][0:1, :]), 16)
                        st.inc(s.dma_start(out=dbg_r.ap(), in_=bufs["r_row"][0:1, :]), 16)
                        st.inc(s.dma_start(out=dbg_pre.ap(), in_=bufs["pre"][:, :]), 16)
                        st.inc(s.dma_start(out=dbg_csw.ap(), in_=bufs["csw_bc"][:, :]), 16)
                        st.inc(s.dma_start(out=dbg_exps.ap(), in_=bufs["exps"][:, :]), 16)
                        st.inc(s.dma_start(out=dbg_z.ap(), in_=bufs["zrow"][:, :]), 16)
                        st.inc(s.dma_start(out=dbg_sct.ap(), in_=bufs["sc_t"][:, :]), 16)
                        st.inc(s.dma_start(out=dbg_mx.ap(), in_=bufs["mx"][:, :]), 16)
            with blk(nc, f"xs{t0}") as b:
                @b.sync
                def _(s):
                    for sub in range(TT // 128):
                        row0 = t0 * TT + sub * 128
                        st.inc(s.dma_start(
                            out=bass.AP(sc_out, row0 * E, [[E, 128], [1, E]]),
                            in_=bufs["sc_t"][:, sub * 8:(sub + 1) * 8]), 16)
                    s.wait_ge(st.h, st.n)

        with blk(nc, "fin") as b:
            @b.sync
            def _(s):
                s.wait_ge(st.h, st.n)

    return nc



def _host_constants():
    return {"ident": np.eye(128, dtype=np.float32)}


def _fm_shards(x):
    out = []
    for c in range(NCORES):
        xc = x[BL * c:BL * c + BL]
        xc = np.ascontiguousarray(
            xc.transpose(2, 0, 1).reshape(D, TL)).astype(np.float32, copy=False)
        out.append(xc)
    return out


def run_scores(x_fm_shards, wr, debug=False):
    nc = build_scores(debug)
    consts = _host_constants()
    maps = [{"x": s, "wr": np.ascontiguousarray(wr), "ident": consts["ident"]}
            for s in x_fm_shards]
    res = run_bass_kernel_spmd(nc, maps, core_ids=list(range(NCORES)))
    if debug:
        return res
    return np.concatenate([r["sc"] for r in res.results], axis=0)





def build_moe(debug=False):
    nc = bass.Bass("TRN2", target_bir_lowering=False, debug=False)
    x = nc.dram_tensor("x", [D, TL], F32, kind="ExternalInput")
    sbis_in = nc.dram_tensor("sbis", [128, NTOK // 16], F32, kind="ExternalInput")
    slocT_in = nc.dram_tensor("slocT", [E, TL], F32, kind="ExternalInput")
    w1_in = nc.dram_tensor("w1", [E, D, DI], F32, kind="ExternalInput")
    w2_in = nc.dram_tensor("w2", [E, DI, D], F32, kind="ExternalInput")
    b1f_in = nc.dram_tensor("b1f", [E, 128, DI // 128], F32, kind="ExternalInput")
    b2_in = nc.dram_tensor("b2", [E, D], F32, kind="ExternalInput")
    g16_in = nc.dram_tensor("g16", [128, 8], F32, kind="ExternalInput")
    lohi_in = nc.dram_tensor("lohi", [8, 2], F32, kind="ExternalInput")
    r16_in = nc.dram_tensor("r16", [8, 128], F32, kind="ExternalInput")
    xo = nc.dram_tensor("xo", [D, TL], F32, kind="ExternalOutput")
    if debug:
        dbg_t = nc.dram_tensor("dbg_t", [8, 1], F32, kind="ExternalOutput")
        dbg_m = nc.dram_tensor("dbg_m", [E, TL], F32, kind="ExternalOutput")
        dbg_it = nc.dram_tensor("dbg_it", [8, 12], F32, kind="ExternalOutput")
    mdram = nc.dram_tensor("mdram", [E, TL], F32)
    lndram = nc.dram_tensor("lndram", [D, TL], BF16)

    NHC = DI // 128
    with ExitStack() as ctx:
        lo8 = sbuf(ctx, nc, "lo8", [8, 1], F32)
        hi8 = sbuf(ctx, nc, "hi8", [8, 1], F32)
        mid8 = sbuf(ctx, nc, "mid8", [8, 1], F32)
        gef = sbuf(ctx, nc, "gef", [8, 1], F32)
        d8 = sbuf(ctx, nc, "d8", [8, 1], F32)
        d8b = sbuf(ctx, nc, "d8b", [8, 1], F32)
        t_col = sbuf(ctx, nc, "tcol", [8, 1], F32)
        itdbg = sbuf(ctx, nc, "itdbg", [8, 12], F32)
        ones_col = sbuf(ctx, nc, "onec", [128, 1], F32)
        ones_row = sbuf(ctx, nc, "oner", [1, 128], F32)
        eps1 = sbuf(ctx, nc, "eps1", [1, 1], F32)
        g16 = sbuf(ctx, nc, "g16s", [128, 8], F32)
        r16 = sbuf(ctx, nc, "r16s", [8, 128], F32)
        midbc = sbuf(ctx, nc, "midbc", [128, 1], F32)
        t_row = sbuf(ctx, nc, "trow", [1, E], F32)
        cnt = sbuf(ctx, nc, "cnt", [128, 1], F32)
        lnt2 = [sbuf(ctx, nc, f"lnt{i}", [128, DC * TT], BF16) for i in range(2)]
        mrow_cur = sbuf(ctx, nc, "mrowc", [1, TL], F32)
        mbc_t = sbuf(ctx, nc, "mbct", [128, TT], F32)
        gtmp = [sbuf(ctx, nc, f"gtmp{i}", [128, TT], F32) for i in range(2)]
        osb = [sbuf(ctx, nc, f"osb{i}", [128, TT], F32) for i in range(2)]
        b1sb = sbuf(ctx, nc, "b1sb", [128, DI // 128], F32)
        ps_h = [psum(ctx, nc, f"psh{i}", [128, TT]) for i in range(4)]
        ps_o = [psum(ctx, nc, f"pso{i}", [128, TT]) for i in range(2)]
        ps_g = psum(ctx, nc, "psg", [128, TT])
        ps_h2 = psum(ctx, nc, "psmisc2", [128, TT])
        ld = sem(ctx, nc, "ld")
        lnld = sem(ctx, nc, "lnld")
        wsem = sem(ctx, nc, "wsem")
        dvs = sem(ctx, nc, "dvs")
        mms = sem(ctx, nc, "mms")
        acs = sem(ctx, nc, "acs")
        gps = sem(ctx, nc, "gps")
        sto = sem(ctx, nc, "sto")

        with blk(nc, "m_p0") as b:
            @b.sync
            def _(s):
                ld.inc(s.dma_start(out=g16[:, :], in_=g16_in.ap()), 16)
                ld.inc(s.dma_start(out=r16[:, :], in_=r16_in.ap()), 16)
                s.wait_ge(ld.h, ld.n)

            @b.gpsimd
            def _(g):
                g.memset(ones_col[:, :], 1.0)
                g.memset(ones_row[:, :], 1.0)
                g.memset(eps1[:, :], LN_EPS)

            @b.sync
            def _(s2):
                ld.inc(s2.dma_start(out=lo8[:, :],
                                    in_=bass.AP(lohi_in, 0, [[2, 8], [1, 1]]),
                                    allow_slow_non_contiguous=True), 16)
                ld.inc(s2.dma_start(out=hi8[:, :],
                                    in_=bass.AP(lohi_in, 1, [[2, 8], [1, 1]]),
                                    allow_slow_non_contiguous=True), 16)
                s2.wait_ge(ld.h, ld.n)

        with ExitStack() as c1:
            sbis = sbuf(c1, nc, "sbiss", [128, NTOK // 16], F32)
            junk = sbuf(c1, nc, "junk", [128, NTOK // 16], BF16)
            with blk(nc, "m_p1l") as b:
                @b.sync
                def _(s):
                    ld.inc(s.dma_start(out=sbis[:, :], in_=sbis_in.ap()), 16)
                    s.wait_ge(ld.h, ld.n)

            for i in range(BIS_ITERS):
                with blk(nc, "bisa") as b:
                    @b.vector
                    def _(v):
                        v.tensor_scalar(mid8[:, :], lo8[:, :], hi8[:, 0:1], 0.5,
                                        ALU.add, ALU.mult)

                with blk(nc, "bisb") as b:
                    @b.tensor
                    def _(t):
                        t.matmul(ps_g[:, 0:1], r16[:, :], mid8[:, :],
                                 start=True, stop=True)

                with blk(nc, "bisc") as b:
                    @b.vector
                    def _(v):
                        v.tensor_copy(midbc[:, :], ps_g[:, 0:1])

                with blk(nc, "bisd") as b:
                    @b.vector
                    def _(v):
                        v.tensor_scalar(junk[:, :], sbis[:, :], midbc[:, 0:1],
                                        None, ALU.is_ge)
                        v.drain()
                        v.reduce_sum(cnt[:, 0:1], junk[:, :], AX.X)

                with blk(nc, "bise") as b:
                    @b.tensor
                    def _(t):
                        t.matmul(ps_h2[0:8, 0:1], g16[:, :], cnt[:, :],
                                 start=True, stop=True)

                with blk(nc, "bisf") as b:
                    @b.vector
                    def _(v):
                        v.tensor_scalar(gef[:, :], ps_h2[0:8, 0:1],
                                        float(K_SEL) - 0.5, None, ALU.is_ge)

                if debug and i < 4:
                    with blk(nc, "bisdbg") as b:
                        @b.vector
                        def _(v):
                            v.tensor_copy(itdbg[:, 3 * i:3 * i + 1], mid8[:, :])
                            v.tensor_copy(itdbg[:, 3 * i + 1:3 * i + 2], ps_h2[0:8, 0:1])
                            v.tensor_copy(itdbg[:, 3 * i + 2:3 * i + 3], gef[:, :])

                with blk(nc, "bisg") as b:
                    @b.vector
                    def _(v):
                        v.tensor_sub(d8[:, :], mid8[:, :], lo8[:, :])
                        v.tensor_sub(d8b[:, :], hi8[:, :], mid8[:, :])
                        v.drain()
                        v.scalar_tensor_tensor(lo8[:, :], d8[:, :], gef[:, 0:1],
                                               lo8[:, :], ALU.mult, ALU.add)
                        v.scalar_tensor_tensor(hi8[:, :], d8b[:, :], gef[:, 0:1],
                                               mid8[:, :], ALU.mult, ALU.add)

            with blk(nc, "m_p1f") as b:
                @b.vector
                def _(v):
                    dvs.inc(v.tensor_copy(t_col[:, :], lo8[:, :]))

        c2 = ctx.enter_context(ExitStack())
        srow = sbuf(c2, nc, "srow", [1, TL], F32)
        mrow_w = sbuf(c2, nc, "mroww", [1, TL], F32)
        tdram = nc.dram_tensor("tdram", [8, 1], F32)
        with blk(nc, "m_p2a") as b:
            @b.sync
            def _(s):
                sto.inc(s.dma_start(out=tdram.ap(), in_=t_col[:, :]), 16)
                s.wait_ge(sto.h, sto.n)
                sto.inc(s.dma_start(out=t_row[0:1, :],
                                    in_=bass.AP(tdram, 0, [[8, 1], [1, 8]])), 16)
                s.wait_ge(sto.h, sto.n)
                if debug:
                    sto.inc(s.dma_start(out=dbg_t.ap(), in_=t_col[:, :]), 16)
                    sto.inc(s.dma_start(out=dbg_it.ap(), in_=itdbg[:, :]), 16)
                    s.wait_ge(sto.h, sto.n)

        for ee in range(E):
            with blk(nc, f"m_p2r{ee}") as b:
                @b.sync
                def _(s):
                    sto.inc(s.dma_start(
                        out=srow[0:1, :],
                        in_=bass.AP(slocT_in, ee * TL, [[TL, 1], [1, TL]])), 16)
                    s.wait_ge(sto.h, sto.n)

            with blk(nc, f"m_p2m{ee}") as b:
                @b.vector
                def _(v):
                    v.tensor_scalar(mrow_w[0:1, :], srow[0:1, :],
                                    t_row[0:1, ee:ee + 1], None, ALU.is_ge)

            with blk(nc, f"m_p2n{ee}") as b:
                @b.vector
                def _(v):
                    v.scalar_tensor_tensor(mrow_w[0:1, :], mrow_w[0:1, :], 0.5,
                                           srow[0:1, :], ALU.mult, ALU.mult)

            with blk(nc, f"m_p2s{ee}") as b:
                @b.sync
                def _(s):
                    sto.inc(s.dma_start(
                        out=bass.AP(mdram, ee * TL, [[TL, 1], [1, TL]]),
                        in_=mrow_w[0:1, :]), 16)
                    s.wait_ge(sto.h, sto.n)
                    if debug:
                        sto.inc(s.dma_start(
                            out=bass.AP(dbg_m, ee * TL, [[TL, 1], [1, TL]]),
                            in_=mrow_w[0:1, :]), 16)
                        s.wait_ge(sto.h, sto.n)

        c2.close()

        with ExitStack() as c3:
            lbufs = {
                "xt": sbuf(c3, nc, "xt", [128, DC * TT], F32),
                "xsq": sbuf(c3, nc, "xsq", [128, DC * TT], F32),
                "ones_col": ones_col, "ones_row": ones_row, "eps1": eps1,
                "ps_sum": ps_g[0:1, 0:TT], "ps_sq": ps_h2[0:1, 0:TT],
                "ps_tr": ps_h[0][:, 0:8],
                "m_row": sbuf(c3, nc, "mrow", [1, TT], F32),
                "r_row": sbuf(c3, nc, "rrow", [1, TT], F32),
                "var_row": sbuf(c3, nc, "vrow", [1, TT], F32),
                "mcol": sbuf(c3, nc, "mcol", [128, 4], F32),
                "rcol": sbuf(c3, nc, "rcol", [128, 4], F32),
            }
            lnc_s = [sbuf(c3, nc, f"lnc{i}", [128, TT], BF16) for i in range(2)]
            tmpc = sbuf(c3, nc, "tmpc", [128, TT], F32)
            for t0 in range(NT):
                with blk(nc, f"m_p3l{t0}") as b:
                    @b.sync
                    def _(s):
                        s.wait_ge(sto.h, sto.n)
                        load_fm_tile(s, lbufs["xt"], x, t0 * TT, lnld)
                        s.wait_ge(lnld.h, lnld.n)
                        store_fm_tile(s, xo, lbufs["xt"], t0 * TT, sto)
                emit_ln_stats(nc, lbufs)
                with blk(nc, f"m_p3b{t0}") as b:
                    @b.tensor
                    def _(t):
                        mms.inc(t.matmul(ps_g[:, 0:TT], ones_row[:, :],
                                         lbufs["m_row"][0:1, :], start=True, stop=True))
                        mms.inc(t.matmul(ps_h2[:, 0:TT], ones_row[:, :],
                                         lbufs["r_row"][0:1, :], start=True, stop=True))

                with blk(nc, f"m_p3c{t0}") as b:
                    dv0, st0 = dvs.n, sto.n

                    @b.vector
                    def _(v):
                        for c in range(DC):
                            if c >= 2:
                                v.wait_ge(sto.h, st0 + 16 * (c - 1))
                            v.tensor_sub(tmpc[:, :], lbufs["xt"][:, c * TT:(c + 1) * TT],
                                         ps_g[:, 0:TT])
                            dvs.inc(v.tensor_mul(lnc_s[c % 2][:, :], tmpc[:, :],
                                                 ps_h2[:, 0:TT]))

                    @b.sync
                    def _(s):
                        for c in range(DC):
                            s.wait_ge(dvs.h, dv0 + c + 1)
                            ap_out = bass.AP(lndram, t0 * TT + c * 128 * TL,
                                             [[TL, 128], [1, TT]])
                            sto.inc(s.dma_start(out=ap_out, in_=lnc_s[c % 2][:, :]), 16)

        with blk(nc, "m_p3fin") as b:
            @b.sync
            def _(s):
                s.wait_ge(sto.h, sto.n)

        with ExitStack() as c5:
            w1sb = sbuf(c5, nc, "w1sb", [128, DC * DI], BF16)
            w2sb = sbuf(c5, nc, "w2sb", [128, NHC * D], BF16)
            h_sb = sbuf(c5, nc, "hsb", [128, NHC * TT], BF16)

            ln_base_m = [lnld.n]
            with blk(nc, "m_pre") as b:
                @b.sync
                def _(s):
                    ap_in = bass.AP(lndram, 0,
                                    [[TL, 128], [TL * 128, DC], [1, TT]])
                    ap_o = bass.AP(lnt2[0], 0, [[DC * TT, 128], [TT, DC], [1, TT]])
                    lnld.inc(s.dma_start(out=ap_o, in_=ap_in), 16)

            seq_i = [0]
            for e in range(E):
                with blk(nc, f"m_w{e}") as b:
                    @b.gpsimd
                    def _(g):
                        for kc in range(DC):
                            ap_in = bass.AP(w1_in, e * D * DI + kc * 128 * DI,
                                            [[DI, 128], [1, DI]])
                            wsem.inc(g.dma_start(
                                out=w1sb[:, kc * DI:(kc + 1) * DI], in_=ap_in), 16)
                        for kc in range(NHC):
                            ap_in = bass.AP(w2_in, e * DI * D + kc * 128 * D,
                                            [[D, 128], [1, D]])
                            wsem.inc(g.dma_start(
                                out=w2sb[:, kc * D:(kc + 1) * D], in_=ap_in), 16)
                        wsem.inc(g.dma_start(
                            out=b1sb[:, :],
                            in_=bass.AP(b1f_in, e * 128 * NHC, [[NHC, 128], [1, NHC]])), 16)
                        wsem.inc(g.dma_start(
                            out=mrow_cur[0:1, :],
                            in_=bass.AP(mdram, e * TL, [[TL, 1], [1, TL]])), 16)

                for t0 in range(NT):
                    with blk(nc, f"m_e{e}t{t0}") as b:
                        mm0, ac0, dv0, gp0 = mms.n, acs.n, dvs.n, gps.n
                        i_seq = seq_i[0]
                        seq_i[0] += 1
                        lnt = lnt2[i_seq % 2]
                        my_lnld = ln_base_m[0] + 16 * (i_seq + 1)
                        nt0 = (t0 + 1) % NT

                        @b.sync
                        def _(s):
                            if not (e == E - 1 and t0 == NT - 1):
                                ap_in = bass.AP(lndram, nt0 * TT,
                                                [[TL, 128], [TL * 128, DC], [1, TT]])
                                ap_o = bass.AP(lnt2[(i_seq + 1) % 2], 0,
                                               [[DC * TT, 128], [TT, DC], [1, TT]])
                                lnld.inc(s.dma_start(out=ap_o, in_=ap_in), 16)

                        @b.tensor
                        def _(t):
                            t.wait_ge(wsem.h, wsem.n)
                            t.wait_ge(lnld.h, my_lnld)
                            mms.inc(t.matmul(
                                ps_g[:, 0:TT], ones_row[:, :],
                                mrow_cur[0:1, t0 * TT:(t0 + 1) * TT],
                                start=True, stop=True))
                            for hc in range(NHC):
                                if hc >= 4:
                                    t.wait_ge(acs.h, ac0 + hc - 3)
                                for kc in range(DC):
                                    ins = t.matmul(
                                        ps_h[hc % 4][:, :],
                                        w1sb[:, kc * DI + hc * 128: kc * DI + hc * 128 + 128],
                                        lnt[:, kc * TT:(kc + 1) * TT],
                                        start=(kc == 0), stop=(kc == DC - 1))
                                mms.inc(ins)
                            t.wait_ge(dvs.h, dv0 + 1 + NHC)
                            for oc in range(DC):
                                if oc >= 2:
                                    t.wait_ge(acs.h, ac0 + NHC + oc - 1)
                                for hc in range(NHC):
                                    ins = t.matmul(
                                        ps_o[oc % 2][:, :],
                                        w2sb[:, hc * D + oc * 128: hc * D + oc * 128 + 128],
                                        h_sb[:, hc * TT:(hc + 1) * TT],
                                        start=(hc == 0), stop=(hc == NHC - 1))
                                mms.inc(ins)

                        @b.vector
                        def _(v):
                            v.wait_ge(mms.h, mm0 + 1)
                            dvs.inc(v.tensor_copy(mbc_t[:, :], ps_g[:, 0:TT]))
                            for hc in range(NHC):
                                v.wait_ge(acs.h, ac0 + hc + 1)
                                dvs.inc(v.tensor_mul(h_sb[:, hc * TT:(hc + 1) * TT],
                                                     gtmp[hc % 2][:, :], mbc_t[:, :]))

                        @b.scalar
                        def _(sc):
                            sc.wait_ge(wsem.h, wsem.n)
                            for hc in range(NHC):
                                sc.wait_ge(mms.h, mm0 + 1 + hc + 1)
                                if hc >= 2:
                                    sc.wait_ge(dvs.h, dv0 + 1 + hc - 1)
                                acs.inc(sc.activation(gtmp[hc % 2][:, :],
                                                      ps_h[hc % 4][:, :], AF.Gelu,
                                                      bias=b1sb[:, hc:hc + 1]))
                            for oc in range(DC):
                                sc.wait_ge(mms.h, mm0 + 1 + NHC + oc + 1)
                                if oc >= 2:
                                    sc.wait_ge(gps.h, gp0 + 16 * (oc - 1))
                                acs.inc(sc.activation(osb[oc % 2][:, :],
                                                      ps_o[oc % 2][:, :], AF.Copy))

                        @b.gpsimd
                        def _(g):
                            for oc in range(DC):
                                g.wait_ge(acs.h, ac0 + NHC + oc + 1)
                                ap_out = bass.AP(xo, oc * 128 * TL + t0 * TT,
                                                 [[TL, 128], [1, TT]])
                                gps.inc(g.dma_start(out=ap_out, in_=osb[oc % 2][:, :],
                                                    accum_op=ALU.add), 16)

        b2sb = sbuf(ctx, nc, "b2sb", [8, D], F32)
        mt8 = sbuf(ctx, nc, "mt8", [8, TT], F32)
        with blk(nc, "m_p6l") as b:
            @b.sync
            def _(s):
                ld.inc(s.dma_start(out=b2sb[:, :], in_=b2_in.ap()), 16)
                s.wait_ge(ld.h, ld.n)
        for t0 in range(NT):
            with blk(nc, f"m_p6_{t0}") as b:
                mm0, ac0, gp0 = mms.n, acs.n, gps.n

                @b.sync
                def _(s):
                    lnld.inc(s.dma_start(
                        out=mt8[:, :],
                        in_=bass.AP(mdram, t0 * TT, [[TL, E], [1, TT]])), 16)

                @b.tensor
                def _(t):
                    t.wait_ge(lnld.h, lnld.n)
                    for oc in range(DC):
                        if oc >= 2:
                            t.wait_ge(acs.h, ac0 + oc - 1)
                        mms.inc(t.matmul(ps_o[oc % 2][:, :],
                                         b2sb[:, oc * 128:(oc + 1) * 128],
                                         mt8[:, :], start=True, stop=True))

                @b.scalar
                def _(sc):
                    for oc in range(DC):
                        sc.wait_ge(mms.h, mm0 + oc + 1)
                        if oc >= 2:
                            sc.wait_ge(gps.h, gp0 + 16 * (oc - 1))
                        acs.inc(sc.activation(osb[oc % 2][:, :], ps_o[oc % 2][:, :],
                                              AF.Copy))

                @b.gpsimd
                def _(g):
                    for oc in range(DC):
                        g.wait_ge(acs.h, ac0 + oc + 1)
                        ap_out = bass.AP(xo, oc * 128 * TL + t0 * TT,
                                         [[TL, 128], [1, TT]])
                        gps.inc(g.dma_start(out=ap_out, in_=osb[oc % 2][:, :],
                                            accum_op=ALU.add), 16)

        with blk(nc, "m_fin") as b:
            @b.sync
            def _(s):
                s.wait_ge(gps.h, gps.n)
                s.wait_ge(sto.h, sto.n)

    return nc


def moe_host_inputs(scores_full, core, ins_w, x_fm):
    sT = np.ascontiguousarray(scores_full.T)
    sbis = np.ascontiguousarray(sT.reshape(128, NTOK // 16))
    slocT = np.ascontiguousarray(sT[:, core * TL:(core + 1) * TL])
    g16 = np.zeros((128, 8), np.float32)
    for r in range(128):
        g16[r, r // 16] = 1.0
    r16 = np.zeros((8, 128), np.float32)
    for r in range(128):
        r16[r // 16, r] = 1.0
    b1f = np.ascontiguousarray(
        ins_w["L0_ffn_b1"].reshape(E, DI // 128, 128).transpose(0, 2, 1))
    return {
        "x": x_fm, "sbis": sbis, "slocT": slocT,
        "w1": ins_w["L0_ffn_W1"], "w2": ins_w["L0_ffn_W2"],
        "b1f": b1f, "b2": ins_w["L0_ffn_b2"],
        "g16": g16, "r16": r16,
        "lohi": np.stack([np.zeros(8, np.float32), np.ones(8, np.float32)], 1),
    }


def emit_ln_stats2(nc, bufs, W=TT):
    xt = bufs["xt"]; xsqc = bufs["xsqc"]
    ps_sum = bufs["ps_sum"]; ps_sq = bufs["ps_sq"]; ps_tr = bufs["ps_tr"]
    m_row = bufs["m_row"]; r_row = bufs["r_row"]; var_row = bufs["var_row"]
    mcol = bufs["mcol"]; rcol = bufs["rcol"]
    nsub = W // 128
    mms2 = bufs["_mms"]; acs2 = bufs["_acs"]

    with blk(nc, "ln2a") as b:
        mm0, ac0 = mms2.n, acs2.n

        @b.scalar
        def _(sc):
            for c in range(DC):
                if c >= 2:
                    sc.wait_ge(mms2.h, mm0 + c - 1)
                acs2.inc(sc.activation(xsqc[c % 2][:, :], xt[:, c * W:(c + 1) * W],
                                       AF.Square))

        @b.tensor
        def _(t):
            for c in range(DC):
                t.matmul(ps_sum[0:1, :W], bufs["ones_col"][:, :],
                         xt[:, c * W:(c + 1) * W],
                         start=(c == 0), stop=(c == DC - 1))
            for c in range(DC):
                t.wait_ge(acs2.h, ac0 + c + 1)
                ins = t.matmul(ps_sq[0:1, :W], bufs["ones_col"][:, :],
                               xsqc[c % 2][:, :],
                               start=(c == 0), stop=(c == DC - 1))
                mms2.inc(ins)

    with blk(nc, "ln2b") as b:
        @b.vector
        def _(v):
            v.tensor_scalar_mul(m_row[0:1, :W], ps_sum[0:1, :W], 1.0 / D)
            v.tensor_scalar_mul(var_row[0:1, :W], ps_sq[0:1, :W], 1.0 / D)

    with blk(nc, "ln2c") as b:
        @b.scalar
        def _(sc):
            sc.activation(r_row[0:1, :W], m_row[0:1, :W], AF.Square)

    with blk(nc, "ln2d") as b:
        @b.vector
        def _(v):
            v.tensor_sub(var_row[0:1, :W], var_row[0:1, :W], r_row[0:1, :W])

    with blk(nc, "ln2e") as b:
        @b.scalar
        def _(sc):
            sc.activation(var_row[0:1, :W], var_row[0:1, :W], AF.Sqrt,
                          bias=bufs["eps1"][0:1, 0:1])

    with blk(nc, "ln2f") as b:
        @b.vector
        def _(v):
            v.reciprocal(r_row[0:1, :W], var_row[0:1, :W])

    with blk(nc, "ln2g") as b:
        @b.tensor
        def _(t):
            one11 = bufs["ones_row"][0:1, 0:1]
            for s in range(nsub):
                t.matmul(ps_tr[:, s:s + 1], m_row[0:1, s * 128:(s + 1) * 128],
                         one11, start=True, stop=True)
            for s in range(nsub):
                t.matmul(ps_tr[:, nsub + s:nsub + s + 1],
                         r_row[0:1, s * 128:(s + 1) * 128], one11,
                         start=True, stop=True)

    with blk(nc, "ln2h") as b:
        @b.vector
        def _(v):
            v.tensor_copy(mcol[:, 0:nsub], ps_tr[:, 0:nsub])
            v.tensor_copy(rcol[:, 0:nsub], ps_tr[:, nsub:2 * nsub])




def build_attn0():
    nc = bass.Bass("TRN2", target_bir_lowering=False, debug=False)
    x = nc.dram_tensor("x", [D, TL], F32, kind="ExternalInput")
    wq_in = nc.dram_tensor("wq", [D, D], F32, kind="ExternalInput")
    wk_in = nc.dram_tensor("wk", [D, D], F32, kind="ExternalInput")
    wv_in = nc.dram_tensor("wv", [D, D], F32, kind="ExternalInput")
    wo_in = nc.dram_tensor("wo", [D, D], F32, kind="ExternalInput")
    wr_in = nc.dram_tensor("wr", [D, E], F32, kind="ExternalInput")
    xo = nc.dram_tensor("xo", [D, TL], F32, kind="ExternalOutput")
    sc_out = nc.dram_tensor("sc", [TL, E], F32, kind="ExternalOutput")
    lnf = nc.dram_tensor("lnf", [D, TL], F32)

    NKC = SEQ // 128
    NQT = SEQ // TT

    with ExitStack() as ctx:
        ones_col = sbuf(ctx, nc, "onec", [128, 1], F32)
        ones_row = sbuf(ctx, nc, "oner", [1, 128], F32)
        eps1 = sbuf(ctx, nc, "eps1", [1, 1], F32)
        c20n = sbuf(ctx, nc, "c20n", [128, 1], F32)
        rzA = sbuf(ctx, nc, "rzA", [1, TT], F32)
        rzB = sbuf(ctx, nc, "rzB", [1, TT], F32)
        psSA = [psum(ctx, nc, f"psSA{i}", [128, TT]) for i in range(2)]
        psSB = [psum(ctx, nc, f"psSB{i}", [128, TT]) for i in range(2)]
        psZA = psum(ctx, nc, "psZA", [128, TT])
        psZB = psum(ctx, nc, "psZB", [128, TT])
        psRBC = psum(ctx, nc, "psRBC", [128, TT])
        psO = psum(ctx, nc, "psO", [128, TT])

        ld = sem(ctx, nc, "ld")
        wsem = sem(ctx, nc, "wsem")
        mms = sem(ctx, nc, "mms")
        acs = sem(ctx, nc, "acs")
        dvs = sem(ctx, nc, "dvs")
        sto = sem(ctx, nc, "sto")

        actx = ctx.enter_context(ExitStack())
        w0 = sbuf(actx, nc, "w0", [128, DC * D], BF16)
        w1 = sbuf(actx, nc, "w1b", [128, DC * D], BF16)
        w2 = sbuf(actx, nc, "w2b", [128, DC * D], BF16)
        q_sb = sbuf(actx, nc, "q_sb", [128, DC * SEQ], BF16)
        k_sb = sbuf(actx, nc, "k_sb", [128, DC * SEQ], BF16)
        v_sb = sbuf(actx, nc, "v_sb", [128, 16 * 1040], BF16)
        znorm = sbuf(actx, nc, "znorm", [128, DC * TT], BF16)
        zsb = sbuf(actx, nc, "zsb", [128, TT], F32)
        pA = [sbuf(actx, nc, f"pA{i}", [128, TT], BF16) for i in range(2)]
        pB = [sbuf(actx, nc, f"pB{i}", [128, TT], BF16) for i in range(2)]
        osb = [sbuf(actx, nc, f"osb{i}", [128, TT], F32) for i in range(2)]
        xc1 = sbuf(actx, nc, "xc1", [128, TT], F32)
        lc1 = sbuf(actx, nc, "lc1", [128, TT], F32)

        def load_w(q, dst, src, sem_):
            for kc in range(DC):
                ap_in = bass.AP(src, kc * 128 * D, [[D, 128], [1, D]])
                sem_.inc(q.dma_start(out=dst[:, kc * D:(kc + 1) * D], in_=ap_in), 16)

        with blk(nc, "a_p0") as b:
            @b.gpsimd
            def _(g):
                load_w(g, w0, wq_in, wsem)
                load_w(g, w1, wk_in, wsem)
                load_w(g, w2, wv_in, wsem)

            @b.vector
            def _(v):
                v.memset(ones_col[:, :], 1.0)
                v.memset(ones_row[:, :], 1.0)
                v.memset(eps1[:, :], LN_EPS)
                v.memset(c20n[:, :], -20.0)
                ap = bass.AP(v_sb, 64, [[16 * 1040, 128], [1040, 16], [65, 16], [1, 1]])
                v.memset(ap, 1.0)

        lbufs = {
            "xt": sbuf(actx, nc, "xt", [128, DC * TT], F32),
            "xsqc": [sbuf(actx, nc, f"xsqc{i}", [128, TT], F32) for i in range(2)],
            "_mms": mms, "_acs": acs,
            "ones_col": ones_col, "ones_row": ones_row, "eps1": eps1,
            "ps_sum": psZA[0:1, 0:TT], "ps_sq": psZB[0:1, 0:TT],
            "ps_tr": psRBC[:, 0:8],
            "m_row": sbuf(actx, nc, "mrow", [1, TT], F32),
            "r_row": sbuf(actx, nc, "rrow", [1, TT], F32),
            "var_row": sbuf(actx, nc, "vrow", [1, TT], F32),
            "mcol": sbuf(actx, nc, "mcol", [128, 4], F32),
            "rcol": sbuf(actx, nc, "rcol", [128, 4], F32),
        }
        lnt = sbuf(actx, nc, "lnta", [128, DC * TT], BF16)
        tmpc = lbufs["xsqc"][0]
        lnfc = [lbufs["xsqc"][1], sbuf(actx, nc, "lnfc1", [128, TT], F32)]

        for bb_ in range(BL):
            if bb_ > 0:
                with blk(nc, f"a_wq{bb_}") as b:
                    @b.gpsimd
                    def _(g):
                        load_w(g, w0, wq_in, wsem)
            for t0 in range(NQT):
                gt = bb_ * NQT + t0
                with blk(nc, f"a_l{gt}") as b:
                    @b.sync
                    def _(s):
                        s.wait_ge(sto.h, sto.n)
                        load_fm_tile(s, lbufs["xt"], x, gt * TT, ld)
                        s.wait_ge(ld.h, ld.n)
                emit_ln_stats2(nc, lbufs)
                with blk(nc, f"a_lb{gt}") as b:
                    @b.tensor
                    def _(t):
                        mms.inc(t.matmul(psZA[:, 0:TT], ones_row[:, :],
                                         lbufs["m_row"][0:1, :], start=True, stop=True))
                        mms.inc(t.matmul(psZB[:, 0:TT], ones_row[:, :],
                                         lbufs["r_row"][0:1, :], start=True, stop=True))

                with blk(nc, f"a_lc{gt}") as b:
                    dv0, st0 = dvs.n, sto.n

                    @b.vector
                    def _(v):
                        for c in range(DC):
                            if c >= 2:
                                v.wait_ge(sto.h, st0 + 16 * (c - 1))
                            v.tensor_sub(tmpc[:, :], lbufs["xt"][:, c * TT:(c + 1) * TT],
                                         psZA[:, 0:TT])
                            v.tensor_mul(lnfc[c % 2][:, :], tmpc[:, :], psZB[:, 0:TT])
                            v.drain()
                            dvs.inc(v.tensor_copy(lnt[:, c * TT:(c + 1) * TT],
                                                  lnfc[c % 2][:, :]))

                    @b.sync
                    def _(s):
                        for c in range(DC):
                            s.wait_ge(dvs.h, dv0 + c + 1)
                            ap_out = bass.AP(lnf, gt * TT + c * 128 * TL,
                                             [[TL, 128], [1, TT]])
                            sto.inc(s.dma_start(out=ap_out, in_=lnfc[c % 2][:, :]), 16)

                with blk(nc, f"a_pr{gt}") as b:
                    mm0, ac0 = mms.n, acs.n

                    @b.tensor
                    def _(t):
                        t.wait_ge(wsem.h, wsem.n)
                        n_ev = 0
                        for oc in range(DC):
                            for (wsb, dst_q) in ((w0, True), (w1, False)):
                                if n_ev >= 2:
                                    t.wait_ge(acs.h, ac0 + n_ev - 1)
                                bank = psSA[n_ev % 2]
                                for kc in range(DC):
                                    ins = t.matmul(
                                        bank[:, :],
                                        wsb[:, kc * D + oc * 128: kc * D + oc * 128 + 128],
                                        lnt[:, kc * TT:(kc + 1) * TT],
                                        start=(kc == 0), stop=(kc == DC - 1))
                                mms.inc(ins)
                                n_ev += 1
                        for sub in range(TT // 128):
                            s4 = t0 * 4 + sub
                            for g2 in range(2):
                                if n_ev >= 2:
                                    t.wait_ge(acs.h, ac0 + n_ev - 1)
                                bank = psSB[n_ev % 2]
                                for kc in range(DC):
                                    ins = t.matmul(
                                        bank[:, :],
                                        lnt[:, kc * TT + sub * 128: kc * TT + sub * 128 + 128],
                                        w2[:, kc * D + g2 * 512: kc * D + g2 * 512 + 512],
                                        start=(kc == 0), stop=(kc == DC - 1))
                                mms.inc(ins)
                                n_ev += 1

                    @b.scalar
                    def _(sc):
                        n_ev = 0
                        for oc in range(DC):
                            for dst_q in (True, False):
                                sc.wait_ge(mms.h, mm0 + n_ev + 1)
                                dst = q_sb if dst_q else k_sb
                                acs.inc(sc.activation(
                                    dst[:, oc * SEQ + t0 * TT: oc * SEQ + (t0 + 1) * TT],
                                    psSA[n_ev % 2][:, :], AF.Copy))
                                n_ev += 1
                        for sub in range(TT // 128):
                            s4 = t0 * 4 + sub
                            for g2 in range(2):
                                sc.wait_ge(mms.h, mm0 + n_ev + 1)
                                ap_out = bass.AP(
                                    v_sb, s4 * 1040 + g2 * 8 * 65,
                                    [[16 * 1040, 128], [65, 8], [1, 64]])
                                acs.inc(sc.activation(ap_out, psSB[n_ev % 2][:, :],
                                                      AF.Copy))
                                n_ev += 1

            with blk(nc, f"a_wo{bb_}") as b:
                @b.gpsimd
                def _(g):
                    load_w(g, w0, wo_in, wsem)

            for qt in range(NQT):
                gq = bb_ * NQT + qt
                for hp in range(DC):
                    with blk(nc, f"a_at{gq}_{hp}") as b:
                        mm0, ac0, dv0 = mms.n, acs.n, dvs.n

                        @b.tensor
                        def _(t):
                            t.wait_ge(wsem.h, wsem.n)
                            for kc in range(NKC):
                                if kc >= 2:
                                    t.wait_ge(acs.h, ac0 + kc - 1)
                                t.matmul(psSA[kc % 2][:, :],
                                         k_sb[0:64, hp * SEQ + kc * 128: hp * SEQ + kc * 128 + 128],
                                         q_sb[0:64, hp * SEQ + qt * TT: hp * SEQ + (qt + 1) * TT],
                                         start=True, stop=True)
                                mms.inc(t.matmul(
                                    psSB[kc % 2][:, :],
                                    k_sb[64:128, hp * SEQ + kc * 128: hp * SEQ + kc * 128 + 128],
                                    q_sb[64:128, hp * SEQ + qt * TT: hp * SEQ + (qt + 1) * TT],
                                    start=True, stop=True))
                                if kc >= 1:
                                    t.wait_ge(acs.h, ac0 + kc)
                                    t.matmul(psZA[0:65, :],
                                             v_sb[:, (kc - 1) * 1040 + (2 * hp) * 65: (kc - 1) * 1040 + (2 * hp) * 65 + 65],
                                             pA[(kc - 1) % 2][:, :],
                                             start=(kc == 1), stop=(kc == NKC))
                                    mms.inc(t.matmul(
                                        psZB[0:65, :],
                                        v_sb[:, (kc - 1) * 1040 + (2 * hp + 1) * 65: (kc - 1) * 1040 + (2 * hp + 1) * 65 + 65],
                                        pB[(kc - 1) % 2][:, :],
                                        start=(kc == 1), stop=(kc == NKC)))
                            kc = NKC
                            t.wait_ge(acs.h, ac0 + kc)
                            t.matmul(psZA[0:65, :],
                                     v_sb[:, (kc - 1) * 1040 + (2 * hp) * 65: (kc - 1) * 1040 + (2 * hp) * 65 + 65],
                                     pA[(kc - 1) % 2][:, :], start=False, stop=True)
                            mms.inc(t.matmul(
                                psZB[0:65, :],
                                v_sb[:, (kc - 1) * 1040 + (2 * hp + 1) * 65: (kc - 1) * 1040 + (2 * hp + 1) * 65 + 65],
                                pB[(kc - 1) % 2][:, :], start=False, stop=True))
                            t.wait_ge(dvs.h, dv0 + 1)
                            t.matmul(psRBC[0:64, :], ones_row[0:1, 0:64], rzA[0:1, :],
                                     start=True, stop=True)
                            mms.inc(t.matmul(psRBC[64:128, :], ones_row[0:1, 0:64],
                                             rzB[0:1, :], start=True, stop=True,
                                             tile_position=(0, 64)))

                        @b.scalar
                        def _(sc):
                            for kc in range(NKC):
                                sc.wait_ge(mms.h, mm0 + (2 * kc if kc else 1))
                                sc.activation(pA[kc % 2][:, :], psSA[kc % 2][:, :],
                                              AF.Exp, bias=c20n[:, 0:1], scale=0.125)
                                acs.inc(sc.activation(pB[kc % 2][:, :], psSB[kc % 2][:, :],
                                                      AF.Exp, bias=c20n[:, 0:1],
                                                      scale=0.125))
                            sc.wait_ge(mms.h, mm0 + 2 * NKC)
                            sc.activation(zsb[0:64, :], psZA[0:64, :], AF.Copy)
                            acs.inc(sc.activation(zsb[64:128, :], psZB[0:64, :], AF.Copy))

                        @b.vector
                        def _(v):
                            v.wait_ge(mms.h, mm0 + 2 * NKC)
                            v.reciprocal(rzA[0:1, :], psZA[64:65, :])
                            v.reciprocal(rzB[0:1, :], psZB[64:65, :])
                            v.drain()
                            dvs.inc(v.engine_nop())
                            v.wait_ge(mms.h, mm0 + 2 * NKC + 1)
                            v.wait_ge(acs.h, ac0 + NKC + 1)
                            v.tensor_mul(znorm[:, hp * TT:(hp + 1) * TT],
                                         zsb[:, :], psRBC[:, :])

                with blk(nc, f"a_op{gq}") as b:
                    mm0, ac0, dv0, st0 = mms.n, acs.n, dvs.n, sto.n

                    @b.sync
                    def _(s):
                        for oc in range(DC):
                            ap_x = bass.AP(x, oc * 128 * TL + gq * TT, [[TL, 128], [1, TT]])
                            ld.inc(s.dma_start(out=xc1[:, :], in_=ap_x), 16)
                            ap_l = bass.AP(lnf, oc * 128 * TL + gq * TT, [[TL, 128], [1, TT]])
                            ld.inc(s.dma_start(out=lc1[:, :], in_=ap_l), 16)
                            s.wait_ge(dvs.h, dv0 + oc + 1)
                            ap_o = bass.AP(xo, oc * 128 * TL + gq * TT, [[TL, 128], [1, TT]])
                            sto.inc(s.dma_start(out=ap_o, in_=osb[oc % 2][:, :]), 16)
                        s.wait_ge(sto.h, sto.n)

                    @b.tensor
                    def _(t):
                        for oc in range(DC):
                            if oc >= 1:
                                t.wait_ge(acs.h, ac0 + oc)
                            for hp in range(DC):
                                ins = t.matmul(
                                    psO[:, :],
                                    w0[:, hp * D + oc * 128: hp * D + oc * 128 + 128],
                                    znorm[:, hp * TT:(hp + 1) * TT],
                                    start=(hp == 0), stop=(hp == DC - 1))
                            mms.inc(ins)

                    @b.scalar
                    def _(sc):
                        for oc in range(DC):
                            sc.wait_ge(mms.h, mm0 + oc + 1)
                            if oc >= 2:
                                sc.wait_ge(dvs.h, dv0 + oc - 1)
                                sc.wait_ge(sto.h, st0 + 16 * (oc - 1))
                            acs.inc(sc.activation(osb[oc % 2][:, :], psO[:, :], AF.Copy))

                    @b.vector
                    def _(v):
                        for oc in range(DC):
                            v.wait_ge(acs.h, ac0 + oc + 1)
                            v.wait_ge(ld.h, st0 * 0 + mm0 * 0 + ld.n - 16 * (2 * (DC - 1 - oc)))
                            v.tensor_add(osb[oc % 2][:, :], osb[oc % 2][:, :],
                                         xc1[:, :])
                            v.drain()
                            dvs.inc(v.tensor_add(osb[oc % 2][:, :], osb[oc % 2][:, :],
                                                 lc1[:, :]))

        actx.close()
        sbufs = alloc_scores_bufs(ctx, nc, "s3", ps={
            "ps_sum": psSA[0][0:1, 0:TT], "ps_sq": psSA[1][0:1, 0:TT],
            "ps_tr": psSB[0][:, 0:8], "ps_raw": psSB[1][:, 0:32],
            "ps_bc": psZA[:, 0:8]})
        with blk(nc, "a_id") as b:
            @b.vector
            def _(v):
                v.memset(sbufs["ones_col"][:, :], 1.0)
                v.memset(sbufs["ones_row"][:, :], 1.0)
                v.memset(sbufs["eps1"][:, :], LN_EPS)
        emit_scores_prep(nc, sbufs, wr_in, ld)
        st2 = sem(ctx, nc, "st2")
        for t0 in range(NT):
            with blk(nc, f"a_sl{t0}") as b:
                @b.sync
                def _(s):
                    s.wait_ge(st2.h, st2.n)
                    load_fm_tile(s, sbufs["xt"], xo, t0 * TT, ld)
                    s.wait_ge(ld.h, ld.n)
            emit_ln_stats(nc, sbufs)
            emit_scores_tile(nc, sbufs)
            with blk(nc, f"a_ss{t0}") as b:
                @b.sync
                def _(s):
                    for sub in range(TT // 128):
                        row0 = t0 * TT + sub * 128
                        st2.inc(s.dma_start(
                            out=bass.AP(sc_out, row0 * E, [[E, 128], [1, E]]),
                            in_=sbufs["sc_t"][:, sub * 8:(sub + 1) * 8]), 16)
                    s.wait_ge(st2.h, st2.n)

        with blk(nc, "a_fin") as b:
            @b.sync
            def _(s):
                s.wait_ge(sto.h, sto.n)
                s.wait_ge(st2.h, st2.n)

    return nc



def build_l1ffn(nmat=9, skip=()):
    nc = bass.Bass("TRN2", target_bir_lowering=False, debug=False)
    x = nc.dram_tensor("x", [D, TL], F32, kind="ExternalInput")
    gwT_in = nc.dram_tensor("gwT", [D, E], F32, kind="ExternalInput")
    sg_in = nc.dram_tensor("sg", [D, D], F32, kind="ExternalInput")
    su_in = nc.dram_tensor("su", [D, D], F32, kind="ExternalInput")
    sd_in = nc.dram_tensor("sd", [D, D], F32, kind="ExternalInput")
    eg_in = nc.dram_tensor("eg", [E, D, D], F32, kind="ExternalInput")
    eu_in = nc.dram_tensor("eu", [E, D, D], F32, kind="ExternalInput")
    ed_in = nc.dram_tensor("ed", [E, D, D], F32, kind="ExternalInput")
    ident_in = nc.dram_tensor("ident", [128, 128], F32, kind="ExternalInput")
    xo = nc.dram_tensor("xo", [D, TL], F32, kind="ExternalOutput")
    lndram = nc.dram_tensor("lndram", [D, TL], BF16)
    cwdram = nc.dram_tensor("cwdram", [E, TL], F32)

    with ExitStack() as ctx:
        ld = sem(ctx, nc, "ld")
        lnld = sem(ctx, nc, "lnld")
        wsem = sem(ctx, nc, "wsem")
        dvs = sem(ctx, nc, "dvs")
        mms = sem(ctx, nc, "mms")
        acs = sem(ctx, nc, "acs")
        gps = sem(ctx, nc, "gps")
        sto = sem(ctx, nc, "sto")

        ps = [psum(ctx, nc, f"f_ps{i}", [128, TT]) for i in range(8)]
        ps_g = [ps[0], ps[1]]
        ps_u = [ps[2], ps[3]]
        ps_o = [ps[4], ps[5]]
        ps_m1 = ps[6]
        ps_m2 = ps[7]

        sbufs = alloc_scores_bufs(ctx, nc, "f", ps={
            "ps_sum": ps_m1[0:1, 0:TT], "ps_sq": ps_m2[0:1, 0:TT],
            "ps_tr": ps[0][:, 0:8], "ps_raw": ps[1][:, 0:32],
            "ps_bc": ps[2][:, 0:8]})
        ident = sbuf(ctx, nc, "identb", [128, 128], F32)
        lnc_s = [sbuf(ctx, nc, f"lnc{i}", [128, TT], BF16) for i in range(2)]
        cwsb = sbuf(ctx, nc, "cwsb", [8, TL], F32)
        cw_t = sbuf(ctx, nc, "cw_t", [128, 32], F32)
        m2c = sbuf(ctx, nc, "m2c", [128, 4], F32)
        eqm = sbuf(ctx, nc, "eqm", [128, 32], F32)
        l2 = sbuf(ctx, nc, "l2", [128, 32], F32)
        pind = sbuf(ctx, nc, "pind", [128, 32], F32)
        s2c = sbuf(ctx, nc, "s2c", [128, 4], F32)
        denc = sbuf(ctx, nc, "denc", [128, 4], F32)
        mxp = sbuf(ctx, nc, "mxp", [128, 4], F32)

        with blk(nc, "f_p0") as b:
            @b.sync
            def _(s):
                ld.inc(s.dma_start(out=ident[:, :], in_=ident_in.ap()), 16)
                s.wait_ge(ld.h, ld.n)

            @b.vector
            def _(v):
                v.memset(sbufs["ones_col"][:, :], 1.0)
                v.memset(sbufs["ones_row"][:, :], 1.0)
                v.memset(sbufs["eps1"][:, :], LN_EPS)

        emit_scores_prep(nc, sbufs, gwT_in, ld)

        for t0 in range(NT):
            with blk(nc, f"f_l{t0}") as b:
                @b.sync
                def _(s):
                    s.wait_ge(sto.h, sto.n)
                    load_fm_tile(s, sbufs["xt"], x, t0 * TT, lnld)
                    s.wait_ge(lnld.h, lnld.n)
                    store_fm_tile(s, xo, sbufs["xt"], t0 * TT, sto)
            emit_ln_stats(nc, sbufs)
            emit_scores_tile(nc, sbufs)
            with blk(nc, f"f_cw{t0}") as b:
                @b.vector
                def _(v):
                    pre = sbufs["pre"]; exps = sbufs["exps"]
                    v.tensor_reduce(mxp[:, 0:4].unsqueeze(2),
                                    pre[:, 0:32].rearrange("p (s e) -> p s e", e=8),
                                    AX.X, ALU.max)
                    v.drain()
                    for s in range(4):
                        v.tensor_scalar(eqm[:, s * 8:(s + 1) * 8],
                                        pre[:, s * 8:(s + 1) * 8],
                                        mxp[:, s:s + 1], None, ALU.is_ge)
                    v.drain()
                    for s in range(4):
                        v.scalar_tensor_tensor(l2[:, s * 8:(s + 1) * 8],
                                               eqm[:, s * 8:(s + 1) * 8], -1e30,
                                               pre[:, s * 8:(s + 1) * 8],
                                               ALU.mult, ALU.add)
                    v.drain()
                    v.tensor_reduce(m2c[:, 0:4].unsqueeze(2),
                                    l2[:, 0:32].rearrange("p (s e) -> p s e", e=8),
                                    AX.X, ALU.max)
                    v.drain()
                    for s in range(4):
                        v.tensor_scalar(eqm[:, s * 8:(s + 1) * 8],
                                        pre[:, s * 8:(s + 1) * 8],
                                        m2c[:, s:s + 1], None, ALU.is_ge)
                    v.drain()
                    v.tensor_mul(pind[:, 0:32], exps[:, 0:32], eqm[:, 0:32])
                    v.drain()
                    v.tensor_reduce(s2c[:, 0:4].unsqueeze(2),
                                    pind[:, 0:32].rearrange("p (s e) -> p s e", e=8),
                                    AX.X, ALU.add)
                    v.drain()
                    v.scalar_tensor_tensor(denc[:, 0:4], sbufs["zrow"][:, 0:4], 1e-8,
                                           s2c[:, 0:4], ALU.mult, ALU.add)
                    v.drain()
                    v.reciprocal(denc[:, 0:4], denc[:, 0:4])
                    v.drain()
                    for s in range(4):
                        dvs.inc(v.tensor_scalar(cw_t[:, s * 8:(s + 1) * 8],
                                                pind[:, s * 8:(s + 1) * 8],
                                                denc[:, s:s + 1], 0.5,
                                                ALU.mult, ALU.mult))

            with blk(nc, f"f_cwt{t0}") as b:
                @b.tensor
                def _(t):
                    for s in range(4):
                        t.transpose(ps_m1[0:8, s * 128:(s + 1) * 128],
                                    cw_t[:, s * 8:(s + 1) * 8], ident[:, :])

            with blk(nc, f"f_cwe{t0}") as b:
                @b.vector
                def _(v):
                    v.tensor_copy(cwsb[0:8, t0 * TT:(t0 + 1) * TT],
                                  ps_m1[0:8, 0:TT])

            with blk(nc, f"f_lb{t0}") as b:
                @b.tensor
                def _(t):
                    t.matmul(ps_m1[:, 0:TT], sbufs["ones_row"][:, :],
                             sbufs["m_row"][0:1, :], start=True, stop=True)
                    t.matmul(ps_m2[:, 0:TT], sbufs["ones_row"][:, :],
                             sbufs["r_row"][0:1, :], start=True, stop=True)

            with blk(nc, f"f_lc{t0}") as b:
                dv0, st0 = dvs.n, sto.n

                @b.vector
                def _(v):
                    for c in range(DC):
                        if c >= 2:
                            v.wait_ge(sto.h, st0 + 16 * (c - 1))
                        v.tensor_sub(sbufs["xsq"][:, 0:TT],
                                     sbufs["xt"][:, c * TT:(c + 1) * TT],
                                     ps_m1[:, 0:TT])
                        dvs.inc(v.tensor_mul(lnc_s[c % 2][:, :],
                                             sbufs["xsq"][:, 0:TT], ps_m2[:, 0:TT]))

                @b.sync
                def _(s):
                    for c in range(DC):
                        s.wait_ge(dvs.h, dv0 + c + 1)
                        ap_out = bass.AP(lndram, t0 * TT + c * 128 * TL,
                                         [[TL, 128], [1, TT]])
                        sto.inc(s.dma_start(out=ap_out, in_=lnc_s[c % 2][:, :]), 16)

        with blk(nc, "f_cwd") as b:
            @b.sync
            def _(s):
                sto.inc(s.dma_start(out=cwdram.ap(), in_=cwsb[:, :]), 16)
                s.wait_ge(sto.h, sto.n)

        with ExitStack() as c5:
            gg = sbuf(c5, nc, "gg", [128, DC * D], BF16)
            gu = sbuf(c5, nc, "gu", [128, DC * D], BF16)
            dt_ = sbuf(c5, nc, "dt", [128, DC * D], BF16)
            lnt5b = [sbuf(c5, nc, f"lnt5{i}", [128, DC * TT], BF16) for i in range(2)]
            t_sb = sbuf(c5, nc, "tsb", [128, DC * TT], BF16)
            sg_s = [sbuf(c5, nc, f"sgs{i}", [128, TT], F32) for i in range(2)]
            u_s = [sbuf(c5, nc, f"us{i}", [128, TT], F32) for i in range(2)]
            t1f = sbuf(c5, nc, "t1f", [128, TT], F32)
            wbc = sbuf(c5, nc, "wbc", [128, TT], F32)
            osb = [sbuf(c5, nc, f"osb5{i}", [128, TT], F32) for i in range(2)]
            wrow = sbuf(c5, nc, "wrow", [1, TL], F32)

            ln_base_5 = [lnld.n]
            with blk(nc, "f_pre") as b:
                @b.sync
                def _(s):
                    ap_in = bass.AP(lndram, 0, [[TL, 128], [TL * 128, DC], [1, TT]])
                    ap_o = bass.AP(lnt5b[0], 0, [[DC * TT, 128], [TT, DC], [1, TT]])
                    lnld.inc(s.dma_start(out=ap_o, in_=ap_in), 16)

            seq5 = [0]
            for mat in range(nmat):
                with blk(nc, f"f_w{mat}") as b:
                    @b.gpsimd
                    def _(g):
                        for kc in range(DC):
                            if mat == 0:
                                srcs = [(sg_in, 0), (su_in, 0), (sd_in, 0)]
                            else:
                                ee = mat - 1
                                srcs = [(eg_in, ee * D * D), (eu_in, ee * D * D),
                                        (ed_in, ee * D * D)]
                            for (srct, base), dst in zip(srcs, (gg, gu, dt_)):
                                ap_in = bass.AP(srct, base + kc * 128 * D,
                                                [[D, 128], [1, D]])
                                wsem.inc(g.dma_start(
                                    out=dst[:, kc * D:(kc + 1) * D], in_=ap_in), 16)
                        if mat > 0:
                            wsem.inc(g.dma_start(
                                out=wrow[0:1, :],
                                in_=bass.AP(cwdram, (mat - 1) * TL,
                                            [[TL, 1], [1, TL]])), 16)

                    if mat == 0:
                        @b.vector
                        def _(v):
                            v.memset(wbc[:, :], 0.5)

                for t0 in range(NT if "fm" not in skip else 0):
                    with blk(nc, f"f_m{mat}t{t0}") as b:
                        mm0, ac0, dv0, gp0 = mms.n, acs.n, dvs.n, gps.n
                        off = 1 if mat > 0 else 0
                        i5 = seq5[0]
                        seq5[0] += 1
                        lnt = lnt5b[i5 % 2]
                        my_ln5 = ln_base_5[0] + 16 * (i5 + 1)
                        nt5 = (t0 + 1) % NT

                        @b.sync
                        def _(s):
                            if not (mat == nmat - 1 and t0 == NT - 1):
                                ap_in = bass.AP(lndram, nt5 * TT,
                                                [[TL, 128], [TL * 128, DC], [1, TT]])
                                ap_o = bass.AP(lnt5b[(i5 + 1) % 2], 0,
                                               [[DC * TT, 128], [TT, DC], [1, TT]])
                                lnld.inc(s.dma_start(out=ap_o, in_=ap_in), 16)

                        @b.tensor
                        def _(t):
                            t.wait_ge(wsem.h, wsem.n)
                            t.wait_ge(lnld.h, my_ln5)
                            if mat > 0:
                                mms.inc(t.matmul(ps_m1[:, 0:TT], sbufs["ones_row"][:, :],
                                                 wrow[0:1, t0 * TT:(t0 + 1) * TT],
                                                 start=True, stop=True))
                            for fc in range(DC):
                                if fc >= 2:
                                    t.wait_ge(acs.h, ac0 + 2 * (fc - 2) + 2)
                                for kc in range(DC):
                                    i1 = t.matmul(
                                        ps_g[fc % 2][:, :],
                                        gg[:, kc * D + fc * 128: kc * D + fc * 128 + 128],
                                        lnt[:, kc * TT:(kc + 1) * TT],
                                        start=(kc == 0), stop=(kc == DC - 1))
                                mms.inc(i1)
                                for kc in range(DC):
                                    i2 = t.matmul(
                                        ps_u[fc % 2][:, :],
                                        gu[:, kc * D + fc * 128: kc * D + fc * 128 + 128],
                                        lnt[:, kc * TT:(kc + 1) * TT],
                                        start=(kc == 0), stop=(kc == DC - 1))
                                mms.inc(i2)
                            if "down" not in skip:
                                t.wait_ge(dvs.h, dv0 + off + 2 * DC)
                                for oc in range(DC):
                                    if oc >= 2:
                                        t.wait_ge(acs.h, ac0 + 2 * DC + oc - 1)
                                    for ic in range(DC):
                                        i3 = t.matmul(
                                            ps_o[oc % 2][:, :],
                                            dt_[:, ic * D + oc * 128: ic * D + oc * 128 + 128],
                                            t_sb[:, ic * TT:(ic + 1) * TT],
                                            start=(ic == 0), stop=(ic == DC - 1))
                                    mms.inc(i3)

                        @b.scalar
                        def _(sc):
                            for fc in range(DC):
                                sc.wait_ge(mms.h, mm0 + off + 2 * fc + 1)
                                if fc >= 2:
                                    sc.wait_ge(dvs.h, dv0 + off + 2 * (fc - 2) + 2)
                                acs.inc(sc.activation(sg_s[fc % 2][:, :],
                                                      ps_g[fc % 2][:, :],
                                                      AF.Copy if "silu" in skip else AF.Silu))
                                sc.wait_ge(mms.h, mm0 + off + 2 * fc + 2)
                                acs.inc(sc.activation(u_s[fc % 2][:, :],
                                                      ps_u[fc % 2][:, :], AF.Copy))
                            for oc in range(DC if "down" not in skip else 0):
                                sc.wait_ge(mms.h, mm0 + off + 2 * DC + oc + 1)
                                if oc >= 2:
                                    sc.wait_ge(gps.h, gp0 + 16 * (oc - 1))
                                acs.inc(sc.activation(osb[oc % 2][:, :],
                                                      ps_o[oc % 2][:, :], AF.Copy))

                        @b.vector
                        def _(v):
                            if mat > 0:
                                v.wait_ge(mms.h, mm0 + 1)
                                dvs.inc(v.tensor_copy(wbc[:, :], ps_m1[:, 0:TT]))
                            for fc in range(DC):
                                v.wait_ge(acs.h, ac0 + 2 * fc + 2)
                                dvs.inc(v.tensor_mul(t1f[:, :], sg_s[fc % 2][:, :],
                                                     u_s[fc % 2][:, :]))
                                dvs.inc(v.tensor_mul(t_sb[:, fc * TT:(fc + 1) * TT],
                                                     t1f[:, :], wbc[:, :]))

                        @b.gpsimd
                        def _(g):
                            for oc in range(DC if "down" not in skip else 0):
                                g.wait_ge(acs.h, ac0 + 2 * DC + oc + 1)
                                ap_out = bass.AP(xo, oc * 128 * TL + t0 * TT,
                                                 [[TL, 128], [1, TT]])
                                kw = {} if "add" in skip else {"accum_op": ALU.add}
                                gps.inc(g.dma_start(out=ap_out, in_=osb[oc % 2][:, :],
                                                    **kw), 16)

        with blk(nc, "f_fin") as b:
            @b.sync
            def _(s):
                s.wait_ge(gps.h, gps.n)
                s.wait_ge(sto.h, sto.n)

    return nc


def l1ffn_host_inputs(ins_w, x_fm):
    def padT(a, rows=None):
        aT = np.ascontiguousarray(a.T.astype(np.float32))
        out = np.zeros((D, D), np.float32)
        out[:aT.shape[0], :aT.shape[1]] = aT
        return out

    sg = ins_w["L1_ffn_shared_gate_up"][:I1]
    su = ins_w["L1_ffn_shared_gate_up"][I1:2 * I1]
    sd = ins_w["L1_ffn_shared_down"]
    eg = np.stack([padT(ins_w["L1_ffn_experts_gate_up"][e][:I1]) for e in range(E)])
    eu = np.stack([padT(ins_w["L1_ffn_experts_gate_up"][e][I1:2 * I1]) for e in range(E)])
    ed = np.stack([padT(ins_w["L1_ffn_experts_down"][e]) for e in range(E)])
    return {
        "x": x_fm,
        "gwT": np.ascontiguousarray(ins_w["L1_ffn_gate_weight"].T.astype(np.float32)),
        "sg": padT(sg), "su": padT(su), "sd": padT(sd),
        "eg": eg, "eu": eu, "ed": ed,
        "ident": np.eye(128, dtype=np.float32),
    }



def build_attn1():
    nc = bass.Bass("TRN2", target_bir_lowering=False, debug=False)
    x = nc.dram_tensor("x", [D, TL], F32, kind="ExternalInput")
    wqkv_in = nc.dram_tensor("wqkv", [D, 3 * D], F32, kind="ExternalInput")
    wo_in = nc.dram_tensor("wo", [D, D], F32, kind="ExternalInput")
    ctab_in = nc.dram_tensor("ctab", [128, SEQ], F32, kind="ExternalInput")
    stab_in = nc.dram_tensor("stab", [128, SEQ], F32, kind="ExternalInput")
    p32_in = nc.dram_tensor("p32", [128, 128], F32, kind="ExternalInput")
    tri_in = nc.dram_tensor("tri", [128, 896], F32, kind="ExternalInput")
    xo = nc.dram_tensor("xo", [D, TL], F32, kind="ExternalOutput")

    NKC = SEQ // 128
    NQT = SEQ // TT

    with ExitStack() as ctx:
        ones_col = sbuf(ctx, nc, "onec", [128, 1], F32)
        ones_row = sbuf(ctx, nc, "oner", [1, 128], F32)
        eps1 = sbuf(ctx, nc, "eps1", [1, 1], F32)
        c20n = sbuf(ctx, nc, "c20n", [128, 1], F32)
        rzB = sbuf(ctx, nc, "rzB", [1, TT], F32)
        psSA = [psum(ctx, nc, f"psSA{i}", [128, TT]) for i in range(2)]
        psSB = [psum(ctx, nc, f"psSB{i}", [128, TT]) for i in range(2)]
        psZA = psum(ctx, nc, "psZA", [128, TT])
        psZB = psum(ctx, nc, "psZB", [128, TT])
        psRBC = psum(ctx, nc, "psRBC", [128, TT])
        psO = psum(ctx, nc, "psO", [128, TT])

        ld = sem(ctx, nc, "ld")
        wsem = sem(ctx, nc, "wsem")
        mms = sem(ctx, nc, "mms")
        acs = sem(ctx, nc, "acs")
        dvs = sem(ctx, nc, "dvs")
        sto = sem(ctx, nc, "sto")

        wq = sbuf(ctx, nc, "wq1", [128, DC * D], BF16)
        wk = sbuf(ctx, nc, "wk1", [128, DC * D], BF16)
        wv = sbuf(ctx, nc, "wv1", [128, DC * D], BF16)
        wo = sbuf(ctx, nc, "wo1", [128, DC * D], BF16)
        q_sb = sbuf(ctx, nc, "q_sb1", [128, DC * SEQ], BF16)
        k_sb = sbuf(ctx, nc, "k_sb1", [128, DC * SEQ], BF16)
        v_sb = sbuf(ctx, nc, "v_sb1", [128, 16 * 1040], BF16)
        ctab = sbuf(ctx, nc, "ctab_s", [128, SEQ], BF16)
        stab = sbuf(ctx, nc, "stab_s", [128, SEQ], BF16)
        p32 = sbuf(ctx, nc, "p32s", [128, 128], BF16)
        tri = sbuf(ctx, nc, "tris", [128, 896], BF16)

        xt1h = sbuf(ctx, nc, "xt1", [128, DC * TT], F32)
        znorm = xt1h.bitcast(BF16)
        lbufs = {
            "xt": xt1h,
            "xsqc": [sbuf(ctx, nc, f"xs1{i}", [128, TT], F32) for i in range(2)],
            "_mms": mms, "_acs": acs,
            "ones_col": ones_col, "ones_row": ones_row, "eps1": eps1,
            "ps_sum": psZA[0:1, 0:TT], "ps_sq": psZB[0:1, 0:TT],
            "ps_tr": psRBC[:, 0:8],
            "m_row": sbuf(ctx, nc, "mrow1", [1, TT], F32),
            "r_row": sbuf(ctx, nc, "rrow1", [1, TT], F32),
            "var_row": sbuf(ctx, nc, "vrow1", [1, TT], F32),
            "mcol": sbuf(ctx, nc, "mcol1", [128, 4], F32),
            "rcol": sbuf(ctx, nc, "rcol1", [128, 4], F32),
        }
        lnt = sbuf(ctx, nc, "lnt1", [128, DC * TT], BF16)
        pf32 = lnt.bitcast(F32)
        pA = [lnt[:, 0:TT], lnt[:, 0:TT]]
        pB = [lnt[:, TT:2 * TT], lnt[:, TT:2 * TT]]
        zsb = pf32[:, TT:2 * TT]
        osb = [pf32[:, 2 * TT:3 * TT], pf32[:, 2 * TT:3 * TT]]
        xc1 = zsb
        rzA = pf32[0:1, 3 * TT:3 * TT + TT]
        rtmp = lbufs["xsqc"][0]
        rtmp2 = lbufs["xsqc"][1]

        def load_wslice(g, dst, col0):
            for kc in range(DC):
                ap_in = bass.AP(wqkv_in, kc * 128 * 3 * D + col0,
                                [[3 * D, 128], [1, D]])
                wsem.inc(g.dma_start(out=dst[:, kc * D:(kc + 1) * D], in_=ap_in), 16)

        with blk(nc, "b_p0") as b:
            @b.gpsimd
            def _(g):
                load_wslice(g, wq, 0)
                load_wslice(g, wk, D)
                load_wslice(g, wv, 2 * D)
                for kc in range(DC):
                    ap_in = bass.AP(wo_in, kc * 128 * D, [[D, 128], [1, D]])
                    wsem.inc(g.dma_start(out=wo[:, kc * D:(kc + 1) * D], in_=ap_in), 16)
                wsem.inc(g.dma_start(out=ctab[:, :], in_=ctab_in.ap()), 16)
                wsem.inc(g.dma_start(out=stab[:, :], in_=stab_in.ap()), 16)
                wsem.inc(g.dma_start(out=p32[:, :], in_=p32_in.ap()), 16)
                wsem.inc(g.dma_start(out=tri[:, :], in_=tri_in.ap()), 16)

            @b.vector
            def _(v):
                v.memset(ones_col[:, :], 1.0)
                v.memset(ones_row[:, :], 1.0)
                v.memset(eps1[:, :], LN_EPS)
                v.memset(c20n[:, :], -20.0)
                ap = bass.AP(v_sb, 64, [[16 * 1040, 128], [1040, 16], [65, 16], [1, 1]])
                v.memset(ap, 1.0)

        for bb_ in range(BL):
            for t0 in range(NQT):
                gt = bb_ * NQT + t0
                with blk(nc, f"b_l{gt}") as b:
                    @b.sync
                    def _(s):
                        s.wait_ge(sto.h, sto.n)
                        load_fm_tile(s, lbufs["xt"], x, gt * TT, ld)
                        s.wait_ge(ld.h, ld.n)
                emit_ln_stats2(nc, lbufs)
                with blk(nc, f"b_lb{gt}") as b:
                    @b.tensor
                    def _(t):
                        mms.inc(t.matmul(psZA[:, 0:TT], ones_row[:, :],
                                         lbufs["m_row"][0:1, :], start=True, stop=True))
                        mms.inc(t.matmul(psZB[:, 0:TT], ones_row[:, :],
                                         lbufs["r_row"][0:1, :], start=True, stop=True))

                with blk(nc, f"b_lc{gt}") as b:
                    @b.vector
                    def _(v):
                        for c in range(DC):
                            v.tensor_sub(rtmp[:, :], lbufs["xt"][:, c * TT:(c + 1) * TT],
                                         psZA[:, 0:TT])
                            v.tensor_mul(lnt[:, c * TT:(c + 1) * TT], rtmp[:, :],
                                         psZB[:, 0:TT])

                with blk(nc, f"b_pr{gt}") as b:
                    mm0, ac0 = mms.n, acs.n

                    @b.tensor
                    def _(t):
                        t.wait_ge(wsem.h, wsem.n)
                        n_ev = 0
                        for oc in range(DC):
                            for wsb in (wq, wk):
                                if n_ev >= 2:
                                    t.wait_ge(acs.h, ac0 + n_ev - 1)
                                bank = psSA[n_ev % 2]
                                for kc in range(DC):
                                    ins = t.matmul(
                                        bank[:, :],
                                        wsb[:, kc * D + oc * 128: kc * D + oc * 128 + 128],
                                        lnt[:, kc * TT:(kc + 1) * TT],
                                        start=(kc == 0), stop=(kc == DC - 1))
                                mms.inc(ins)
                                n_ev += 1
                        for sub in range(TT // 128):
                            for g2 in range(2):
                                if n_ev >= 2:
                                    t.wait_ge(acs.h, ac0 + n_ev - 1)
                                bank = psSB[n_ev % 2]
                                for kc in range(DC):
                                    ins = t.matmul(
                                        bank[:, :],
                                        lnt[:, kc * TT + sub * 128: kc * TT + sub * 128 + 128],
                                        wv[:, kc * D + g2 * 512: kc * D + g2 * 512 + 512],
                                        start=(kc == 0), stop=(kc == DC - 1))
                                mms.inc(ins)
                                n_ev += 1

                    @b.scalar
                    def _(sc):
                        n_ev = 0
                        for oc in range(DC):
                            for dst in (q_sb, k_sb):
                                sc.wait_ge(mms.h, mm0 + n_ev + 1)
                                acs.inc(sc.activation(
                                    dst[:, oc * SEQ + t0 * TT: oc * SEQ + (t0 + 1) * TT],
                                    psSA[n_ev % 2][:, :], AF.Copy))
                                n_ev += 1
                        for sub in range(TT // 128):
                            s4 = t0 * 4 + sub
                            for g2 in range(2):
                                sc.wait_ge(mms.h, mm0 + n_ev + 1)
                                ap_out = bass.AP(
                                    v_sb, s4 * 1040 + g2 * 8 * 65,
                                    [[16 * 1040, 128], [65, 8], [1, 64]])
                                acs.inc(sc.activation(ap_out, psSB[n_ev % 2][:, :],
                                                      AF.Copy))
                                n_ev += 1

            for qk, buf in (("q", q_sb), ("k", k_sb)):
                for hp in range(DC):
                    with blk(nc, f"b_ro{bb_}{qk}{hp}") as b:
                        mm0 = mms.n

                        @b.tensor
                        def _(t):
                            for qt in range(NQT):
                                bank = (psSA + psSB)[qt]
                                mms.inc(t.matmul(
                                    bank[:, :], p32[:, :],
                                    buf[:, hp * SEQ + qt * TT: hp * SEQ + (qt + 1) * TT],
                                    start=True, stop=True))

                        @b.vector
                        def _(v):
                            for qt in range(NQT):
                                bank = (psSA + psSB)[qt]
                                v.wait_ge(mms.h, mm0 + qt + 1)
                                v.tensor_mul(rtmp[:, :], bank[:, :],
                                             stab[:, qt * TT:(qt + 1) * TT])
                                v.tensor_mul(rtmp2[:, :],
                                             buf[:, hp * SEQ + qt * TT: hp * SEQ + (qt + 1) * TT],
                                             ctab[:, qt * TT:(qt + 1) * TT])
                                v.tensor_add(
                                    buf[:, hp * SEQ + qt * TT: hp * SEQ + (qt + 1) * TT],
                                    rtmp[:, :], rtmp2[:, :])

            for qt in range(NQT):
                gq = bb_ * NQT + qt
                nkc = 4 * (qt + 1)
                for hp in range(DC):
                    with blk(nc, f"b_at{gq}_{hp}") as b:
                        mm0, ac0, dv0 = mms.n, acs.n, dvs.n
                        zw = []

                        @b.scalar
                        def _(sc):
                            nonlocal zw
                            for kc in range(nkc):
                                sc.wait_ge(mms.h, mm0 + (2 * kc + 1 if kc else 1))
                                sc.activation(pA[kc % 2][:, :], psSA[kc % 2][:, :],
                                              AF.Exp, bias=c20n[:, 0:1], scale=0.125)
                                acs.inc(sc.activation(pB[kc % 2][:, :], psSB[kc % 2][:, :],
                                                      AF.Exp, bias=c20n[:, 0:1],
                                                      scale=0.125))
                            sc.wait_ge(mms.h, mm0 + 2 * nkc)
                            sc.activation(zsb[0:64, :], psZA[0:64, :], AF.Copy)
                            acs.inc(sc.activation(zsb[64:128, :], psZB[0:64, :], AF.Copy))

                        @b.vector
                        def _(v):
                            for kc in range(nkc):
                                if kc >= 4 * qt:
                                    j = kc - 4 * qt
                                    v.wait_ge(acs.h, ac0 + kc + 1)
                                    o_j = 384 - 128 * j
                                    v.tensor_mul(pA[kc % 2][:, :], pA[kc % 2][:, :],
                                                 tri[:, o_j:o_j + TT])
                                    dvs.inc(v.tensor_mul(pB[kc % 2][:, :], pB[kc % 2][:, :],
                                                         tri[:, o_j:o_j + TT]))
                            v.wait_ge(mms.h, mm0 + 2 * nkc)
                            v.reciprocal(rzA[0:1, :], psZA[64:65, :])
                            v.reciprocal(rzB[0:1, :], psZB[64:65, :])
                            v.drain()
                            dvs.inc(v.engine_nop())
                            v.wait_ge(mms.h, mm0 + 2 * nkc + 1)
                            v.wait_ge(acs.h, ac0 + nkc + 1)
                            v.tensor_mul(znorm[:, hp * TT:(hp + 1) * TT],
                                         zsb[:, :], psRBC[:, :])

                        @b.tensor
                        def _(t):
                            ndv = 0
                            for kc in range(nkc):
                                if kc >= 2:
                                    t.wait_ge(acs.h, ac0 + kc - 1)
                                t.matmul(psSA[kc % 2][:, :],
                                         k_sb[0:64, hp * SEQ + kc * 128: hp * SEQ + kc * 128 + 128],
                                         q_sb[0:64, hp * SEQ + qt * TT: hp * SEQ + (qt + 1) * TT],
                                         start=True, stop=True)
                                mms.inc(t.matmul(
                                    psSB[kc % 2][:, :],
                                    k_sb[64:128, hp * SEQ + kc * 128: hp * SEQ + kc * 128 + 128],
                                    q_sb[64:128, hp * SEQ + qt * TT: hp * SEQ + (qt + 1) * TT],
                                    start=True, stop=True))
                                if kc >= 1:
                                    kp = kc - 1
                                    if kp >= 4 * qt:
                                        ndv += 1
                                        t.wait_ge(dvs.h, dv0 + ndv)
                                    else:
                                        t.wait_ge(acs.h, ac0 + kc)
                                    t.matmul(psZA[0:65, :],
                                             v_sb[:, kp * 1040 + (2 * hp) * 65: kp * 1040 + (2 * hp) * 65 + 65],
                                             pA[kp % 2][:, :],
                                             start=(kp == 0), stop=False)
                                    mms.inc(t.matmul(
                                        psZB[0:65, :],
                                        v_sb[:, kp * 1040 + (2 * hp + 1) * 65: kp * 1040 + (2 * hp + 1) * 65 + 65],
                                        pB[kp % 2][:, :],
                                        start=(kp == 0), stop=False))
                            kp = nkc - 1
                            ndv += 1
                            t.wait_ge(dvs.h, dv0 + ndv)
                            t.matmul(psZA[0:65, :],
                                     v_sb[:, kp * 1040 + (2 * hp) * 65: kp * 1040 + (2 * hp) * 65 + 65],
                                     pA[kp % 2][:, :], start=(kp == 0), stop=True)
                            mms.inc(t.matmul(
                                psZB[0:65, :],
                                v_sb[:, kp * 1040 + (2 * hp + 1) * 65: kp * 1040 + (2 * hp + 1) * 65 + 65],
                                pB[kp % 2][:, :], start=(kp == 0), stop=True))
                            t.wait_ge(dvs.h, dv0 + ndv + 1)
                            t.matmul(psRBC[0:64, :], ones_row[0:1, 0:64], rzA[0:1, :],
                                     start=True, stop=True)
                            mms.inc(t.matmul(psRBC[64:128, :], ones_row[0:1, 0:64],
                                             rzB[0:1, :], start=True, stop=True,
                                             tile_position=(0, 64)))

                with blk(nc, f"b_op{gq}") as b:
                    mm0, ac0, dv0, st0 = mms.n, acs.n, dvs.n, sto.n

                    @b.sync
                    def _(s):
                        for oc in range(DC):
                            ap_x = bass.AP(x, oc * 128 * TL + gq * TT, [[TL, 128], [1, TT]])
                            ld.inc(s.dma_start(out=xc1[:, :], in_=ap_x), 16)
                            s.wait_ge(dvs.h, dv0 + oc + 1)
                            ap_o = bass.AP(xo, oc * 128 * TL + gq * TT, [[TL, 128], [1, TT]])
                            sto.inc(s.dma_start(out=ap_o, in_=osb[oc % 2][:, :]), 16)
                        s.wait_ge(sto.h, sto.n)

                    @b.tensor
                    def _(t):
                        for oc in range(DC):
                            if oc >= 1:
                                t.wait_ge(acs.h, ac0 + oc)
                            for hp in range(DC):
                                ins = t.matmul(
                                    psO[:, :],
                                    wo[:, hp * D + oc * 128: hp * D + oc * 128 + 128],
                                    znorm[:, hp * TT:(hp + 1) * TT],
                                    start=(hp == 0), stop=(hp == DC - 1))
                            mms.inc(ins)

                    @b.scalar
                    def _(sc):
                        for oc in range(DC):
                            sc.wait_ge(mms.h, mm0 + oc + 1)
                            if oc >= 1:
                                sc.wait_ge(dvs.h, dv0 + oc)
                                sc.wait_ge(sto.h, st0 + 16 * oc)
                            acs.inc(sc.activation(osb[oc % 2][:, :], psO[:, :], AF.Copy))

                    @b.vector
                    def _(v):
                        for oc in range(DC):
                            v.wait_ge(acs.h, ac0 + oc + 1)
                            v.wait_ge(ld.h, ld.n - 16 * (DC - 1 - oc))
                            dvs.inc(v.tensor_add(osb[oc % 2][:, :], osb[oc % 2][:, :],
                                                 xc1[:, :]))

        with blk(nc, "b_fin") as b:
            @b.sync
            def _(s):
                s.wait_ge(sto.h, sto.n)

    return nc


def attn1_host_inputs(ins_w, x_fm):
    half = DH // 2
    freqs = np.arange(half, dtype=np.float32) / half
    theta = 1.0 / 10000.0 ** freqs
    t = np.arange(SEQ, dtype=np.float32)
    ang = t[:, None] * theta[None, :]
    cos, sin = np.cos(ang), np.sin(ang)
    ctab = np.zeros((128, SEQ), np.float32)
    stab = np.zeros((128, SEQ), np.float32)
    for r in range(128):
        i = r % half
        ctab[r] = cos[:, i]
        stab[r] = sin[:, i] * (1.0 if (r % DH) >= half else -1.0)
    p32 = np.zeros((128, 128), np.float32)
    for k in range(128):
        p32[k, k ^ 32] = 1.0
    tri = np.zeros((128, 896), np.float32)
    for p in range(128):
        tri[p, p + 384:] = 1.0
    return {
        "x": x_fm, "wqkv": ins_w["L1_attn_W_qkv"], "wo": ins_w["L1_attn_W_o"],
        "ctab": ctab, "stab": stab, "p32": p32, "tri": tri,
    }


def _run(nc, maps):
    res = run_bass_kernel_spmd(nc, maps, core_ids=list(range(NCORES)))
    return res.results


def kernel(**inputs):
    ins = {k: np.asarray(v) for k, v in inputs.items()}
    x_fm = _fm_shards(ins["L0_attn_x"].astype(np.float32, copy=False))
    ident = np.eye(128, dtype=np.float32)
    wr = np.ascontiguousarray(ins["L0_ffn_Wr"].astype(np.float32, copy=False))

    nc_sc = build_scores()
    r = _run(nc_sc, [{"x": s, "wr": wr, "ident": ident} for s in x_fm])
    scores1 = np.concatenate([rr["sc"] for rr in r], axis=0)

    nc_moe = build_moe()
    r = _run(nc_moe, [moe_host_inputs(scores1, c, ins, x_fm[c])
                      for c in range(NCORES)])
    x1 = [rr["xo"] for rr in r]

    nc_a0 = build_attn0()
    r = _run(nc_a0, [{"x": x1[c], "wq": ins["L0_attn_Wq"], "wk": ins["L0_attn_Wk"],
                      "wv": ins["L0_attn_Wv"], "wo": ins["L0_attn_Wout"], "wr": wr}
                     for c in range(NCORES)])
    x2 = [rr["xo"] for rr in r]
    scores3 = np.concatenate([rr["sc"] for rr in r], axis=0)

    r = _run(nc_moe, [moe_host_inputs(scores3, c, ins, x2[c])
                      for c in range(NCORES)])
    x3 = [rr["xo"] for rr in r]

    nc_f1 = build_l1ffn()
    r = _run(nc_f1, [l1ffn_host_inputs(ins, x3[c]) for c in range(NCORES)])
    x4 = [rr["xo"] for rr in r]

    nc_a1 = build_attn1()
    r = _run(nc_a1, [attn1_host_inputs(ins, x4[c]) for c in range(NCORES)])
    x5 = [rr["xo"] for rr in r]

    r = _run(nc_f1, [l1ffn_host_inputs(ins, x5[c]) for c in range(NCORES)])
    x6 = [rr["xo"] for rr in r]

    out = np.empty((16, SEQ, D), dtype=np.float32)
    for c in range(NCORES):
        out[BL * c:BL * c + BL] = x6[c].reshape(D, BL, SEQ).transpose(1, 2, 0)
    return out
